# version 13
# baseline (speedup 1.0000x reference)
import numpy as np
import ml_dtypes
import concourse.bass as bass
import concourse.mybir as mybir
from concourse.bass_utils import run_bass_kernel_spmd
from contextlib import ExitStack

F32 = mybir.dt.float32
BF16 = mybir.dt.bfloat16
I32 = mybir.dt.int32
U8 = mybir.dt.uint8
AF = mybir.ActivationFunctionType
ALU = mybir.AluOpType
IOA = bass.IndirectOffsetOnAxis
AX = mybir.AxisListType.X

D = 1024
HT = 8
E = 8
F = 4096
FT = 32
QDIM = 64
ROWB = 1152
EPS = 1e-5


def build_program(S=2048, KSEL=512, BIS=34, stop=6):
    nchunks = S // 128
    gw = min(512, S)
    ngroups = S // gw
    nslots = KSEL // 128
    S16 = S // 16
    prefix_steps = []
    k = 1
    while k < nchunks:
        prefix_steps.append(k)
        k *= 2

    nc = bass.Bass("TRN2", target_bir_lowering=False)

    x_d = nc.dram_tensor("x", [S, D], F32, kind="ExternalInput").ap()
    wq_d = nc.dram_tensor("wq", [QDIM, D], F32, kind="ExternalInput").ap()
    wk_d = nc.dram_tensor("wk", [QDIM, D], F32, kind="ExternalInput").ap()
    wv_d = nc.dram_tensor("wv", [QDIM, D], F32, kind="ExternalInput").ap()
    wo_d = nc.dram_tensor("wo", [D, QDIM], F32, kind="ExternalInput").ap()
    wg_d = nc.dram_tensor("wg", [E, D], F32, kind="ExternalInput").ap()
    w1_d = nc.dram_tensor("w1", [E, D, F], F32, kind="ExternalInput").ap()
    b1_d = nc.dram_tensor("b1", [E, F], F32, kind="ExternalInput").ap()
    w2_d = nc.dram_tensor("w2", [E, F, D], F32, kind="ExternalInput").ap()
    b2_d = nc.dram_tensor("b2", [E, D], F32, kind="ExternalInput").ap()
    triu_d = nc.dram_tensor("triu", [128, 128], BF16, kind="ExternalInput").ap()
    tris_d = nc.dram_tensor("tris", [128, 128], BF16, kind="ExternalInput").ap()
    eg_d = nc.dram_tensor("eg16", [128, 128], BF16, kind="ExternalInput").ap()
    idf_d = nc.dram_tensor("identf", [128, 128], F32, kind="ExternalInput").ap()
    idb_d = nc.dram_tensor("identb", [128, 128], BF16, kind="ExternalInput").ap()
    ids_d = nc.dram_tensor("idsc", [128, nchunks * 2], BF16, kind="ExternalInput").ap()
    cst_d = nc.dram_tensor("csts", [128, 8], F32, kind="ExternalInput").ap()
    out_d = nc.dram_tensor("out", [S, D], F32, kind="ExternalOutput").ap()

    xbf_dram = nc.dram_tensor("xbfd", [S, D], BF16, kind="Internal").ap()
    smd = nc.dram_tensor("smd", [S, E], F32, kind="Internal").ap()
    x1acc = nc.dram_tensor("x1acc", [S, D], F32, kind="Internal").ap()
    compact = [nc.dram_tensor(f"compact{e}", [KSEL, ROWB], BF16, kind="Internal").ap()
               for e in range(E)]
    d8a = nc.dram_tensor("d8a", [8, 1], F32, kind="Internal").ap()
    w1bf_d = nc.dram_tensor("w1bfd", [E, D, F], BF16, kind="Internal").ap()
    w2bf_d = nc.dram_tensor("w2bfd", [E, F, D], BF16, kind="Internal").ap()

    VS = {}
    for i, name in enumerate([
            "ones", "v1col", "xzero", "ids", "lohi",
            "cwq", *[f"ewq{j}" for j in range(8)],
            "cwk", *[f"ewk{j}" for j in range(8)],
            "cwv", *[f"ewv{j}" for j in range(8)],
            "cwo", *[f"ewo{j}" for j in range(8)],
            *[f"ewg{j}" for j in range(8)],
            *[f"eb1_{j}" for j in range(32)],
            "b2c0"]):
        VS[name] = i + 1
    N_VS = len(VS)

    TS = {}
    for i, name in enumerate([
            *[f"twq{j}" for j in range(8)],
            *[f"twk{j}" for j in range(8)],
            *[f"twv{j}" for j in range(8)],
            *[f"two{j}" for j in range(8)],
            *[f"twg{j}" for j in range(8)],
            *[f"tb1_{j}" for j in range(32)]]):
        TS[name] = i + 1
    N_TS = len(TS)

    T_PRJ = 3 * ngroups
    V_PRJ = 5 * ngroups
    _N = S // 128
    LC = [[2, 3, 4, 5, 6, 7, 8, 14], [0, 1, 9, 10, 11, 12, 13, 15]]

    def tA(c):
        return 1 if c == 0 else 16 * c

    def tK(c):
        return 2 if c == 0 else 16 * c + 1

    def tV(c):
        return 3 if c == 0 else 16 * c + 2

    def tnum(c):
        return 4 if c == 0 else 16 * c + 4

    def tdel(c):
        return 5 if c == 0 else 16 * c + 5

    def tndt(c):
        return 6 if c == 0 else 16 * c + 6

    def topj(c):
        return 7 if c == 0 else 16 * c + 7

    def ttr(c, j):
        return 16 * c + 8 + j

    def tgate(c):
        return 16 * c + 19 if c < _N - 1 else 16 * c + 16

    def vam(c):
        return 1 if c == 0 else 10 * c - 1

    def vst(c):
        return 2 if c == 0 else 10 * c

    def vnd(c):
        return 10 * c + 3

    def vadda(c):
        return 10 * c + 4

    def vaddb(c):
        return 10 * c + 5

    def vve(c):
        return 10 * c + 6

    def vnmr(c):
        return 10 * c + 7

    def vpu(c):
        return 10 * c + 8

    def vlg(c):
        return 10 * c + 11 if c < _N - 1 else 10 * c + 9

    def vfin(c):
        return 10 * c + 12 if c < _N - 1 else 10 * c + 10

    def akv(c):
        return 1 if c == 0 else 10 * c

    def ak1(c):
        return 2 if c == 0 else 10 * c + 1

    def av1(c):
        return 3 if c == 0 else 10 * c + 2

    def andt(c):
        return 4 if c == 0 else 10 * c + 4

    def asq(c):
        return 10 * c + 5

    def ar(c):
        return 10 * c + 6

    def axbf(c):
        return 10 * c + 7

    def aevA(c):
        return 10 * c + 8

    def aevB(c):
        return 10 * c + 9

    def aexp(c):
        return 10 * c + 13 if c < _N - 1 else 10 * c + 10
    T_AT = 16
    V_AT = 10
    A_AT = 10
    RANK_EXTRA = 9
    SEL_V_TOTAL = 2 * BIS + RANK_EXTRA
    T_FFN = nslots + FT + 4 * nslots
    V_FFN = 3 + 4 * nslots
    A_FFN = FT

    with ExitStack() as st_:
        def sb(name, shape, dt):
            return st_.enter_context(nc.sbuf_tensor(name, shape, dt))

        def ps(name, shape, dt=F32):
            return st_.enter_context(nc.psum_tensor(name, shape, dt))

        def sem(name):
            return st_.enter_context(nc.semaphore(name))

        big0 = sb("big0", [128, max(HT * S, FT * 512)], BF16)

        def xT(j):
            return big0[:, j * S:(j + 1) * S]

        def h1T(k_):
            return big0[:, k_ * KSEL:(k_ + 1) * KSEL]

        xbf = sb("xbf", [128, nchunks, ROWB], BF16)
        w1q = [sb(f"w1q{i}", [128, HT, 512], BF16) for i in range(3)]
        w2q = [sb(f"w2q{i}", [128, FT, 256], BF16) for i in range(3)]
        ovl = sb("ovl", [128, 4096], F32)
        w2q.append(ovl[:].bitcast(BF16).rearrange("p (f h) -> p f h", f=FT))
        exT2 = [sb(f"exT{i}", [128, 9 * KSEL], BF16) for i in range(2)]
        outsb = sb("outsb", [128, nslots, D], F32)
        qkv = outsb[:].rearrange("p s d -> p (s d)").bitcast(BF16)
        QT = qkv[0:QDIM, 0 * S:1 * S]
        KT = qkv[0:QDIM, 1 * S:2 * S]
        VT = qkv[0:QDIM, 2 * S:3 * S]

        x_tile = [ovl[:, 2048:3072], ovl[:, 3072:4096]]
        sq_scr = exT2[0][:, 0:D]
        sm_sb = sb("sm_sb", [128, nchunks, E], F32)
        x1T_stage = ovl[:, 1024:2048]

        am_sb = sb("am_sb", [128, 128], BF16)
        k1_sb = sb("k1_sb", [128, QDIM], BF16)
        v1_sb = sb("v1_sb", [128, QDIM + 1], BF16)
        nd_sb = sb("nd_sb", [128, QDIM], BF16)
        ndT_sb = sb("ndT_sb", [QDIM, 128], BF16)
        kv_sb = sb("kv_sb", [QDIM, QDIM + 1], BF16)
        state_f = sb("state_f", [QDIM, QDIM + 1], F32)
        e_sb = sb("e_sb", [128, E], F32)
        scal = sb("scal", [128, 16], F32)
        lg_sb = sb("lg_sb", [128, E], F32)

        smbis = sb("smbis", [128, S16], F32)
        cmpb = sb("cmpb", [128, S16], BF16)
        bcol = sb("bcol", [128, 8], F32)
        predu = sb("predu", [128, 1], U8)
        npredu = sb("npredu", [128, 1], U8)
        lo8 = sb("lo8", [1, 8], F32)
        lo_tm = sb("lo_tm", [128, E], F32)
        cmp_tm = sb("cmp_tm", [128, nchunks, E], F32)
        cmpb_tm = sb("cmpb_tm", [128, nchunks, E], BF16)
        rsum = sb("rsum", [128, E], BF16)
        pbufA = sb("pbufA", [128, nchunks, E], F32)
        pbufB = sb("pbufB", [128, nchunks, E], F32)
        exclp = sb("exclp", [128, E], F32)
        off_i = sb("off_i", [128, nchunks, E], I32)

        exv = sb("exv", [128, 32 * nslots], BF16)
        G_sb = sb("G_sb", [128, nslots], F32)
        idf_sb = sb("idf_sb", [128, nslots], F32)
        idx_i = sb("idx_i", [128, nslots], I32)

        stag_f = ovl[:, 0:1024]
        stag_b = sb("stag_b", [QDIM, D], BF16)
        b1tmp = exT2[1][0:8, 0:1024].bitcast(F32)
        wqT = sb("wqT", [128, HT, QDIM], BF16)
        wkT = sb("wkT", [128, HT, QDIM], BF16)
        wvT = sb("wvT", [128, HT, QDIM], BF16)
        wo_b = sb("wo_b", [128, HT, QDIM], BF16)
        woT = sb("woT", [QDIM, D], BF16)
        wgT = sb("wgT", [128, HT, E], F32)
        b1f = sb("b1f", [128, FT, E], F32)
        b2bf2 = [woT[0:1, :], stag_b[0:1, :]]
        ones_f = sb("ones_f", [1, 128], F32)
        ones_b = sb("ones_b", [1, 128], BF16)
        triu = sb("triu_s", [128, 128], BF16)
        tris = sb("tris_s", [128, 128], BF16)
        eg16 = sb("eg16_s", [128, 128], BF16)
        identf = sb("identf_s", [128, 128], F32)
        identb = sb("identb_s", [128, 128], BF16)
        ids_sb = sb("ids_sb", [128, nchunks * 2], BF16)
        cst_sb = sb("cst_sb", [128, 8], F32)
        _w1q0f = w1q[0][:].rearrange("p j f -> p (j f)").bitcast(F32)
        _w1q1f = w1q[1][:].rearrange("p j f -> p (j f)").bitcast(F32)
        x2t = [_w1q0f[:, 0:1024], _w1q0f[:, 1024:2048],
               _w1q1f[:, 0:1024], _w1q1f[:, 1024:2048]]

        ps_a = ps("ps_a", [128, 512])
        ps_b = ps("ps_b", [128, 512])
        ps_c = ps("ps_c", [128, 512])
        ps_n = ps("ps_n", [128, 512])
        ps_state = ps("ps_state", [QDIM, QDIM + 1])
        ps_tr = ps("ps_tr", [128, 128], BF16)
        ps_tr2 = ps("ps_tr2", [128, 128], BF16)
        ps_g = ps("ps_g", [128, 8])

        s_sw = sem("s_sw")
        s_ldk = sem("s_ldk")
        s_ldv = sem("s_ldv")
        s_ldo = sem("s_ldo")
        s_ldg = sem("s_ldg")
        s_b1ld = sem("s_b1ld")
        s_vsw = sem("s_vsw")
        s_tsw = sem("s_tsw")
        s_cast = sem("s_cast")
        s_wc = sem("s_wc")
        s_xt = sem("s_xt")
        s_tprj = sem("s_tprj")
        s_vprj = sem("s_vprj")
        s_aprj = sem("s_aprj")
        s_tat = sem("s_tat")
        s_vat = sem("s_vat")
        s_aat = sem("s_aat")
        s_xch = [sem("s_xch0"), sem("s_xch1")]
        s_x1st = [sem("s_x1st0"), sem("s_x1st1")]
        s_smd = sem("s_smd")
        s_bisld = sem("s_bisld")
        s_tsel = sem("s_tsel")
        s_vsel = sem("s_vsel")
        s_sel2 = sem("s_sel2")
        s_wl1 = [sem("s_wl1a"), sem("s_wl1b"), sem("s_wl1c")]
        s_wl2 = [sem(f"s_wl2{i}") for i in range(4)]
        s_scat = [sem(f"s_scat{e}") for e in range(E)]
        s_exl2 = [sem("s_exla"), sem("s_exlb")]
        s_tffn = sem("s_tffn")
        s_vffn = sem("s_vffn")
        s_affn = sem("s_affn")
        s_b2p = [sem("s_b2a"), sem("s_b2b")]
        s_gsc = sem("s_gsc")
        s_x2ld = [sem(f"s_x2ld{i}") for i in range(4)]
        s_ost = [sem(f"s_ost{i}") for i in range(4)]
        s_vself = sem("s_vself")
        s_aself = sem("s_aself")

        block = st_.enter_context(nc.Block())

        C_EPS = cst_sb[:, 0:1]

        class StopSync(Exception):
            pass

        def _dummy_out(sync):
            if stop >= 1:
                sync.wait_ge(s_vsw, N_VS)
                sync.wait_ge(s_tsw, N_TS)
            if stop >= 2:
                sync.wait_ge(s_vprj, V_PRJ)
            if stop >= 3:
                sync.wait_ge(s_vat, V_AT * nchunks)
                sync.wait_ge(s_aat, A_AT * nchunks)
                sync.wait_ge(s_x1st[0], 16 * ((nchunks + 1) // 2))
                sync.wait_ge(s_x1st[1], 16 * (nchunks // 2))
            if stop >= 4:
                sync.wait_ge(s_vsel, SEL_V_TOTAL)
            if stop >= 5:
                for e in range(E):
                    sync.wait_ge(s_scat[e], 16 * nchunks)
            for cc in range(nchunks):
                sync.dma_start(out=out_d[128 * cc:128 * (cc + 1), :],
                               in_=x_tile[0][:]).then_inc(s_ost[cc % 4], 16)

        def _sync_body(sync):
            for src, dst in [
                    (triu_d, triu[:]), (tris_d, tris[:]), (eg_d, eg16[:]),
                    (idf_d, identf[:]), (idb_d, identb[:]), (ids_d, ids_sb[:]),
                    (cst_d, cst_sb[:]),
                    (x_d[0:128, :], x_tile[0][:]), (x_d[128:256, :], x_tile[1][:]),
                    (wq_d, stag_f[0:QDIM, :])]:
                sync.dma_start(out=dst, in_=src).then_inc(s_sw, 16)
            sync.wait_ge(s_vsw, VS["cwq"])
            sync.dma_start(out=stag_f[0:QDIM, :], in_=wk_d).then_inc(s_ldk, 16)
            sync.wait_ge(s_vsw, VS["cwk"])
            sync.dma_start(out=stag_f[0:QDIM, :], in_=wv_d).then_inc(s_ldv, 16)
            sync.wait_ge(s_vsw, VS["cwv"])
            sync.dma_start(
                out=stag_f[:, 0:512].rearrange("p (j m) -> p j m", j=HT),
                in_=wo_d.rearrange("(j p) m -> p j m", p=128)).then_inc(s_ldo, 16)
            sync.wait_ge(s_vsw, VS["cwo"])
            sync.dma_start(out=stag_f[0:E, :], in_=wg_d).then_inc(s_ldg, 16)
            sync.wait_ge(s_tsw, TS["twg7"])

            for c in range(8 if stop >= 1 else 0):
                if c > 0:
                    sync.wait_ge(s_tsw, TS[f"tb1_{4 * c - 1}"])
                sync.dma_start(out=b1tmp[:],
                               in_=b1_d[:, 512 * c:512 * (c + 1)]).then_inc(s_b1ld, 16)

            if stop < 1:
                raise StopSync

            if stop < 2:
                raise StopSync
            sync.wait_ge(s_cast, 16 * HT)
            for j in range(HT):
                sync.dma_start(out=xT(j), in_=xbf_dram[:, 128 * j:128 * (j + 1)],
                               transpose=True).then_inc(s_xt, 16)

            if stop < 3:
                raise StopSync
            for cc in range(nchunks):
                if cc >= 2:
                    sync.wait_ge(s_aat, A_AT * (cc - 2) + 7)
                    sync.wait_ge(s_x1st[cc % 2], 16 * (cc // 2))
                    sync.dma_start(out=x_tile[cc % 2][:],
                                   in_=x_d[128 * cc:128 * (cc + 1), :]).then_inc(s_xch[cc % 2], 16)
                sync.wait_ge(s_aat, A_AT * cc + 7)
                sync.dma_start(out=x1acc[128 * cc:128 * (cc + 1), :],
                               in_=x_tile[cc % 2][:]).then_inc(s_x1st[cc % 2], 16)

            if stop < 4:
                raise StopSync
            sync.wait_ge(s_vat, V_AT * nchunks)
            sync.dma_start(out=smd.rearrange("(ct p) e -> p ct e", p=128),
                           in_=sm_sb[:]).then_inc(s_smd, 16)
            sync.wait_ge(s_smd, 16)
            for e in range(4):
                sync.dma_start(
                    out=smbis[16 * e:16 * (e + 1), :].unsqueeze(2),
                    in_=smd.rearrange("(pp c) e -> pp c e", pp=16)[:, :, e:e + 1],
                    allow_slow_non_contiguous=True).then_inc(s_bisld, 16)

            sync.wait_ge(s_vsel, 2 * BIS)
            sync.dma_start(out=d8a, in_=bcol[0:128:16, 0:1]).then_inc(s_sel2, 16)
            sync.wait_ge(s_sel2, 16)
            sync.dma_start(out=lo8[:], in_=d8a.rearrange("e o -> o e")).then_inc(s_sel2, 16)

            if stop < 6:
                raise StopSync

            def s_w1l(e, g8):
                gi = e * 8 + g8
                sync.wait_ge(s_wc, 16 * (2 * e + 1))
                if gi >= 3:
                    pe_, pg_ = divmod(gi - 3, 8)
                    sync.wait_ge(s_tffn, pe_ * T_FFN + nslots + pg_ * 4 + 4)
                sync.dma_start(
                    out=w1q[gi % 3][:],
                    in_=w1bf_d[e].rearrange("(j p) f -> p j f", p=128)
                        [:, :, 512 * g8:512 * (g8 + 1)],
                ).then_inc(s_wl1[gi % 3], 16)

            def s_w2l(e, hq):
                qi = e * 4 + hq
                sync.wait_ge(s_wc, 16 * (2 * e + 2))
                if qi >= 4:
                    pe_, pq_ = divmod(qi - 4, 4)
                    done = ((pq_ // 2) * nslots + nslots - 1) * 2 + (pq_ % 2) + 1
                    sync.wait_ge(s_tffn, pe_ * T_FFN + nslots + FT + done)
                sync.dma_start(
                    out=w2q[qi % 4][:],
                    in_=w2bf_d[e].rearrange("(j p) h -> p j h", p=128)
                        [:, :, 256 * hq:256 * (hq + 1)],
                ).then_inc(s_wl2[qi % 4], 16)

            def s_exT(e):
                sync.wait_ge(s_scat[e], 16 * nchunks)
                if e > 1:
                    sync.wait_ge(s_tffn, (e - 2) * T_FFN + nslots + FT)
                for t in range(9):
                    sync.dma_start(out=exT2[e % 2][:, t * KSEL:(t + 1) * KSEL],
                                   in_=compact[e][:, 128 * t:128 * (t + 1)],
                                   transpose=True).then_inc(s_exl2[e % 2], 16)

            s_w1l(0, 0); s_w1l(0, 1); s_w1l(0, 2)
            s_w2l(0, 0); s_w2l(0, 1); s_w2l(0, 2)
            s_exT(0)
            sync.wait_ge(s_x1st[0], 16 * ((nchunks + 1) // 2))
            sync.wait_ge(s_x1st[1], 16 * (nchunks // 2))
            s_w2l(0, 3)
            for e in range(E):
                for g8 in range(3, 8):
                    s_w1l(e, g8)
                if e + 1 < E:
                    s_w1l(e + 1, 0); s_w1l(e + 1, 1); s_w1l(e + 1, 2)
                    s_exT(e + 1)
                    for hq in range(4):
                        s_w2l(e + 1, hq)

            sync.wait_ge(s_gsc, 16 * nslots * E)
            for cc in range(min(4, nchunks)):
                sync.dma_start(out=x2t[cc % 4][:],
                               in_=x1acc[128 * cc:128 * (cc + 1), :]).then_inc(s_x2ld[cc % 4], 16)
            for cc in range(nchunks):
                sync.wait_ge(s_vat, V_AT * nchunks + 3 * cc + 3)
                sync.dma_start(out=out_d[128 * cc:128 * (cc + 1), :],
                               in_=x2t[cc % 4][:]).then_inc(s_ost[cc % 4], 16)
                if cc + 4 < nchunks:
                    sync.wait_ge(s_ost[cc % 4], 16 * (cc // 4 + 1))
                    sync.dma_start(out=x2t[cc % 4][:],
                                   in_=x1acc[128 * (cc + 4):128 * (cc + 5), :]
                                   ).then_inc(s_x2ld[cc % 4], 16)

        @block.sync
        def _(sync):
            try:
                _sync_body(sync)
            except StopSync:
                _dummy_out(sync)

        @block.gpsimd
        def _(gpsimd):
            for j in range(HT if stop >= 2 else 0):
                gpsimd.dma_start(out=xbf_dram[:, 128 * j:128 * (j + 1)],
                                 in_=x_d[:, 128 * j:128 * (j + 1)]).then_inc(s_cast, 16)

            _wcn = [0]

            def wcast(which, e):
                if _wcn[0] > 0:
                    gpsimd.wait_ge(s_wc, 16 * _wcn[0])
                src_ap = w1_d[e] if which == 1 else w2_d[e]
                dst_ap = w1bf_d[e] if which == 1 else w2bf_d[e]
                gpsimd.dma_start(out=dst_ap, in_=src_ap).then_inc(s_wc, 16)
                _wcn[0] += 1

            bcreg = [None]

            def scatters(e, lo=0, hi=None):
                if hi is None:
                    hi = nchunks
                if e == 0 and lo == 0:
                    gpsimd.wait_ge(s_vsel, SEL_V_TOTAL)
                    gpsimd.wait_ge(s_aat, A_AT * nchunks)
                    bcreg[0] = gpsimd.to_reg(KSEL - 1)
                elif lo == 0:
                    gpsimd.wait_ge(s_scat[e - 1], 16 * nchunks)
                for ct in range(lo, hi):
                    gpsimd.indirect_dma_start(
                        compact[e], IOA(off_i[:, ct, e:e + 1], 0),
                        xbf[:, ct, :], None,
                        bounds_check=bcreg[0], oob_is_err=False,
                    ).then_inc(s_scat[e], 16)

            def b2load(e):
                if e >= 2:
                    gpsimd.wait_ge(s_tffn, (e - 1) * T_FFN)
                gpsimd.dma_start(out=b2bf2[e % 2][:],
                                 in_=b2_d[e:e + 1, :]).then_inc(s_b2p[e % 2], 16)

            def scatter_adds(e):
                gpsimd.wait_ge(s_x1st[0], 16 * ((nchunks + 1) // 2))
                gpsimd.wait_ge(s_x1st[1], 16 * (nchunks // 2))
                gpsimd.wait_ge(s_gsc, 16 * nslots * e)
                gpsimd.wait_ge(s_vffn, e * V_FFN + V_FFN)
                for sN in range(nslots):
                    gpsimd.indirect_dma_start(
                        x1acc, IOA(idx_i[:, sN:sN + 1], 0),
                        outsb[:, sN, :], None,
                        compute_op=ALU.add,
                    ).then_inc(s_gsc, 16)

            if stop == 5:
                for e in range(E):
                    scatters(e)
            if stop >= 6:
                gpsimd.wait_ge(s_xt, 16 * HT)
                for e in range(3):
                    wcast(1, e)
                    wcast(2, e)
                scatters(0)
                b2load(0); b2load(1)
            for e in range(E if stop >= 6 else 0):
                if e > 0:
                    scatter_adds(e - 1)
                if e + 1 < E:
                    scatters(e + 1)
                if e + 3 < E:
                    wcast(1, e + 3)
                    wcast(2, e + 3)
                if e + 2 < E:
                    b2load(e + 2)
            if stop >= 6:
                scatter_adds(E - 1)
            for _i in range(4):
                gpsimd.wait_ge(s_ost[_i], 16 * ((nchunks + 3 - _i) // 4))

        @block.tensor
        def _(tensor):
            tensor.nop()
            if stop < 1:
                return
            for wname in ["wq", "wk", "wv"]:
                cast_key = {"wq": "cwq", "wk": "cwk", "wv": "cwv"}[wname]
                for j in range(8):
                    if j == 0:
                        tensor.wait_ge(s_vsw, VS[cast_key])
                    if j >= 2:
                        tensor.wait_ge(s_vsw, VS[f"e{wname}{j - 2}"])
                    dst = ps_tr[:, 0:QDIM] if j % 2 == 0 else ps_tr2[:, 0:QDIM]
                    tensor.transpose(dst, stag_b[:, 128 * j:128 * (j + 1)],
                                     identb[0:QDIM, 0:QDIM]).then_inc(s_tsw, 1)
            for j in range(8):
                tensor.wait_ge(s_vsw, VS["cwo"] if j < 2 else VS[f"ewo{j - 2}"])
                dst = ps_tr[0:QDIM, 0:128] if j % 2 == 0 else ps_tr2[0:QDIM, 0:128]
                tensor.transpose(dst, wo_b[:, j, :], identb[0:128, 0:128]).then_inc(s_tsw, 1)
            tensor.wait_ge(s_ldg, 16)
            for j in range(8):
                if j >= 1:
                    tensor.wait_ge(s_vsw, VS[f"ewg{j - 1}"])
                tensor.transpose(ps_g[:], stag_f[0:E, 128 * j:128 * (j + 1)],
                                 identf[0:E, 0:E]).then_inc(s_tsw, 1)
            for j in range(FT):
                c, cc2 = divmod(j, 4)
                tensor.wait_ge(s_b1ld, 16 * (c + 1))
                if j == 0:
                    tensor.wait_ge(s_vsw, VS["ewg7"])
                if j > 0:
                    tensor.wait_ge(s_vsw, VS[f"eb1_{j - 1}"])
                tensor.transpose(ps_g[:], b1tmp[:, 128 * cc2:128 * (cc2 + 1)],
                                 identf[0:8, 0:8]).then_inc(s_tsw, 1)

            if stop < 2:
                return
            tensor.wait_ge(s_xt, 16 * HT)
            for g in range(ngroups):
                sl = slice(gw * g, gw * (g + 1))
                if g > 0:
                    tensor.wait_ge(s_vprj, 5 * (g - 1) + 2)
                for j in range(HT):
                    mm = tensor.matmul(ps_a[0:QDIM, 0:gw], wqT[:, j, :], xT(j)[:, sl],
                                       start=(j == 0), stop=(j == HT - 1))
                mm.then_inc(s_tprj, 1)
                if g > 0:
                    tensor.wait_ge(s_vprj, 5 * (g - 1) + 4)
                for j in range(HT):
                    mm = tensor.matmul(ps_b[0:QDIM, 0:gw], wkT[:, j, :], xT(j)[:, sl],
                                       start=(j == 0), stop=(j == HT - 1))
                mm.then_inc(s_tprj, 1)
                if g > 0:
                    tensor.wait_ge(s_vprj, 5 * (g - 1) + 5)
                for j in range(HT):
                    mm = tensor.matmul(ps_c[0:QDIM, 0:gw], wvT[:, j, :], xT(j)[:, sl],
                                       start=(j == 0), stop=(j == HT - 1))
                mm.then_inc(s_tprj, 1)

            if stop < 3:
                return
            def t_qkv_head(c):
                csl_ = slice(128 * c, 128 * (c + 1))
                if c > 0:
                    tensor.wait_ge(s_vat, vam(c - 1))
                tensor.matmul(ps_c[:, 0:128], KT[:, csl_], QT[:, csl_],
                              start=True, stop=True).then_inc(s_tat, 1)
                if c > 0:
                    tensor.wait_ge(s_aat, andt(c - 1))
                tensor.transpose(ps_tr[:, 0:QDIM], KT[:, csl_],
                                 identb[0:QDIM, 0:QDIM]).then_inc(s_tat, 1)
                if c > 0:
                    tensor.wait_ge(s_aat, av1(c - 1))
                tensor.transpose(ps_tr2[:, 0:QDIM], VT[:, csl_],
                                 identb[0:QDIM, 0:QDIM]).then_inc(s_tat, 1)

            def t_attn_body(c):
                csl_ = slice(128 * c, 128 * (c + 1))
                tensor.wait_ge(s_vat, vam(c))
                tensor.wait_ge(s_aat, av1(c))
                mm = tensor.matmul(ps_n[:, 0:QDIM + 1], am_sb[:], v1_sb[:],
                                   start=True, stop=(c == 0))
                if c > 0:
                    mm = tensor.matmul(ps_n[:, 0:QDIM + 1], QT[:, csl_], kv_sb[:],
                                       start=False, stop=True)
                mm.then_inc(s_tat, 1)
                tensor.wait_ge(s_vat, vst(c))
                tensor.matmul(ps_state[:], k1_sb[:], v1_sb[:],
                              start=True, stop=True).then_inc(s_tat, 1)
                tensor.wait_ge(s_vat, vnd(c))
                tensor.wait_ge(s_aat, ak1(c))
                tensor.transpose(ps_tr[0:QDIM, 0:128], nd_sb[:],
                                 identb[0:128, 0:128]).then_inc(s_tat, 1)
                tensor.wait_ge(s_aat, andt(c))
                tensor.matmul(ps_a[:, 0:512], ndT_sb[:], woT[:, 0:512],
                              start=True, stop=True)
                tensor.matmul(ps_b[:, 0:512], ndT_sb[:], woT[:, 512:1024],
                              start=True, stop=True).then_inc(s_tat, 1)

            tensor.wait_ge(s_vprj, V_PRJ)
            t_qkv_head(0)
            t_attn_body(0)
            for cc in range(nchunks):
                par_t = x_tile[cc % 2]
                for j in range(HT):
                    if j == 0:
                        tensor.wait_ge(s_vat, vpu(cc))
                    pst_x = [ps_a, ps_b][j // 4]
                    tensor.transpose(pst_x[:, 128 * (j % 4):128 * (j % 4 + 1)],
                                     par_t[:, 128 * j:128 * (j + 1)],
                                     identf[:]).then_inc(s_tat, 1)
                if cc + 1 < nchunks:
                    t_qkv_head(cc + 1)
                tensor.wait_ge(s_aat, aevB(cc))
                for j in range(HT):
                    mm = tensor.matmul(ps_g[:, 0:E],
                                       x1T_stage[:, 128 * j:128 * (j + 1)],
                                       wgT[:, j, :], start=(j == 0), stop=(j == HT - 1))
                mm.then_inc(s_tat, 1)
                if cc + 1 < nchunks:
                    t_attn_body(cc + 1)

            if stop < 4:
                return
            for i in range(BIS):
                tensor.wait_ge(s_vsel, 2 * i + 1)
                if i > 0:
                    tensor.wait_ge(s_vsel, 2 * (i - 1) + 2)
                tensor.matmul(ps_n[:, 0:S16], eg16[:], cmpb[:],
                              start=True, stop=True).then_inc(s_tsel, 1)
            tensor.wait_ge(s_sel2, 32)
            tensor.matmul(ps_g[:, 0:E], ones_f[:], lo8[:],
                          start=True, stop=True).then_inc(s_tsel, 1)
            tensor.wait_ge(s_vsel, 2 * BIS + 4)
            tensor.matmul(ps_n[:, 0:E], tris[:], rsum[:],
                          start=True, stop=True).then_inc(s_tsel, 1)

            if stop < 6:
                return
            for e in range(E):
                TB = e * T_FFN
                AB = e * A_FFN
                VB = e * V_FFN
                tensor.wait_ge(s_exl2[e % 2], 16 * 9 * (e // 2 + 1))
                if e == 0:
                    tensor.wait_ge(s_vat, V_AT * nchunks)
                    tensor.wait_ge(s_vsel, SEL_V_TOTAL)
                if e > 0:
                    tensor.wait_ge(s_vffn, (e - 1) * V_FFN + 1)
                for sN in range(nslots):
                    tensor.transpose(ps_tr[:, 32 * sN:32 * (sN + 1)],
                                     exT2[e % 2][0:32, 8 * KSEL + 128 * sN:8 * KSEL + 128 * (sN + 1)],
                                     identb[0:32, 0:32]).then_inc(s_tffn, 1)
                if e > 0:
                    tensor.wait_ge(s_vffn, e * V_FFN)
                for g8 in range(8):
                    gi = e * 8 + g8
                    tensor.wait_ge(s_wl1[gi % 3], 16 * (gi // 3 + 1))
                    for jj in range(4):
                        kk = g8 * 4 + jj
                        psum = [ps_a, ps_b][kk % 2]
                        if AB + kk - 1 > 0:
                            tensor.wait_ge(s_affn, AB + kk - 1)
                        for kt in range(HT):
                            mm = tensor.matmul(psum[:, 0:KSEL],
                                               w1q[gi % 3][:, kt, 128 * jj:128 * (jj + 1)],
                                               exT2[e % 2][:, kt * KSEL:(kt + 1) * KSEL],
                                               start=(kt == 0), stop=(kt == HT - 1))
                        mm.then_inc(s_tffn, 1)
                tensor.wait_ge(s_affn, AB + FT)
                tensor.wait_ge(s_vffn, VB + 3)
                tensor.wait_ge(s_b2p[e % 2], 16 * (e // 2 + 1))
                PS4 = [ps_c, ps_n, ps_a, ps_b]
                for p2 in range(2):
                    q0, q1 = 2 * p2, 2 * p2 + 1
                    gi0, gi1 = e * 4 + q0, e * 4 + q1
                    tensor.wait_ge(s_wl2[gi0 % 4], 16 * (gi0 // 4 + 1))
                    tensor.wait_ge(s_wl2[gi1 % 4], 16 * (gi1 // 4 + 1))
                    for sN in range(nslots):
                        it = p2 * nslots + sN
                        psA = PS4[(2 * it) % 4]
                        psB = PS4[(2 * it + 1) % 4]
                        for half in range(2):
                            k = it * 2 + half
                            if k >= 4:
                                tensor.wait_ge(s_vffn, VB + 3 + (k - 4) + 1)
                            elif e > 0:
                                tensor.wait_ge(s_vffn,
                                               (e - 1) * V_FFN + 3 + (4 * nslots - 4 + k % 4) + 1)
                        for f in range(FT):
                            tensor.matmul(psA[:, 0:256],
                                          h1T(f)[:, 128 * sN:128 * (sN + 1)],
                                          w2q[gi0 % 4][:, f, :],
                                          start=(f == 0), stop=False)
                            tensor.matmul(psB[:, 0:256],
                                          h1T(f)[:, 128 * sN:128 * (sN + 1)],
                                          w2q[gi1 % 4][:, f, :],
                                          start=(f == 0), stop=False)
                        tensor.matmul(psA[:, 0:256], ones_b[0:1, :],
                                      b2bf2[e % 2][0:1, 256 * q0:256 * (q0 + 1)],
                                      start=False, stop=True).then_inc(s_tffn, 1)
                        tensor.matmul(psB[:, 0:256], ones_b[0:1, :],
                                      b2bf2[e % 2][0:1, 256 * q1:256 * (q1 + 1)],
                                      start=False, stop=True).then_inc(s_tffn, 1)

        @block.vector
        def _(vector):
            vcnt = [0]

            def vg(ins, milestone=None, n=1):
                ins.then_inc(s_vself, 1)
                vcnt[0] += 1
                vector.wait_ge(s_vself, vcnt[0])
                if milestone is not None:
                    vector.sem_inc(milestone, n)

            vg(vector.memset(ones_f[:], 1.0), s_vsw)
            vg(vector.memset(v1_sb[:, QDIM:QDIM + 1], 1.0), s_vsw)
            vg(vector.memset(xbf[:, :, 1042:ROWB], 0), s_vsw)
            vector.wait_ge(s_sw, 160)
            vg(vector.tensor_copy(xbf[:, :, 1040:1042],
                                  ids_sb[:].rearrange("p (c t) -> p c t", t=2)), s_vsw)
            vg(vector.memset(state_f[:], 0.0))
            vg(vector.memset(bcol[:, 0:1], 0.0))
            vg(vector.memset(bcol[:, 1:2], 1.0))
            vg(vector.memset(ones_b[:], 1.0), s_vsw)

            def _evacs(wname, wbuf):
                for j in range(8):
                    vector.wait_ge(s_tsw, TS[f"t{wname}{j}"])
                    if wname == "wo":
                        s = ps_tr[0:QDIM, 0:128] if j % 2 == 0 else ps_tr2[0:QDIM, 0:128]
                        vg(vector.tensor_copy(woT[:, 128 * j:128 * (j + 1)], s), s_vsw)
                    elif wname == "wg":
                        s = ps_tr[:, 0:E] if j % 2 == 0 else ps_tr2[:, 0:E]
                        vg(vector.tensor_copy(wgT[:, j, :], s), s_vsw)
                    else:
                        s = ps_tr[:, 0:QDIM] if j % 2 == 0 else ps_tr2[:, 0:QDIM]
                        vg(vector.tensor_copy(wbuf[:, j, :], s), s_vsw)

            vg(vector.tensor_copy(stag_b[:], stag_f[0:QDIM, :]), s_vsw)
            if stop >= 1:
                _evacs("wq", wqT)
            vector.wait_ge(s_ldk, 16)
            vector.wait_ge(s_tsw, TS["twq7"])
            vg(vector.tensor_copy(stag_b[:], stag_f[0:QDIM, :]), s_vsw)
            if stop >= 1:
                _evacs("wk", wkT)
            vector.wait_ge(s_ldv, 16)
            vector.wait_ge(s_tsw, TS["twk7"])
            vg(vector.tensor_copy(stag_b[:], stag_f[0:QDIM, :]), s_vsw)
            if stop >= 1:
                _evacs("wv", wvT)
            vector.wait_ge(s_ldo, 16)
            vg(vector.tensor_copy(wo_b[:].rearrange("p j m -> p (j m)"),
                                  stag_f[:, 0:512]), s_vsw)
            if stop >= 1:
                _evacs("wo", None)
            if stop >= 1:
                for j in range(8):
                    vector.wait_ge(s_tsw, TS[f"twg{j}"])
                    vg(vector.tensor_copy(wgT[:, j, :], ps_g[:, 0:E]), s_vsw)
            if stop < 1:
                return
            for j in range(FT):
                vector.wait_ge(s_tsw, TS[f"tb1_{j}"])
                vg(vector.tensor_copy(b1f[:, j, :], ps_g[:]), s_vsw)
            vector.sem_inc(s_vsw, 1)

            if stop < 2:
                return
            for g in range(ngroups):
                sl = slice(gw * g, gw * (g + 1))
                vector.wait_ge(s_tprj, 3 * g + 1)
                vg(vector.tensor_scalar_min(stag_f[0:QDIM, 0:gw],
                                            ps_a[0:QDIM, 0:gw], 0.0), s_vprj)
                vector.wait_ge(s_aprj, 2 * g + 1)
                vg(vector.tensor_scalar_max(stag_f[0:QDIM, 0:gw],
                                            ps_a[0:QDIM, 0:gw], 0.0))
                vg(vector.tensor_add(QT[:, sl], stag_b[:, 0:gw],
                                     stag_f[0:QDIM, 0:gw]), s_vprj)
                vector.wait_ge(s_tprj, 3 * g + 2)
                vg(vector.tensor_scalar_min(stag_f[0:QDIM, 0:gw],
                                            ps_b[0:QDIM, 0:gw], 0.0), s_vprj)
                vector.wait_ge(s_aprj, 2 * g + 2)
                vg(vector.tensor_scalar_max(stag_f[0:QDIM, 0:gw],
                                            ps_b[0:QDIM, 0:gw], 0.0))
                vg(vector.tensor_add(KT[:, sl], stag_b[:, 0:gw],
                                     stag_f[0:QDIM, 0:gw]), s_vprj)
                vector.wait_ge(s_tprj, 3 * g + 3)
                vg(vector.tensor_copy(VT[:, sl], ps_c[0:QDIM, 0:gw]), s_vprj)

            inv_d = 1.0 / D
            if stop < 3:
                return
            vector.wait_ge(s_tat, tA(0))
            vg(vector.tensor_tensor(am_sb[:], ps_c[:, 0:128], triu[:],
                                    ALU.mult), s_vat)
            vector.sem_inc(s_vat, 1)
            for cc in range(nchunks):
                vector.wait_ge(s_tat, tnum(cc))
                vg(vector.tensor_scalar_add(scal[:, 9:10], ps_n[:, QDIM:QDIM + 1], 1e-6))
                vg(vector.reciprocal(scal[:, 10:11], scal[:, 9:10]))
                vg(vector.tensor_scalar(nd_sb[:], ps_n[:, 0:QDIM], scal[:, 10:11],
                                        None, ALU.mult), s_vat)
                vector.wait_ge(s_tat, topj(cc))
                if cc >= 2:
                    vector.wait_ge(s_xch[cc % 2], 16 * (cc // 2))
                par = x_tile[cc % 2]
                vg(vector.scalar_tensor_tensor(par[:, 0:512], par[:, 0:512], 1.0,
                                               ps_a[:, 0:512], ALU.mult, ALU.add,
                                               accum_out=scal[:, 0:1]), s_vat)
                vg(vector.scalar_tensor_tensor(par[:, 512:1024], par[:, 512:1024], 1.0,
                                               ps_b[:, 0:512], ALU.mult, ALU.add,
                                               accum_out=scal[:, 1:2]), s_vat)
                vg(vector.tensor_add(scal[:, 2:3], scal[:, 0:1], scal[:, 1:2]))
                vg(vector.tensor_scalar_mul(scal[:, 3:4], scal[:, 2:3], inv_d))
                vg(vector.tensor_mul(scal[:, 4:5], scal[:, 3:4], scal[:, 3:4]))
                vector.wait_ge(s_aat, asq(cc))
                vg(vector.scalar_tensor_tensor(scal[:, 5:6], scal[:, 14:15], inv_d,
                                               scal[:, 4:5], ALU.mult, ALU.subtract),
                   s_vat)
                vector.wait_ge(s_aat, ar(cc))
                vg(vector.scalar_tensor_tensor(scal[:, 8:9], scal[:, 3:4], -1.0,
                                               scal[:, 7:8], ALU.mult, ALU.mult),
                   s_vat)
                vg(vector.scalar_tensor_tensor(
                    par[:], par[:], scal[:, 7:8],
                    scal[:, 8:9].broadcast_to((128, D)),
                    ALU.mult, ALU.add), s_vat)
                if cc + 1 < nchunks:
                    vector.wait_ge(s_tat, tA(cc + 1))
                    vg(vector.tensor_tensor(am_sb[:], ps_c[:, 0:128], triu[:],
                                            ALU.mult), s_vat)
                    vector.wait_ge(s_tat, tdel(cc))
                    vector.wait_ge(s_aat, akv(cc))
                    vg(vector.tensor_tensor(state_f[:], state_f[:], ps_state[:],
                                            ALU.add), s_vat)
                vector.wait_ge(s_tat, tgate(cc))
                vg(vector.tensor_copy(lg_sb[:], ps_g[:, 0:E]))
                vg(vector.reduce_max(scal[:, 15:16], lg_sb[:], AX))
                vg(vector.tensor_scalar_mul(scal[:, 13:14], scal[:, 15:16], -1.0),
                   s_vat)
                vector.wait_ge(s_aat, aexp(cc))
                vg(vector.reciprocal(scal[:, 12:13], scal[:, 11:12]))
                vg(vector.tensor_scalar(sm_sb[:, cc, :], e_sb[:], scal[:, 12:13], None,
                                        ALU.mult))
                vg(vector.tensor_copy(xbf[:, cc, 1024:1032], sm_sb[:, cc, :]))
                vg(vector.tensor_tensor(xbf[:, cc, 1032:1040], sm_sb[:, cc, :],
                                        xbf[:, cc, 1024:1032],
                                        ALU.subtract), s_vat)

            if stop < 4:
                return
            vector.wait_ge(s_bisld, 16 * 8)
            for i in range(BIS):
                vg(vector.tensor_add(bcol[:, 3:4], bcol[:, 0:1], bcol[:, 1:2]))
                vg(vector.tensor_scalar_mul(bcol[:, 2:3], bcol[:, 3:4], 0.5))
                vg(vector.tensor_scalar(cmpb[:], smbis[:], bcol[:, 2:3], None,
                                        ALU.is_gt), s_vsel)
                vector.wait_ge(s_tsel, i + 1)
                vg(vector.reduce_sum(bcol[:, 4:5], ps_n[:, 0:S16], AX))
                vg(vector.tensor_scalar(predu[:], bcol[:, 4:5], float(KSEL), None,
                                        ALU.is_ge))
                vg(vector.tensor_scalar(npredu[:], bcol[:, 4:5], float(KSEL), None,
                                        ALU.is_lt))
                vg(vector.select(bcol[:, 0:1], predu[:], bcol[:, 2:3], bcol[:, 0:1],
                                 add_drain=True))
                vg(vector.select(bcol[:, 1:2], npredu[:], bcol[:, 2:3], bcol[:, 1:2],
                                 add_drain=True), s_vsel)
            vector.wait_ge(s_tsel, BIS + 1)
            vg(vector.tensor_copy(lo_tm[:], ps_g[:, 0:E]))
            vg(vector.tensor_tensor(cmp_tm[:], sm_sb[:],
                                    lo_tm[:].unsqueeze(1).broadcast_to((128, nchunks, E)),
                                    ALU.is_gt), s_vsel)
            vg(vector.tensor_copy(cmpb_tm[:], cmp_tm[:]), s_vsel, 2)
            with nc.allow_low_precision(reason="0/1 counts <=16 exact in bf16"):
                vg(vector.reduce_sum(rsum[:].unsqueeze(2),
                                     cmpb_tm[:].rearrange("p c e -> p e c"), AX),
                   s_vsel)
            cur, nxt = cmp_tm, pbufA
            for k2 in prefix_steps:
                vg(vector.tensor_tensor(nxt[:, k2:, :], cur[:, k2:, :],
                                        cur[:, 0:nchunks - k2, :], ALU.add))
                vg(vector.tensor_copy(nxt[:, 0:k2, :], cur[:, 0:k2, :]))
                cur, nxt = nxt, (pbufB if nxt is pbufA else pbufA)
            vg(vector.tensor_sub(nxt[:], cur[:], cmp_tm[:]))
            excl_ct = nxt
            vector.wait_ge(s_tsel, BIS + 2)
            vg(vector.tensor_copy(exclp[:], ps_n[:, 0:E]))
            rank = cur
            vg(vector.tensor_tensor(rank[:], excl_ct[:],
                                    exclp[:].unsqueeze(1).broadcast_to((128, nchunks, E)),
                                    ALU.add))
            vg(vector.scalar_tensor_tensor(rank[:], cmp_tm[:], -4096.0, rank[:],
                                           ALU.mult, ALU.add))
            vg(vector.tensor_scalar_add(rank[:], rank[:], 4096.0))
            vg(vector.tensor_copy(off_i[:], rank[:]), s_vsel, RANK_EXTRA - 4)

            if stop < 6:
                return
            for e in range(E):
                TB = e * T_FFN
                VB = e * V_FFN
                vector.wait_ge(s_tffn, TB + nslots)
                if e > 0:
                    vector.wait_ge(s_gsc, 16 * nslots * e)
                vg(vector.tensor_copy(exv[:, 0:32 * nslots],
                                      ps_tr[:, 0:32 * nslots]), s_vffn)
                for sN in range(nslots):
                    o = 32 * sN
                    vg(vector.tensor_tensor(G_sb[:, sN:sN + 1], exv[:, o + e:o + e + 1],
                                            exv[:, o + 8 + e:o + 8 + e + 1], ALU.add))
                    vg(vector.scalar_tensor_tensor(idf_sb[:, sN:sN + 1],
                                                   exv[:, o + 17:o + 18], 256.0,
                                                   exv[:, o + 16:o + 17],
                                                   ALU.mult, ALU.add))
                vg(vector.tensor_copy(idx_i[:], idf_sb[:]), s_vffn)
                vector.sem_inc(s_vffn, 1)
                PS4 = [ps_c, ps_n, ps_a, ps_b]
                for p2 in range(2):
                    for sN in range(nslots):
                        it = p2 * nslots + sN
                        for half in range(2):
                            hq = 2 * p2 + half
                            psum = PS4[(2 * it + half) % 4]
                            vector.wait_ge(s_tffn, TB + nslots + FT + it * 2 + half + 1)
                            vg(vector.tensor_scalar(outsb[:, sN, 256 * hq:256 * (hq + 1)],
                                                    psum[:, 0:256], G_sb[:, sN:sN + 1],
                                                    None, ALU.mult), s_vffn)

            def _l2sum(c):
                L = LC[c % 2]
                vector.wait_ge(s_x2ld[c % 4], 16 * (c // 4 + 1))
                par_ = x2t[c % 4]
                vg(vector.reduce_sum(scal[:, L[0]:L[0] + 1], par_[:], AX))
                vg(vector.tensor_scalar_mul(scal[:, L[1]:L[1] + 1],
                                            scal[:, L[0]:L[0] + 1], inv_d))
                vg(vector.tensor_mul(scal[:, L[2]:L[2] + 1],
                                     scal[:, L[1]:L[1] + 1],
                                     scal[:, L[1]:L[1] + 1]))

            _l2sum(0)
            for cc in range(nchunks):
                L = LC[cc % 2]
                par = x2t[cc % 4]
                vector.wait_ge(s_aat, A_AT * nchunks + 2 * cc + 1)
                vg(vector.scalar_tensor_tensor(scal[:, L[3]:L[3] + 1],
                                               scal[:, L[7]:L[7] + 1], inv_d,
                                               scal[:, L[2]:L[2] + 1],
                                               ALU.mult, ALU.subtract),
                   s_vat)
                if cc + 1 < nchunks:
                    _l2sum(cc + 1)
                vector.wait_ge(s_aat, A_AT * nchunks + 2 * cc + 2)
                vg(vector.scalar_tensor_tensor(scal[:, L[6]:L[6] + 1],
                                               scal[:, L[1]:L[1] + 1], -1.0,
                                               scal[:, L[5]:L[5] + 1],
                                               ALU.mult, ALU.mult),
                   s_vat)
                vg(vector.scalar_tensor_tensor(
                    par[:], par[:], scal[:, L[5]:L[5] + 1],
                    scal[:, L[6]:L[6] + 1].broadcast_to((128, D)),
                    ALU.mult, ALU.add), s_vat)

        @block.scalar
        def _(scalar):
            acnt = [0]

            def ag(ins):
                ins.then_inc(s_aself, 1)
                acnt[0] += 1
                scalar.wait_ge(s_aself, acnt[0])

            scalar.nop()
            if stop < 2:
                return
            for g in range(ngroups):
                scalar.wait_ge(s_vprj, 5 * g + 1)
                scalar.activation(stag_b[:, 0:gw], stag_f[0:QDIM, 0:gw],
                                  AF.Exp).then_inc(s_aprj, 1)
                scalar.wait_ge(s_vprj, 5 * g + 3)
                scalar.activation(stag_b[:, 0:gw], stag_f[0:QDIM, 0:gw],
                                  AF.Exp).then_inc(s_aprj, 1)
            if stop < 3:
                return
            scalar.wait_ge(s_vat, vst(0))
            scalar.activation(kv_sb[:], state_f[:],
                              AF.Copy).then_inc(s_aat, 1)
            scalar.wait_ge(s_tat, tK(0))
            scalar.activation(k1_sb[:], ps_tr[:, 0:QDIM],
                              AF.Copy).then_inc(s_aat, 1)
            scalar.wait_ge(s_tat, tV(0))
            scalar.activation(v1_sb[:, 0:QDIM], ps_tr2[:, 0:QDIM],
                              AF.Copy).then_inc(s_aat, 1)
            scalar.wait_ge(s_tat, tndt(0))
            scalar.activation(ndT_sb[:], ps_tr[0:QDIM, 0:128],
                              AF.Copy).then_inc(s_aat, 1)
            for cc in range(nchunks):
                par = x_tile[cc % 2]
                scalar.wait_ge(s_vat, vaddb(cc))
                scalar.activation(sq_scr[:], par[:], AF.Square,
                                  accum_out=scal[:, 14:15]).then_inc(s_aat, 1)
                scalar.wait_ge(s_vat, vve(cc))
                ag(scalar.activation(scal[:, 6:7], scal[:, 5:6], AF.Ln, bias=C_EPS))
                scalar.activation(scal[:, 7:8], scal[:, 6:7], AF.Exp,
                                  scale=-0.5).then_inc(s_aat, 1)
                scalar.wait_ge(s_vat, vpu(cc))
                scalar.activation(xbf[:, cc, 0:1024], par[:],
                                  AF.Copy).then_inc(s_aat, 1)
                scalar.wait_ge(s_tat, ttr(cc, 3))
                scalar.activation(x1T_stage[:, 0:512], ps_a[:, 0:512],
                                  AF.Copy).then_inc(s_aat, 1)
                scalar.wait_ge(s_tat, ttr(cc, 7))
                scalar.activation(x1T_stage[:, 512:1024], ps_b[:, 0:512],
                                  AF.Copy).then_inc(s_aat, 1)
                if cc + 1 < nchunks:
                    scalar.wait_ge(s_vat, vst(cc + 1))
                    scalar.activation(kv_sb[:], state_f[:],
                                      AF.Copy).then_inc(s_aat, 1)
                    scalar.wait_ge(s_tat, tK(cc + 1))
                    scalar.activation(k1_sb[:], ps_tr[:, 0:QDIM],
                                      AF.Copy).then_inc(s_aat, 1)
                    scalar.wait_ge(s_tat, tV(cc + 1))
                    scalar.activation(v1_sb[:, 0:QDIM], ps_tr2[:, 0:QDIM],
                                      AF.Copy).then_inc(s_aat, 1)
                scalar.wait_ge(s_vat, vlg(cc))
                scalar.activation(e_sb[:], lg_sb[:], AF.Exp, bias=scal[:, 13:14],
                                  accum_out=scal[:, 11:12]).then_inc(s_aat, 1)
                if cc + 1 < nchunks:
                    scalar.wait_ge(s_tat, tndt(cc + 1))
                    scalar.activation(ndT_sb[:], ps_tr[0:QDIM, 0:128],
                                      AF.Copy).then_inc(s_aat, 1)
            if stop < 4:
                return
            scalar.wait_ge(s_smd, 16)
            for e in range(4, 8):
                scalar.dma_start(
                    out=smbis[16 * e:16 * (e + 1), :].unsqueeze(2),
                    in_=smd.rearrange("(pp c) e -> pp c e", pp=16)[:, :, e:e + 1],
                    allow_slow_non_contiguous=True).then_inc(s_bisld, 16)
            if stop < 6:
                return
            for e in range(E):
                TB = e * T_FFN
                AB = e * A_FFN
                for kk in range(FT):
                    psum = [ps_a, ps_b][kk % 2]
                    scalar.wait_ge(s_tffn, TB + nslots + kk + 1)
                    scalar.activation(h1T(kk), psum[:, 0:KSEL], AF.Gelu_apprx_tanh,
                                      bias=b1f[:, kk, e:e + 1]).then_inc(s_affn, 1)
            for cc in range(nchunks):
                L = LC[cc % 2]
                par = x2t[cc % 4]
                scalar.wait_ge(s_x2ld[cc % 4], 16 * (cc // 4 + 1))
                scalar.activation(sq_scr[:], par[:], AF.Square,
                                  accum_out=scal[:, L[7]:L[7] + 1]).then_inc(s_aat, 1)
                scalar.wait_ge(s_vat, V_AT * nchunks + 3 * cc + 1)
                ag(scalar.activation(scal[:, L[4]:L[4] + 1], scal[:, L[3]:L[3] + 1],
                                     AF.Ln, bias=C_EPS))
                scalar.activation(scal[:, L[5]:L[5] + 1], scal[:, L[4]:L[4] + 1],
                                  AF.Exp, scale=-0.5).then_inc(s_aat, 1)

    return nc


def _consts(S):
    nchunks = S // 128
    i = np.arange(128)
    triu = (i[:, None] <= i[None, :]).astype(ml_dtypes.bfloat16)
    tris = (i[:, None] < i[None, :]).astype(ml_dtypes.bfloat16)
    eg = (i[:, None] // 16 == i[None, :] // 16).astype(ml_dtypes.bfloat16)
    identf = np.eye(128, dtype=np.float32)
    identb = np.eye(128).astype(ml_dtypes.bfloat16)
    t = i[:, None] + 128 * np.arange(nchunks)[None, :]
    ids = np.stack([t % 256, t // 256], axis=-1).reshape(128, nchunks * 2)
    ids = ids.astype(ml_dtypes.bfloat16)
    csts = np.tile(np.array([EPS, 0, 0, 0, 0, 0, 0, 0], np.float32), (128, 1))
    return dict(triu=triu, tris=tris, eg16=eg, identf=identf, identb=identb,
                idsc=ids, csts=csts)


def kernel(L0_attn_x, L0_attn_Wq, L0_attn_Wk, L0_attn_Wv, L0_attn_Wo,
           L0_ffn_Wg, L0_ffn_W1, L0_ffn_b1, L0_ffn_W2, L0_ffn_b2):
    B, S, _ = L0_attn_x.shape
    nc = build_program(S=S, KSEL=512, BIS=34)
    consts = _consts(S)
    base = dict(wq=np.ascontiguousarray(L0_attn_Wq),
                wk=np.ascontiguousarray(L0_attn_Wk),
                wv=np.ascontiguousarray(L0_attn_Wv),
                wo=np.ascontiguousarray(L0_attn_Wo),
                wg=np.ascontiguousarray(L0_ffn_Wg),
                w1=np.ascontiguousarray(L0_ffn_W1),
                b1=np.ascontiguousarray(L0_ffn_b1),
                w2=np.ascontiguousarray(L0_ffn_W2),
                b2=np.ascontiguousarray(L0_ffn_b2),
                **consts)
    in_maps = [dict(x=np.ascontiguousarray(L0_attn_x[b]), **base) for b in range(B)]
    res = run_bass_kernel_spmd(nc, in_maps, core_ids=list(range(B)))
    return np.stack([np.asarray(r["out"]) for r in res.results], axis=0)


# version 14
# speedup vs baseline: 1.2802x; 1.2802x over previous
import numpy as np
import ml_dtypes
import concourse.bass as bass
import concourse.mybir as mybir
from concourse.bass_utils import run_bass_kernel_spmd
from contextlib import ExitStack

F32 = mybir.dt.float32
BF16 = mybir.dt.bfloat16
I32 = mybir.dt.int32
U8 = mybir.dt.uint8
AF = mybir.ActivationFunctionType
ALU = mybir.AluOpType
IOA = bass.IndirectOffsetOnAxis
AX = mybir.AxisListType.X

D = 1024
HT = 8
E = 8
F = 4096
FT = 32
QDIM = 64
ROWB = 1152
EPS = 1e-5


def build_program(S=2048, KSEL=512, BIS=34, stop=6):
    nchunks = S // 128
    gw = min(512, S)
    ngroups = S // gw
    nslots = KSEL // 128
    S16 = S // 16
    prefix_steps = []
    k = 1
    while k < nchunks:
        prefix_steps.append(k)
        k *= 2

    nc = bass.Bass("TRN2", target_bir_lowering=False)

    x_d = nc.dram_tensor("x", [S, D], F32, kind="ExternalInput").ap()
    wq_d = nc.dram_tensor("wq", [QDIM, D], F32, kind="ExternalInput").ap()
    wk_d = nc.dram_tensor("wk", [QDIM, D], F32, kind="ExternalInput").ap()
    wv_d = nc.dram_tensor("wv", [QDIM, D], F32, kind="ExternalInput").ap()
    wo_d = nc.dram_tensor("wo", [D, QDIM], F32, kind="ExternalInput").ap()
    wg_d = nc.dram_tensor("wg", [E, D], F32, kind="ExternalInput").ap()
    w1_d = nc.dram_tensor("w1", [E, D, F], F32, kind="ExternalInput").ap()
    b1_d = nc.dram_tensor("b1", [E, F], F32, kind="ExternalInput").ap()
    w2_d = nc.dram_tensor("w2", [E, F, D], F32, kind="ExternalInput").ap()
    b2_d = nc.dram_tensor("b2", [E, D], F32, kind="ExternalInput").ap()
    triu_d = nc.dram_tensor("triu", [128, 128], BF16, kind="ExternalInput").ap()
    tris_d = nc.dram_tensor("tris", [128, 128], BF16, kind="ExternalInput").ap()
    eg_d = nc.dram_tensor("eg16", [128, 128], BF16, kind="ExternalInput").ap()
    idf_d = nc.dram_tensor("identf", [128, 128], F32, kind="ExternalInput").ap()
    idb_d = nc.dram_tensor("identb", [128, 128], BF16, kind="ExternalInput").ap()
    ids_d = nc.dram_tensor("idsc", [128, nchunks * 2], BF16, kind="ExternalInput").ap()
    cst_d = nc.dram_tensor("csts", [128, 8], F32, kind="ExternalInput").ap()
    out_d = nc.dram_tensor("out", [S, D], F32, kind="ExternalOutput").ap()

    xbf_dram = nc.dram_tensor("xbfd", [S, D], BF16, kind="Internal").ap()
    smd = nc.dram_tensor("smd", [S, E], F32, kind="Internal").ap()
    x1acc = nc.dram_tensor("x1acc", [S, D], F32, kind="Internal").ap()
    compact = [nc.dram_tensor(f"compact{e}", [KSEL, ROWB], BF16, kind="Internal").ap()
               for e in range(E)]
    d8a = nc.dram_tensor("d8a", [8, 1], F32, kind="Internal").ap()

    VS = {}
    for i, name in enumerate([
            "ones", "v1col", "xzero", "ids", "lohi",
            "cwq", *[f"ewq{j}" for j in range(8)],
            "cwk", *[f"ewk{j}" for j in range(8)],
            "cwv", *[f"ewv{j}" for j in range(8)],
            "cwo", *[f"ewo{j}" for j in range(8)],
            *[f"ewg{j}" for j in range(8)],
            *[f"eb1_{j}" for j in range(32)],
            "b2c0"]):
        VS[name] = i + 1
    N_VS = len(VS)

    TS = {}
    for i, name in enumerate([
            *[f"twq{j}" for j in range(8)],
            *[f"twk{j}" for j in range(8)],
            *[f"twv{j}" for j in range(8)],
            *[f"two{j}" for j in range(8)],
            *[f"twg{j}" for j in range(8)],
            *[f"tb1_{j}" for j in range(32)]]):
        TS[name] = i + 1
    N_TS = len(TS)

    T_PRJ = 3 * ngroups
    V_PRJ = 5 * ngroups
    _N = S // 128
    LC = [[2, 3, 4, 5, 6, 7, 8, 14], [0, 1, 9, 10, 11, 12, 13, 15]]

    def tA(c):
        return 1 if c == 0 else 16 * c

    def tK(c):
        return 2 if c == 0 else 16 * c + 1

    def tV(c):
        return 3 if c == 0 else 16 * c + 2

    def tnum(c):
        return 4 if c == 0 else 16 * c + 4

    def tdel(c):
        return 5 if c == 0 else 16 * c + 5

    def tndt(c):
        return 6 if c == 0 else 16 * c + 6

    def topj(c):
        return 7 if c == 0 else 16 * c + 7

    def ttr(c, j):
        return 16 * c + 8 + j

    def tgate(c):
        return 16 * c + 19 if c < _N - 1 else 16 * c + 16

    def vam(c):
        return 1 if c == 0 else 10 * c - 1

    def vst(c):
        return 2 if c == 0 else 10 * c

    def vnd(c):
        return 10 * c + 3

    def vadda(c):
        return 10 * c + 4

    def vaddb(c):
        return 10 * c + 5

    def vve(c):
        return 10 * c + 6

    def vnmr(c):
        return 10 * c + 7

    def vpu(c):
        return 10 * c + 8

    def vlg(c):
        return 10 * c + 11 if c < _N - 1 else 10 * c + 9

    def vfin(c):
        return 10 * c + 12 if c < _N - 1 else 10 * c + 10

    def akv(c):
        return 1 if c == 0 else 10 * c

    def ak1(c):
        return 2 if c == 0 else 10 * c + 1

    def av1(c):
        return 3 if c == 0 else 10 * c + 2

    def andt(c):
        return 4 if c == 0 else 10 * c + 4

    def asq(c):
        return 10 * c + 5

    def ar(c):
        return 10 * c + 6

    def axbf(c):
        return 10 * c + 7

    def aevA(c):
        return 10 * c + 8

    def aevB(c):
        return 10 * c + 9

    def aexp(c):
        return 10 * c + 13 if c < _N - 1 else 10 * c + 10
    T_AT = 16
    V_AT = 10
    A_AT = 10
    RANK_EXTRA = 9
    SEL_V_TOTAL = 2 * BIS + RANK_EXTRA
    T_FFN = nslots + FT + 4 * nslots
    V_FFN = 3 + 4 * nslots
    A_FFN = FT

    with ExitStack() as st_:
        def sb(name, shape, dt):
            return st_.enter_context(nc.sbuf_tensor(name, shape, dt))

        def ps(name, shape, dt=F32):
            return st_.enter_context(nc.psum_tensor(name, shape, dt))

        def sem(name):
            return st_.enter_context(nc.semaphore(name))

        big0 = sb("big0", [128, max(HT * S, FT * 512)], BF16)

        def xT(j):
            return big0[:, j * S:(j + 1) * S]

        def h1T(k_):
            return big0[:, k_ * KSEL:(k_ + 1) * KSEL]

        xbf = sb("xbf", [128, nchunks, ROWB], BF16)
        w1q = [sb(f"w1q{i}", [128, HT, 512], BF16) for i in range(3)]
        w2q = [sb(f"w2q{i}", [128, FT, 256], BF16) for i in range(3)]
        ovl = sb("ovl", [128, 4096], F32)
        w2q.append(ovl[:].bitcast(BF16).rearrange("p (f h) -> p f h", f=FT))
        exT2 = [sb(f"exT{i}", [128, 9 * KSEL], BF16) for i in range(2)]
        outsb = sb("outsb", [128, nslots, D], F32)
        qkv = outsb[:].rearrange("p s d -> p (s d)").bitcast(BF16)
        QT = qkv[0:QDIM, 0 * S:1 * S]
        KT = qkv[0:QDIM, 1 * S:2 * S]
        VT = qkv[0:QDIM, 2 * S:3 * S]

        x_tile = [ovl[:, 2048:3072], ovl[:, 3072:4096]]
        sq_scr = exT2[0][:, 0:D]
        sm_sb = sb("sm_sb", [128, nchunks, E], F32)
        x1T_stage = ovl[:, 1024:2048]

        am_sb = sb("am_sb", [128, 128], BF16)
        k1_sb = sb("k1_sb", [128, QDIM], BF16)
        v1_sb = sb("v1_sb", [128, QDIM + 1], BF16)
        nd_sb = sb("nd_sb", [128, QDIM], BF16)
        ndT_sb = sb("ndT_sb", [QDIM, 128], BF16)
        kv_sb = sb("kv_sb", [QDIM, QDIM + 1], BF16)
        state_f = sb("state_f", [QDIM, QDIM + 1], F32)
        e_sb = sb("e_sb", [128, E], F32)
        scal = sb("scal", [128, 16], F32)
        lg_sb = sb("lg_sb", [128, E], F32)

        smbis = sb("smbis", [128, S16], F32)
        cmpb = sb("cmpb", [128, S16], BF16)
        bcol = sb("bcol", [128, 8], F32)
        predu = sb("predu", [128, 1], U8)
        npredu = sb("npredu", [128, 1], U8)
        lo8 = sb("lo8", [1, 8], F32)
        lo_tm = sb("lo_tm", [128, E], F32)
        cmp_tm = sb("cmp_tm", [128, nchunks, E], F32)
        cmpb_tm = sb("cmpb_tm", [128, nchunks, E], BF16)
        rsum = sb("rsum", [128, E], BF16)
        pbufA = sb("pbufA", [128, nchunks, E], F32)
        pbufB = sb("pbufB", [128, nchunks, E], F32)
        exclp = sb("exclp", [128, E], F32)
        off_i = sb("off_i", [128, nchunks, E], I32)

        exv = sb("exv", [128, 32 * nslots], BF16)
        G_sb = sb("G_sb", [128, nslots], F32)
        idf_sb = sb("idf_sb", [128, nslots], F32)
        idx_i = sb("idx_i", [128, nslots], I32)

        stag_f = ovl[:, 0:1024]
        stag_b = sb("stag_b", [QDIM, D], BF16)
        b1tmp = exT2[1][0:8, 0:1024].bitcast(F32)
        wqT = sb("wqT", [128, HT, QDIM], BF16)
        wkT = sb("wkT", [128, HT, QDIM], BF16)
        wvT = sb("wvT", [128, HT, QDIM], BF16)
        wo_b = sb("wo_b", [128, HT, QDIM], BF16)
        woT = sb("woT", [QDIM, D], BF16)
        wgT = sb("wgT", [128, HT, E], F32)
        b1f = sb("b1f", [128, FT, E], F32)
        b2bf2 = [woT[0:1, :], stag_b[0:1, :]]
        ones_f = sb("ones_f", [1, 128], F32)
        ones_b = sb("ones_b", [1, 128], BF16)
        triu = sb("triu_s", [128, 128], BF16)
        tris = sb("tris_s", [128, 128], BF16)
        eg16 = sb("eg16_s", [128, 128], BF16)
        identf = sb("identf_s", [128, 128], F32)
        identb = sb("identb_s", [128, 128], BF16)
        ids_sb = sb("ids_sb", [128, nchunks * 2], BF16)
        cst_sb = sb("cst_sb", [128, 8], F32)
        _w1q0f = w1q[0][:].rearrange("p j f -> p (j f)").bitcast(F32)
        _w1q1f = w1q[1][:].rearrange("p j f -> p (j f)").bitcast(F32)
        x2t = [_w1q0f[:, 0:1024], _w1q0f[:, 1024:2048],
               _w1q1f[:, 0:1024], _w1q1f[:, 1024:2048]]

        ps_a = ps("ps_a", [128, 512])
        ps_b = ps("ps_b", [128, 512])
        ps_c = ps("ps_c", [128, 512])
        ps_n = ps("ps_n", [128, 512])
        ps_state = ps("ps_state", [QDIM, QDIM + 1])
        ps_tr = ps("ps_tr", [128, 128], BF16)
        ps_tr2 = ps("ps_tr2", [128, 128], BF16)
        ps_g = ps("ps_g", [128, 8])

        s_sw = sem("s_sw")
        s_ldk = sem("s_ldk")
        s_ldv = sem("s_ldv")
        s_ldo = sem("s_ldo")
        s_ldg = sem("s_ldg")
        s_b1ld = sem("s_b1ld")
        s_vsw = sem("s_vsw")
        s_tsw = sem("s_tsw")
        s_cast = sem("s_cast")
        s_xt = sem("s_xt")
        s_tprj = sem("s_tprj")
        s_vprj = sem("s_vprj")
        s_aprj = sem("s_aprj")
        s_tat = sem("s_tat")
        s_vat = sem("s_vat")
        s_aat = sem("s_aat")
        s_xch = [sem("s_xch0"), sem("s_xch1")]
        s_x1st = [sem("s_x1st0"), sem("s_x1st1")]
        s_smd = sem("s_smd")
        s_bisld = sem("s_bisld")
        s_tsel = sem("s_tsel")
        s_vsel = sem("s_vsel")
        s_sel2 = sem("s_sel2")
        s_wl1 = [sem("s_wl1a"), sem("s_wl1b"), sem("s_wl1c")]
        s_wl2 = [sem(f"s_wl2{i}") for i in range(4)]
        s_scat = [sem(f"s_scat{e}") for e in range(E)]
        s_exl2 = [sem("s_exla"), sem("s_exlb")]
        s_tffn = sem("s_tffn")
        s_vffn = sem("s_vffn")
        s_affn = sem("s_affn")
        s_b2p = [sem("s_b2a"), sem("s_b2b")]
        s_gsc = sem("s_gsc")
        s_x2ld = [sem(f"s_x2ld{i}") for i in range(4)]
        s_ost = [sem(f"s_ost{i}") for i in range(4)]
        s_vself = sem("s_vself")
        s_aself = sem("s_aself")

        block = st_.enter_context(nc.Block())

        C_EPS = cst_sb[:, 0:1]

        class StopSync(Exception):
            pass

        def _dummy_out(sync):
            if stop >= 1:
                sync.wait_ge(s_vsw, N_VS)
                sync.wait_ge(s_tsw, N_TS)
            if stop >= 2:
                sync.wait_ge(s_vprj, V_PRJ)
            if stop >= 3:
                sync.wait_ge(s_vat, V_AT * nchunks)
                sync.wait_ge(s_aat, A_AT * nchunks)
                sync.wait_ge(s_x1st[0], 16 * ((nchunks + 1) // 2))
                sync.wait_ge(s_x1st[1], 16 * (nchunks // 2))
            if stop >= 4:
                sync.wait_ge(s_vsel, SEL_V_TOTAL)
            if stop >= 5:
                for e in range(E):
                    sync.wait_ge(s_scat[e], 16 * nchunks)
            for cc in range(nchunks):
                sync.dma_start(out=out_d[128 * cc:128 * (cc + 1), :],
                               in_=x_tile[0][:]).then_inc(s_ost[cc % 4], 16)

        def _sync_body(sync):
            for src, dst in [
                    (triu_d, triu[:]), (tris_d, tris[:]), (eg_d, eg16[:]),
                    (idf_d, identf[:]), (idb_d, identb[:]), (ids_d, ids_sb[:]),
                    (cst_d, cst_sb[:]),
                    (x_d[0:128, :], x_tile[0][:]), (x_d[128:256, :], x_tile[1][:]),
                    (wq_d, stag_f[0:QDIM, :])]:
                sync.dma_start(out=dst, in_=src).then_inc(s_sw, 16)
            sync.wait_ge(s_vsw, VS["cwq"])
            sync.dma_start(out=stag_f[0:QDIM, :], in_=wk_d).then_inc(s_ldk, 16)
            sync.wait_ge(s_vsw, VS["cwk"])
            sync.dma_start(out=stag_f[0:QDIM, :], in_=wv_d).then_inc(s_ldv, 16)
            sync.wait_ge(s_vsw, VS["cwv"])
            sync.dma_start(
                out=stag_f[:, 0:512].rearrange("p (j m) -> p j m", j=HT),
                in_=wo_d.rearrange("(j p) m -> p j m", p=128)).then_inc(s_ldo, 16)
            sync.wait_ge(s_vsw, VS["cwo"])
            sync.dma_start(out=stag_f[0:E, :], in_=wg_d).then_inc(s_ldg, 16)
            sync.wait_ge(s_tsw, TS["twg7"])

            for c in range(8 if stop >= 1 else 0):
                if c > 0:
                    sync.wait_ge(s_tsw, TS[f"tb1_{4 * c - 1}"])
                sync.dma_start(out=b1tmp[:],
                               in_=b1_d[:, 512 * c:512 * (c + 1)]).then_inc(s_b1ld, 16)

            if stop < 1:
                raise StopSync

            if stop < 2:
                raise StopSync
            sync.wait_ge(s_cast, 16 * HT)
            for j in range(HT):
                sync.dma_start(out=xT(j), in_=xbf_dram[:, 128 * j:128 * (j + 1)],
                               transpose=True).then_inc(s_xt, 16)

            if stop < 3:
                raise StopSync
            for cc in range(nchunks):
                if cc >= 2:
                    sync.wait_ge(s_aat, A_AT * (cc - 2) + 7)
                    sync.wait_ge(s_x1st[cc % 2], 16 * (cc // 2))
                    sync.dma_start(out=x_tile[cc % 2][:],
                                   in_=x_d[128 * cc:128 * (cc + 1), :]).then_inc(s_xch[cc % 2], 16)
                sync.wait_ge(s_aat, A_AT * cc + 7)
                sync.dma_start(out=x1acc[128 * cc:128 * (cc + 1), :],
                               in_=x_tile[cc % 2][:]).then_inc(s_x1st[cc % 2], 16)

            if stop < 4:
                raise StopSync
            sync.wait_ge(s_vat, V_AT * nchunks)
            sync.dma_start(out=smd.rearrange("(ct p) e -> p ct e", p=128),
                           in_=sm_sb[:]).then_inc(s_smd, 16)
            sync.wait_ge(s_smd, 16)
            for e in range(4):
                sync.dma_start(
                    out=smbis[16 * e:16 * (e + 1), :].unsqueeze(2),
                    in_=smd.rearrange("(pp c) e -> pp c e", pp=16)[:, :, e:e + 1],
                    allow_slow_non_contiguous=True).then_inc(s_bisld, 16)

            sync.wait_ge(s_vsel, 2 * BIS)
            sync.dma_start(out=d8a, in_=bcol[0:128:16, 0:1]).then_inc(s_sel2, 16)
            sync.wait_ge(s_sel2, 16)
            sync.dma_start(out=lo8[:], in_=d8a.rearrange("e o -> o e")).then_inc(s_sel2, 16)

            if stop < 6:
                raise StopSync
            for e in range(E):
                sync.wait_ge(s_scat[e], 16 * nchunks)
                if e > 1:
                    sync.wait_ge(s_tffn, (e - 2) * T_FFN + nslots + FT)
                for t in range(9):
                    sync.dma_start(out=exT2[e % 2][:, t * KSEL:(t + 1) * KSEL],
                                   in_=compact[e][:, 128 * t:128 * (t + 1)],
                                   transpose=True).then_inc(s_exl2[e % 2], 16)

            sync.wait_ge(s_gsc, 16 * nslots * E)
            for cc in range(min(4, nchunks)):
                sync.dma_start(out=x2t[cc % 4][:],
                               in_=x1acc[128 * cc:128 * (cc + 1), :]).then_inc(s_x2ld[cc % 4], 16)
            for cc in range(nchunks):
                sync.wait_ge(s_vat, V_AT * nchunks + 3 * cc + 3)
                sync.dma_start(out=out_d[128 * cc:128 * (cc + 1), :],
                               in_=x2t[cc % 4][:]).then_inc(s_ost[cc % 4], 16)
                if cc + 4 < nchunks:
                    sync.wait_ge(s_ost[cc % 4], 16 * (cc // 4 + 1))
                    sync.dma_start(out=x2t[cc % 4][:],
                                   in_=x1acc[128 * (cc + 4):128 * (cc + 5), :]
                                   ).then_inc(s_x2ld[cc % 4], 16)

        @block.sync
        def _(sync):
            try:
                _sync_body(sync)
            except StopSync:
                _dummy_out(sync)

        @block.gpsimd
        def _(gpsimd):
            for j in range(HT if stop >= 2 else 0):
                gpsimd.dma_start(out=xbf_dram[:, 128 * j:128 * (j + 1)],
                                 in_=x_d[:, 128 * j:128 * (j + 1)]).then_inc(s_cast, 16)

            def w1load(e, g8):
                gi = e * 8 + g8
                if gi >= 3:
                    pe, pg = divmod(gi - 3, 8)
                    gpsimd.wait_ge(s_tffn, pe * T_FFN + nslots + pg * 4 + 4)
                gpsimd.dma_start(
                    out=w1q[gi % 3][:],
                    in_=w1_d[e].rearrange("(j p) f -> p j f", p=128)[:, :, 512 * g8:512 * (g8 + 1)],
                ).then_inc(s_wl1[gi % 3], 16)

            def w2load(e, hq):
                qi = e * 4 + hq
                if qi >= 4:
                    pe, pq = divmod(qi - 4, 4)
                    done = ((pq // 2) * nslots + nslots - 1) * 2 + (pq % 2) + 1
                    gpsimd.wait_ge(s_tffn, pe * T_FFN + nslots + FT + done)
                gpsimd.dma_start(
                    out=w2q[qi % 4][:],
                    in_=w2_d[e].rearrange("(j p) h -> p j h", p=128)[:, :, 256 * hq:256 * (hq + 1)],
                ).then_inc(s_wl2[qi % 4], 16)

            bcreg = [None]

            def scatters(e, lo=0, hi=None):
                if hi is None:
                    hi = nchunks
                if e == 0 and lo == 0:
                    gpsimd.wait_ge(s_vsel, SEL_V_TOTAL)
                    gpsimd.wait_ge(s_aat, A_AT * nchunks)
                    bcreg[0] = gpsimd.to_reg(KSEL - 1)
                elif lo == 0:
                    gpsimd.wait_ge(s_scat[e - 1], 16 * nchunks)
                for ct in range(lo, hi):
                    gpsimd.indirect_dma_start(
                        compact[e], IOA(off_i[:, ct, e:e + 1], 0),
                        xbf[:, ct, :], None,
                        bounds_check=bcreg[0], oob_is_err=False,
                    ).then_inc(s_scat[e], 16)

            def b2load(e):
                if e >= 2:
                    gpsimd.wait_ge(s_tffn, (e - 1) * T_FFN)
                gpsimd.dma_start(out=b2bf2[e % 2][:],
                                 in_=b2_d[e:e + 1, :]).then_inc(s_b2p[e % 2], 16)

            def scatter_adds(e):
                gpsimd.wait_ge(s_x1st[0], 16 * ((nchunks + 1) // 2))
                gpsimd.wait_ge(s_x1st[1], 16 * (nchunks // 2))
                gpsimd.wait_ge(s_gsc, 16 * nslots * e)
                gpsimd.wait_ge(s_vffn, e * V_FFN + V_FFN)
                for sN in range(nslots):
                    gpsimd.indirect_dma_start(
                        x1acc, IOA(idx_i[:, sN:sN + 1], 0),
                        outsb[:, sN, :], None,
                        compute_op=ALU.add,
                    ).then_inc(s_gsc, 16)

            if stop == 5:
                for e in range(E):
                    scatters(e)
            if stop >= 6:
                w1load(0, 0); w1load(0, 1); w1load(0, 2)
                w2load(0, 0); w2load(0, 1); w2load(0, 2)
                scatters(0)
                gpsimd.wait_ge(s_x1st[0], 16 * ((nchunks + 1) // 2))
                gpsimd.wait_ge(s_x1st[1], 16 * (nchunks // 2))
                w2load(0, 3)
                b2load(0); b2load(1)
            for e in range(E if stop >= 6 else 0):
                if e > 0:
                    scatter_adds(e - 1)
                for g8 in range(3, 8):
                    w1load(e, g8)
                if e + 1 < E:
                    w1load(e + 1, 0); w1load(e + 1, 1); w1load(e + 1, 2)
                    scatters(e + 1)
                    for hq in range(4):
                        w2load(e + 1, hq)
                if e + 2 < E:
                    b2load(e + 2)
            if stop >= 6:
                scatter_adds(E - 1)
            for _i in range(4):
                gpsimd.wait_ge(s_ost[_i], 16 * ((nchunks + 3 - _i) // 4))

        @block.tensor
        def _(tensor):
            tensor.nop()
            if stop < 1:
                return
            for wname in ["wq", "wk", "wv"]:
                cast_key = {"wq": "cwq", "wk": "cwk", "wv": "cwv"}[wname]
                for j in range(8):
                    if j == 0:
                        tensor.wait_ge(s_vsw, VS[cast_key])
                    if j >= 2:
                        tensor.wait_ge(s_vsw, VS[f"e{wname}{j - 2}"])
                    dst = ps_tr[:, 0:QDIM] if j % 2 == 0 else ps_tr2[:, 0:QDIM]
                    tensor.transpose(dst, stag_b[:, 128 * j:128 * (j + 1)],
                                     identb[0:QDIM, 0:QDIM]).then_inc(s_tsw, 1)
            for j in range(8):
                tensor.wait_ge(s_vsw, VS["cwo"] if j < 2 else VS[f"ewo{j - 2}"])
                dst = ps_tr[0:QDIM, 0:128] if j % 2 == 0 else ps_tr2[0:QDIM, 0:128]
                tensor.transpose(dst, wo_b[:, j, :], identb[0:128, 0:128]).then_inc(s_tsw, 1)
            tensor.wait_ge(s_ldg, 16)
            for j in range(8):
                if j >= 1:
                    tensor.wait_ge(s_vsw, VS[f"ewg{j - 1}"])
                tensor.transpose(ps_g[:], stag_f[0:E, 128 * j:128 * (j + 1)],
                                 identf[0:E, 0:E]).then_inc(s_tsw, 1)
            for j in range(FT):
                c, cc2 = divmod(j, 4)
                tensor.wait_ge(s_b1ld, 16 * (c + 1))
                if j == 0:
                    tensor.wait_ge(s_vsw, VS["ewg7"])
                if j > 0:
                    tensor.wait_ge(s_vsw, VS[f"eb1_{j - 1}"])
                tensor.transpose(ps_g[:], b1tmp[:, 128 * cc2:128 * (cc2 + 1)],
                                 identf[0:8, 0:8]).then_inc(s_tsw, 1)

            if stop < 2:
                return
            tensor.wait_ge(s_xt, 16 * HT)
            for g in range(ngroups):
                sl = slice(gw * g, gw * (g + 1))
                if g > 0:
                    tensor.wait_ge(s_vprj, 5 * (g - 1) + 2)
                for j in range(HT):
                    mm = tensor.matmul(ps_a[0:QDIM, 0:gw], wqT[:, j, :], xT(j)[:, sl],
                                       start=(j == 0), stop=(j == HT - 1))
                mm.then_inc(s_tprj, 1)
                if g > 0:
                    tensor.wait_ge(s_vprj, 5 * (g - 1) + 4)
                for j in range(HT):
                    mm = tensor.matmul(ps_b[0:QDIM, 0:gw], wkT[:, j, :], xT(j)[:, sl],
                                       start=(j == 0), stop=(j == HT - 1))
                mm.then_inc(s_tprj, 1)
                if g > 0:
                    tensor.wait_ge(s_vprj, 5 * (g - 1) + 5)
                for j in range(HT):
                    mm = tensor.matmul(ps_c[0:QDIM, 0:gw], wvT[:, j, :], xT(j)[:, sl],
                                       start=(j == 0), stop=(j == HT - 1))
                mm.then_inc(s_tprj, 1)

            if stop < 3:
                return
            def t_qkv_head(c):
                csl_ = slice(128 * c, 128 * (c + 1))
                if c > 0:
                    tensor.wait_ge(s_vat, vam(c - 1))
                tensor.matmul(ps_c[:, 0:128], KT[:, csl_], QT[:, csl_],
                              start=True, stop=True).then_inc(s_tat, 1)
                if c > 0:
                    tensor.wait_ge(s_aat, andt(c - 1))
                tensor.transpose(ps_tr[:, 0:QDIM], KT[:, csl_],
                                 identb[0:QDIM, 0:QDIM]).then_inc(s_tat, 1)
                if c > 0:
                    tensor.wait_ge(s_aat, av1(c - 1))
                tensor.transpose(ps_tr2[:, 0:QDIM], VT[:, csl_],
                                 identb[0:QDIM, 0:QDIM]).then_inc(s_tat, 1)

            def t_attn_body(c):
                csl_ = slice(128 * c, 128 * (c + 1))
                tensor.wait_ge(s_vat, vam(c))
                tensor.wait_ge(s_aat, av1(c))
                mm = tensor.matmul(ps_n[:, 0:QDIM + 1], am_sb[:], v1_sb[:],
                                   start=True, stop=(c == 0))
                if c > 0:
                    mm = tensor.matmul(ps_n[:, 0:QDIM + 1], QT[:, csl_], kv_sb[:],
                                       start=False, stop=True)
                mm.then_inc(s_tat, 1)
                tensor.wait_ge(s_vat, vst(c))
                tensor.matmul(ps_state[:], k1_sb[:], v1_sb[:],
                              start=True, stop=True).then_inc(s_tat, 1)
                tensor.wait_ge(s_vat, vnd(c))
                tensor.wait_ge(s_aat, ak1(c))
                tensor.transpose(ps_tr[0:QDIM, 0:128], nd_sb[:],
                                 identb[0:128, 0:128]).then_inc(s_tat, 1)
                tensor.wait_ge(s_aat, andt(c))
                tensor.matmul(ps_a[:, 0:512], ndT_sb[:], woT[:, 0:512],
                              start=True, stop=True)
                tensor.matmul(ps_b[:, 0:512], ndT_sb[:], woT[:, 512:1024],
                              start=True, stop=True).then_inc(s_tat, 1)

            tensor.wait_ge(s_vprj, V_PRJ)
            t_qkv_head(0)
            t_attn_body(0)
            for cc in range(nchunks):
                par_t = x_tile[cc % 2]
                for j in range(HT):
                    if j == 0:
                        tensor.wait_ge(s_vat, vpu(cc))
                    pst_x = [ps_a, ps_b][j // 4]
                    tensor.transpose(pst_x[:, 128 * (j % 4):128 * (j % 4 + 1)],
                                     par_t[:, 128 * j:128 * (j + 1)],
                                     identf[:]).then_inc(s_tat, 1)
                if cc + 1 < nchunks:
                    t_qkv_head(cc + 1)
                tensor.wait_ge(s_aat, aevB(cc))
                for j in range(HT):
                    mm = tensor.matmul(ps_g[:, 0:E],
                                       x1T_stage[:, 128 * j:128 * (j + 1)],
                                       wgT[:, j, :], start=(j == 0), stop=(j == HT - 1))
                mm.then_inc(s_tat, 1)
                if cc + 1 < nchunks:
                    t_attn_body(cc + 1)

            if stop < 4:
                return
            for i in range(BIS):
                tensor.wait_ge(s_vsel, 2 * i + 1)
                if i > 0:
                    tensor.wait_ge(s_vsel, 2 * (i - 1) + 2)
                tensor.matmul(ps_n[:, 0:S16], eg16[:], cmpb[:],
                              start=True, stop=True).then_inc(s_tsel, 1)
            tensor.wait_ge(s_sel2, 32)
            tensor.matmul(ps_g[:, 0:E], ones_f[:], lo8[:],
                          start=True, stop=True).then_inc(s_tsel, 1)
            tensor.wait_ge(s_vsel, 2 * BIS + 4)
            tensor.matmul(ps_n[:, 0:E], tris[:], rsum[:],
                          start=True, stop=True).then_inc(s_tsel, 1)

            if stop < 6:
                return
            for e in range(E):
                TB = e * T_FFN
                AB = e * A_FFN
                VB = e * V_FFN
                tensor.wait_ge(s_exl2[e % 2], 16 * 9 * (e // 2 + 1))
                if e == 0:
                    tensor.wait_ge(s_vat, V_AT * nchunks)
                    tensor.wait_ge(s_vsel, SEL_V_TOTAL)
                if e > 0:
                    tensor.wait_ge(s_vffn, (e - 1) * V_FFN + 1)
                for sN in range(nslots):
                    tensor.transpose(ps_tr[:, 32 * sN:32 * (sN + 1)],
                                     exT2[e % 2][0:32, 8 * KSEL + 128 * sN:8 * KSEL + 128 * (sN + 1)],
                                     identb[0:32, 0:32]).then_inc(s_tffn, 1)
                if e > 0:
                    tensor.wait_ge(s_vffn, e * V_FFN)
                for g8 in range(8):
                    gi = e * 8 + g8
                    tensor.wait_ge(s_wl1[gi % 3], 16 * (gi // 3 + 1))
                    for jj in range(4):
                        kk = g8 * 4 + jj
                        psum = [ps_a, ps_b][kk % 2]
                        if AB + kk - 1 > 0:
                            tensor.wait_ge(s_affn, AB + kk - 1)
                        for kt in range(HT):
                            mm = tensor.matmul(psum[:, 0:KSEL],
                                               w1q[gi % 3][:, kt, 128 * jj:128 * (jj + 1)],
                                               exT2[e % 2][:, kt * KSEL:(kt + 1) * KSEL],
                                               start=(kt == 0), stop=(kt == HT - 1))
                        mm.then_inc(s_tffn, 1)
                tensor.wait_ge(s_affn, AB + FT)
                tensor.wait_ge(s_vffn, VB + 3)
                tensor.wait_ge(s_b2p[e % 2], 16 * (e // 2 + 1))
                PS4 = [ps_c, ps_n, ps_a, ps_b]
                for p2 in range(2):
                    q0, q1 = 2 * p2, 2 * p2 + 1
                    gi0, gi1 = e * 4 + q0, e * 4 + q1
                    tensor.wait_ge(s_wl2[gi0 % 4], 16 * (gi0 // 4 + 1))
                    tensor.wait_ge(s_wl2[gi1 % 4], 16 * (gi1 // 4 + 1))
                    for sN in range(nslots):
                        it = p2 * nslots + sN
                        psA = PS4[(2 * it) % 4]
                        psB = PS4[(2 * it + 1) % 4]
                        for half in range(2):
                            k = it * 2 + half
                            if k >= 4:
                                tensor.wait_ge(s_vffn, VB + 3 + (k - 4) + 1)
                            elif e > 0:
                                tensor.wait_ge(s_vffn,
                                               (e - 1) * V_FFN + 3 + (4 * nslots - 4 + k % 4) + 1)
                        for f in range(FT):
                            tensor.matmul(psA[:, 0:256],
                                          h1T(f)[:, 128 * sN:128 * (sN + 1)],
                                          w2q[gi0 % 4][:, f, :],
                                          start=(f == 0), stop=False)
                            tensor.matmul(psB[:, 0:256],
                                          h1T(f)[:, 128 * sN:128 * (sN + 1)],
                                          w2q[gi1 % 4][:, f, :],
                                          start=(f == 0), stop=False)
                        tensor.matmul(psA[:, 0:256], ones_b[0:1, :],
                                      b2bf2[e % 2][0:1, 256 * q0:256 * (q0 + 1)],
                                      start=False, stop=True).then_inc(s_tffn, 1)
                        tensor.matmul(psB[:, 0:256], ones_b[0:1, :],
                                      b2bf2[e % 2][0:1, 256 * q1:256 * (q1 + 1)],
                                      start=False, stop=True).then_inc(s_tffn, 1)

        @block.vector
        def _(vector):
            vcnt = [0]

            def vg(ins, milestone=None, n=1):
                ins.then_inc(s_vself, 1)
                vcnt[0] += 1
                vector.wait_ge(s_vself, vcnt[0])
                if milestone is not None:
                    vector.sem_inc(milestone, n)

            vg(vector.memset(ones_f[:], 1.0), s_vsw)
            vg(vector.memset(v1_sb[:, QDIM:QDIM + 1], 1.0), s_vsw)
            vg(vector.memset(xbf[:, :, 1042:ROWB], 0), s_vsw)
            vector.wait_ge(s_sw, 160)
            vg(vector.tensor_copy(xbf[:, :, 1040:1042],
                                  ids_sb[:].rearrange("p (c t) -> p c t", t=2)), s_vsw)
            vg(vector.memset(state_f[:], 0.0))
            vg(vector.memset(bcol[:, 0:1], 0.0))
            vg(vector.memset(bcol[:, 1:2], 1.0))
            vg(vector.memset(ones_b[:], 1.0), s_vsw)

            def _evacs(wname, wbuf):
                for j in range(8):
                    vector.wait_ge(s_tsw, TS[f"t{wname}{j}"])
                    if wname == "wo":
                        s = ps_tr[0:QDIM, 0:128] if j % 2 == 0 else ps_tr2[0:QDIM, 0:128]
                        vg(vector.tensor_copy(woT[:, 128 * j:128 * (j + 1)], s), s_vsw)
                    elif wname == "wg":
                        s = ps_tr[:, 0:E] if j % 2 == 0 else ps_tr2[:, 0:E]
                        vg(vector.tensor_copy(wgT[:, j, :], s), s_vsw)
                    else:
                        s = ps_tr[:, 0:QDIM] if j % 2 == 0 else ps_tr2[:, 0:QDIM]
                        vg(vector.tensor_copy(wbuf[:, j, :], s), s_vsw)

            vg(vector.tensor_copy(stag_b[:], stag_f[0:QDIM, :]), s_vsw)
            if stop >= 1:
                _evacs("wq", wqT)
            vector.wait_ge(s_ldk, 16)
            vector.wait_ge(s_tsw, TS["twq7"])
            vg(vector.tensor_copy(stag_b[:], stag_f[0:QDIM, :]), s_vsw)
            if stop >= 1:
                _evacs("wk", wkT)
            vector.wait_ge(s_ldv, 16)
            vector.wait_ge(s_tsw, TS["twk7"])
            vg(vector.tensor_copy(stag_b[:], stag_f[0:QDIM, :]), s_vsw)
            if stop >= 1:
                _evacs("wv", wvT)
            vector.wait_ge(s_ldo, 16)
            vg(vector.tensor_copy(wo_b[:].rearrange("p j m -> p (j m)"),
                                  stag_f[:, 0:512]), s_vsw)
            if stop >= 1:
                _evacs("wo", None)
            if stop >= 1:
                for j in range(8):
                    vector.wait_ge(s_tsw, TS[f"twg{j}"])
                    vg(vector.tensor_copy(wgT[:, j, :], ps_g[:, 0:E]), s_vsw)
            if stop < 1:
                return
            for j in range(FT):
                vector.wait_ge(s_tsw, TS[f"tb1_{j}"])
                vg(vector.tensor_copy(b1f[:, j, :], ps_g[:]), s_vsw)
            vector.sem_inc(s_vsw, 1)

            if stop < 2:
                return
            for g in range(ngroups):
                sl = slice(gw * g, gw * (g + 1))
                vector.wait_ge(s_tprj, 3 * g + 1)
                vg(vector.tensor_scalar_min(stag_f[0:QDIM, 0:gw],
                                            ps_a[0:QDIM, 0:gw], 0.0), s_vprj)
                vector.wait_ge(s_aprj, 2 * g + 1)
                vg(vector.tensor_scalar_max(stag_f[0:QDIM, 0:gw],
                                            ps_a[0:QDIM, 0:gw], 0.0))
                vg(vector.tensor_add(QT[:, sl], stag_b[:, 0:gw],
                                     stag_f[0:QDIM, 0:gw]), s_vprj)
                vector.wait_ge(s_tprj, 3 * g + 2)
                vg(vector.tensor_scalar_min(stag_f[0:QDIM, 0:gw],
                                            ps_b[0:QDIM, 0:gw], 0.0), s_vprj)
                vector.wait_ge(s_aprj, 2 * g + 2)
                vg(vector.tensor_scalar_max(stag_f[0:QDIM, 0:gw],
                                            ps_b[0:QDIM, 0:gw], 0.0))
                vg(vector.tensor_add(KT[:, sl], stag_b[:, 0:gw],
                                     stag_f[0:QDIM, 0:gw]), s_vprj)
                vector.wait_ge(s_tprj, 3 * g + 3)
                vg(vector.tensor_copy(VT[:, sl], ps_c[0:QDIM, 0:gw]), s_vprj)

            inv_d = 1.0 / D
            if stop < 3:
                return
            vector.wait_ge(s_tat, tA(0))
            vg(vector.tensor_tensor(am_sb[:], ps_c[:, 0:128], triu[:],
                                    ALU.mult), s_vat)
            vector.sem_inc(s_vat, 1)
            for cc in range(nchunks):
                vector.wait_ge(s_tat, tnum(cc))
                vg(vector.tensor_scalar_add(scal[:, 9:10], ps_n[:, QDIM:QDIM + 1], 1e-6))
                vg(vector.reciprocal(scal[:, 10:11], scal[:, 9:10]))
                vg(vector.tensor_scalar(nd_sb[:], ps_n[:, 0:QDIM], scal[:, 10:11],
                                        None, ALU.mult), s_vat)
                vector.wait_ge(s_tat, topj(cc))
                if cc >= 2:
                    vector.wait_ge(s_xch[cc % 2], 16 * (cc // 2))
                par = x_tile[cc % 2]
                vg(vector.scalar_tensor_tensor(par[:, 0:512], par[:, 0:512], 1.0,
                                               ps_a[:, 0:512], ALU.mult, ALU.add,
                                               accum_out=scal[:, 0:1]), s_vat)
                vg(vector.scalar_tensor_tensor(par[:, 512:1024], par[:, 512:1024], 1.0,
                                               ps_b[:, 0:512], ALU.mult, ALU.add,
                                               accum_out=scal[:, 1:2]), s_vat)
                vg(vector.tensor_add(scal[:, 2:3], scal[:, 0:1], scal[:, 1:2]))
                vg(vector.tensor_scalar_mul(scal[:, 3:4], scal[:, 2:3], inv_d))
                vg(vector.tensor_mul(scal[:, 4:5], scal[:, 3:4], scal[:, 3:4]))
                vector.wait_ge(s_aat, asq(cc))
                vg(vector.scalar_tensor_tensor(scal[:, 5:6], scal[:, 14:15], inv_d,
                                               scal[:, 4:5], ALU.mult, ALU.subtract),
                   s_vat)
                vector.wait_ge(s_aat, ar(cc))
                vg(vector.scalar_tensor_tensor(scal[:, 8:9], scal[:, 3:4], -1.0,
                                               scal[:, 7:8], ALU.mult, ALU.mult),
                   s_vat)
                vg(vector.scalar_tensor_tensor(
                    par[:], par[:], scal[:, 7:8],
                    scal[:, 8:9].broadcast_to((128, D)),
                    ALU.mult, ALU.add), s_vat)
                if cc + 1 < nchunks:
                    vector.wait_ge(s_tat, tA(cc + 1))
                    vg(vector.tensor_tensor(am_sb[:], ps_c[:, 0:128], triu[:],
                                            ALU.mult), s_vat)
                    vector.wait_ge(s_tat, tdel(cc))
                    vector.wait_ge(s_aat, akv(cc))
                    vg(vector.tensor_tensor(state_f[:], state_f[:], ps_state[:],
                                            ALU.add), s_vat)
                vector.wait_ge(s_tat, tgate(cc))
                vg(vector.tensor_copy(lg_sb[:], ps_g[:, 0:E]))
                vg(vector.reduce_max(scal[:, 15:16], lg_sb[:], AX))
                vg(vector.tensor_scalar_mul(scal[:, 13:14], scal[:, 15:16], -1.0),
                   s_vat)
                vector.wait_ge(s_aat, aexp(cc))
                vg(vector.reciprocal(scal[:, 12:13], scal[:, 11:12]))
                vg(vector.tensor_scalar(sm_sb[:, cc, :], e_sb[:], scal[:, 12:13], None,
                                        ALU.mult))
                vg(vector.tensor_copy(xbf[:, cc, 1024:1032], sm_sb[:, cc, :]))
                vg(vector.tensor_tensor(xbf[:, cc, 1032:1040], sm_sb[:, cc, :],
                                        xbf[:, cc, 1024:1032],
                                        ALU.subtract), s_vat)

            if stop < 4:
                return
            vector.wait_ge(s_bisld, 16 * 8)
            for i in range(BIS):
                vg(vector.tensor_add(bcol[:, 3:4], bcol[:, 0:1], bcol[:, 1:2]))
                vg(vector.tensor_scalar_mul(bcol[:, 2:3], bcol[:, 3:4], 0.5))
                vg(vector.tensor_scalar(cmpb[:], smbis[:], bcol[:, 2:3], None,
                                        ALU.is_gt), s_vsel)
                vector.wait_ge(s_tsel, i + 1)
                vg(vector.reduce_sum(bcol[:, 4:5], ps_n[:, 0:S16], AX))
                vg(vector.tensor_scalar(predu[:], bcol[:, 4:5], float(KSEL), None,
                                        ALU.is_ge))
                vg(vector.tensor_scalar(npredu[:], bcol[:, 4:5], float(KSEL), None,
                                        ALU.is_lt))
                vg(vector.select(bcol[:, 0:1], predu[:], bcol[:, 2:3], bcol[:, 0:1],
                                 add_drain=True))
                vg(vector.select(bcol[:, 1:2], npredu[:], bcol[:, 2:3], bcol[:, 1:2],
                                 add_drain=True), s_vsel)
            vector.wait_ge(s_tsel, BIS + 1)
            vg(vector.tensor_copy(lo_tm[:], ps_g[:, 0:E]))
            vg(vector.tensor_tensor(cmp_tm[:], sm_sb[:],
                                    lo_tm[:].unsqueeze(1).broadcast_to((128, nchunks, E)),
                                    ALU.is_gt), s_vsel)
            vg(vector.tensor_copy(cmpb_tm[:], cmp_tm[:]), s_vsel, 2)
            with nc.allow_low_precision(reason="0/1 counts <=16 exact in bf16"):
                vg(vector.reduce_sum(rsum[:].unsqueeze(2),
                                     cmpb_tm[:].rearrange("p c e -> p e c"), AX),
                   s_vsel)
            cur, nxt = cmp_tm, pbufA
            for k2 in prefix_steps:
                vg(vector.tensor_tensor(nxt[:, k2:, :], cur[:, k2:, :],
                                        cur[:, 0:nchunks - k2, :], ALU.add))
                vg(vector.tensor_copy(nxt[:, 0:k2, :], cur[:, 0:k2, :]))
                cur, nxt = nxt, (pbufB if nxt is pbufA else pbufA)
            vg(vector.tensor_sub(nxt[:], cur[:], cmp_tm[:]))
            excl_ct = nxt
            vector.wait_ge(s_tsel, BIS + 2)
            vg(vector.tensor_copy(exclp[:], ps_n[:, 0:E]))
            rank = cur
            vg(vector.tensor_tensor(rank[:], excl_ct[:],
                                    exclp[:].unsqueeze(1).broadcast_to((128, nchunks, E)),
                                    ALU.add))
            vg(vector.scalar_tensor_tensor(rank[:], cmp_tm[:], -4096.0, rank[:],
                                           ALU.mult, ALU.add))
            vg(vector.tensor_scalar_add(rank[:], rank[:], 4096.0))
            vg(vector.tensor_copy(off_i[:], rank[:]), s_vsel, RANK_EXTRA - 4)

            if stop < 6:
                return
            for e in range(E):
                TB = e * T_FFN
                VB = e * V_FFN
                vector.wait_ge(s_tffn, TB + nslots)
                if e > 0:
                    vector.wait_ge(s_gsc, 16 * nslots * e)
                vg(vector.tensor_copy(exv[:, 0:32 * nslots],
                                      ps_tr[:, 0:32 * nslots]), s_vffn)
                for sN in range(nslots):
                    o = 32 * sN
                    vg(vector.tensor_tensor(G_sb[:, sN:sN + 1], exv[:, o + e:o + e + 1],
                                            exv[:, o + 8 + e:o + 8 + e + 1], ALU.add))
                    vg(vector.scalar_tensor_tensor(idf_sb[:, sN:sN + 1],
                                                   exv[:, o + 17:o + 18], 256.0,
                                                   exv[:, o + 16:o + 17],
                                                   ALU.mult, ALU.add))
                vg(vector.tensor_copy(idx_i[:], idf_sb[:]), s_vffn)
                vector.sem_inc(s_vffn, 1)
                PS4 = [ps_c, ps_n, ps_a, ps_b]
                for p2 in range(2):
                    for sN in range(nslots):
                        it = p2 * nslots + sN
                        for half in range(2):
                            hq = 2 * p2 + half
                            psum = PS4[(2 * it + half) % 4]
                            vector.wait_ge(s_tffn, TB + nslots + FT + it * 2 + half + 1)
                            vg(vector.tensor_scalar(outsb[:, sN, 256 * hq:256 * (hq + 1)],
                                                    psum[:, 0:256], G_sb[:, sN:sN + 1],
                                                    None, ALU.mult), s_vffn)

            def _l2sum(c):
                L = LC[c % 2]
                vector.wait_ge(s_x2ld[c % 4], 16 * (c // 4 + 1))
                par_ = x2t[c % 4]
                vg(vector.reduce_sum(scal[:, L[0]:L[0] + 1], par_[:], AX))
                vg(vector.tensor_scalar_mul(scal[:, L[1]:L[1] + 1],
                                            scal[:, L[0]:L[0] + 1], inv_d))
                vg(vector.tensor_mul(scal[:, L[2]:L[2] + 1],
                                     scal[:, L[1]:L[1] + 1],
                                     scal[:, L[1]:L[1] + 1]))

            _l2sum(0)
            for cc in range(nchunks):
                L = LC[cc % 2]
                par = x2t[cc % 4]
                vector.wait_ge(s_aat, A_AT * nchunks + 2 * cc + 1)
                vg(vector.scalar_tensor_tensor(scal[:, L[3]:L[3] + 1],
                                               scal[:, L[7]:L[7] + 1], inv_d,
                                               scal[:, L[2]:L[2] + 1],
                                               ALU.mult, ALU.subtract),
                   s_vat)
                if cc + 1 < nchunks:
                    _l2sum(cc + 1)
                vector.wait_ge(s_aat, A_AT * nchunks + 2 * cc + 2)
                vg(vector.scalar_tensor_tensor(scal[:, L[6]:L[6] + 1],
                                               scal[:, L[1]:L[1] + 1], -1.0,
                                               scal[:, L[5]:L[5] + 1],
                                               ALU.mult, ALU.mult),
                   s_vat)
                vg(vector.scalar_tensor_tensor(
                    par[:], par[:], scal[:, L[5]:L[5] + 1],
                    scal[:, L[6]:L[6] + 1].broadcast_to((128, D)),
                    ALU.mult, ALU.add), s_vat)

        @block.scalar
        def _(scalar):
            acnt = [0]

            def ag(ins):
                ins.then_inc(s_aself, 1)
                acnt[0] += 1
                scalar.wait_ge(s_aself, acnt[0])

            scalar.nop()
            if stop < 2:
                return
            for g in range(ngroups):
                scalar.wait_ge(s_vprj, 5 * g + 1)
                scalar.activation(stag_b[:, 0:gw], stag_f[0:QDIM, 0:gw],
                                  AF.Exp).then_inc(s_aprj, 1)
                scalar.wait_ge(s_vprj, 5 * g + 3)
                scalar.activation(stag_b[:, 0:gw], stag_f[0:QDIM, 0:gw],
                                  AF.Exp).then_inc(s_aprj, 1)
            if stop < 3:
                return
            scalar.wait_ge(s_vat, vst(0))
            scalar.activation(kv_sb[:], state_f[:],
                              AF.Copy).then_inc(s_aat, 1)
            scalar.wait_ge(s_tat, tK(0))
            scalar.activation(k1_sb[:], ps_tr[:, 0:QDIM],
                              AF.Copy).then_inc(s_aat, 1)
            scalar.wait_ge(s_tat, tV(0))
            scalar.activation(v1_sb[:, 0:QDIM], ps_tr2[:, 0:QDIM],
                              AF.Copy).then_inc(s_aat, 1)
            scalar.wait_ge(s_tat, tndt(0))
            scalar.activation(ndT_sb[:], ps_tr[0:QDIM, 0:128],
                              AF.Copy).then_inc(s_aat, 1)
            for cc in range(nchunks):
                par = x_tile[cc % 2]
                scalar.wait_ge(s_vat, vaddb(cc))
                scalar.activation(sq_scr[:], par[:], AF.Square,
                                  accum_out=scal[:, 14:15]).then_inc(s_aat, 1)
                scalar.wait_ge(s_vat, vve(cc))
                ag(scalar.activation(scal[:, 6:7], scal[:, 5:6], AF.Ln, bias=C_EPS))
                scalar.activation(scal[:, 7:8], scal[:, 6:7], AF.Exp,
                                  scale=-0.5).then_inc(s_aat, 1)
                scalar.wait_ge(s_vat, vpu(cc))
                scalar.activation(xbf[:, cc, 0:1024], par[:],
                                  AF.Copy).then_inc(s_aat, 1)
                scalar.wait_ge(s_tat, ttr(cc, 3))
                scalar.activation(x1T_stage[:, 0:512], ps_a[:, 0:512],
                                  AF.Copy).then_inc(s_aat, 1)
                scalar.wait_ge(s_tat, ttr(cc, 7))
                scalar.activation(x1T_stage[:, 512:1024], ps_b[:, 0:512],
                                  AF.Copy).then_inc(s_aat, 1)
                if cc + 1 < nchunks:
                    scalar.wait_ge(s_vat, vst(cc + 1))
                    scalar.activation(kv_sb[:], state_f[:],
                                      AF.Copy).then_inc(s_aat, 1)
                    scalar.wait_ge(s_tat, tK(cc + 1))
                    scalar.activation(k1_sb[:], ps_tr[:, 0:QDIM],
                                      AF.Copy).then_inc(s_aat, 1)
                    scalar.wait_ge(s_tat, tV(cc + 1))
                    scalar.activation(v1_sb[:, 0:QDIM], ps_tr2[:, 0:QDIM],
                                      AF.Copy).then_inc(s_aat, 1)
                scalar.wait_ge(s_vat, vlg(cc))
                scalar.activation(e_sb[:], lg_sb[:], AF.Exp, bias=scal[:, 13:14],
                                  accum_out=scal[:, 11:12]).then_inc(s_aat, 1)
                if cc + 1 < nchunks:
                    scalar.wait_ge(s_tat, tndt(cc + 1))
                    scalar.activation(ndT_sb[:], ps_tr[0:QDIM, 0:128],
                                      AF.Copy).then_inc(s_aat, 1)
            if stop < 4:
                return
            scalar.wait_ge(s_smd, 16)
            for e in range(4, 8):
                scalar.dma_start(
                    out=smbis[16 * e:16 * (e + 1), :].unsqueeze(2),
                    in_=smd.rearrange("(pp c) e -> pp c e", pp=16)[:, :, e:e + 1],
                    allow_slow_non_contiguous=True).then_inc(s_bisld, 16)
            if stop < 6:
                return
            for e in range(E):
                TB = e * T_FFN
                AB = e * A_FFN
                for kk in range(FT):
                    psum = [ps_a, ps_b][kk % 2]
                    scalar.wait_ge(s_tffn, TB + nslots + kk + 1)
                    scalar.activation(h1T(kk), psum[:, 0:KSEL], AF.Gelu_apprx_tanh,
                                      bias=b1f[:, kk, e:e + 1]).then_inc(s_affn, 1)
            for cc in range(nchunks):
                L = LC[cc % 2]
                par = x2t[cc % 4]
                scalar.wait_ge(s_x2ld[cc % 4], 16 * (cc // 4 + 1))
                scalar.activation(sq_scr[:], par[:], AF.Square,
                                  accum_out=scal[:, L[7]:L[7] + 1]).then_inc(s_aat, 1)
                scalar.wait_ge(s_vat, V_AT * nchunks + 3 * cc + 1)
                ag(scalar.activation(scal[:, L[4]:L[4] + 1], scal[:, L[3]:L[3] + 1],
                                     AF.Ln, bias=C_EPS))
                scalar.activation(scal[:, L[5]:L[5] + 1], scal[:, L[4]:L[4] + 1],
                                  AF.Exp, scale=-0.5).then_inc(s_aat, 1)

    return nc


def _consts(S):
    nchunks = S // 128
    i = np.arange(128)
    triu = (i[:, None] <= i[None, :]).astype(ml_dtypes.bfloat16)
    tris = (i[:, None] < i[None, :]).astype(ml_dtypes.bfloat16)
    eg = (i[:, None] // 16 == i[None, :] // 16).astype(ml_dtypes.bfloat16)
    identf = np.eye(128, dtype=np.float32)
    identb = np.eye(128).astype(ml_dtypes.bfloat16)
    t = i[:, None] + 128 * np.arange(nchunks)[None, :]
    ids = np.stack([t % 256, t // 256], axis=-1).reshape(128, nchunks * 2)
    ids = ids.astype(ml_dtypes.bfloat16)
    csts = np.tile(np.array([EPS, 0, 0, 0, 0, 0, 0, 0], np.float32), (128, 1))
    return dict(triu=triu, tris=tris, eg16=eg, identf=identf, identb=identb,
                idsc=ids, csts=csts)


def kernel(L0_attn_x, L0_attn_Wq, L0_attn_Wk, L0_attn_Wv, L0_attn_Wo,
           L0_ffn_Wg, L0_ffn_W1, L0_ffn_b1, L0_ffn_W2, L0_ffn_b2):
    B, S, _ = L0_attn_x.shape
    nc = build_program(S=S, KSEL=512, BIS=34)
    consts = _consts(S)
    base = dict(wq=np.ascontiguousarray(L0_attn_Wq),
                wk=np.ascontiguousarray(L0_attn_Wk),
                wv=np.ascontiguousarray(L0_attn_Wv),
                wo=np.ascontiguousarray(L0_attn_Wo),
                wg=np.ascontiguousarray(L0_ffn_Wg),
                w1=np.ascontiguousarray(L0_ffn_W1),
                b1=np.ascontiguousarray(L0_ffn_b1),
                w2=np.ascontiguousarray(L0_ffn_W2),
                b2=np.ascontiguousarray(L0_ffn_b2),
                **consts)
    in_maps = [dict(x=np.ascontiguousarray(L0_attn_x[b]), **base) for b in range(B)]
    res = run_bass_kernel_spmd(nc, in_maps, core_ids=list(range(B)))
    return np.stack([np.asarray(r["out"]) for r in res.results], axis=0)


# version 15 (final)
# speedup vs baseline: 1.3035x; 1.0182x over previous
import numpy as np
import ml_dtypes
import concourse.bass as bass
import concourse.mybir as mybir
from concourse.bass_utils import run_bass_kernel_spmd
from contextlib import ExitStack

F32 = mybir.dt.float32
BF16 = mybir.dt.bfloat16
I32 = mybir.dt.int32
U8 = mybir.dt.uint8
AF = mybir.ActivationFunctionType
ALU = mybir.AluOpType
IOA = bass.IndirectOffsetOnAxis
AX = mybir.AxisListType.X

D = 1024
HT = 8
E = 8
F = 4096
FT = 32
QDIM = 64
ROWB = 1152
EPS = 1e-5


def build_program(S=2048, KSEL=512, BIS=30, stop=6):
    nchunks = S // 128
    gw = min(512, S)
    ngroups = S // gw
    nslots = KSEL // 128
    S16 = S // 16
    prefix_steps = []
    k = 1
    while k < nchunks:
        prefix_steps.append(k)
        k *= 2

    nc = bass.Bass("TRN2", target_bir_lowering=False)

    x_d = nc.dram_tensor("x", [S, D], F32, kind="ExternalInput").ap()
    wq_d = nc.dram_tensor("wq", [QDIM, D], F32, kind="ExternalInput").ap()
    wk_d = nc.dram_tensor("wk", [QDIM, D], F32, kind="ExternalInput").ap()
    wv_d = nc.dram_tensor("wv", [QDIM, D], F32, kind="ExternalInput").ap()
    wo_d = nc.dram_tensor("wo", [D, QDIM], F32, kind="ExternalInput").ap()
    wg_d = nc.dram_tensor("wg", [E, D], F32, kind="ExternalInput").ap()
    w1_d = nc.dram_tensor("w1", [E, D, F], F32, kind="ExternalInput").ap()
    b1_d = nc.dram_tensor("b1", [E, F], F32, kind="ExternalInput").ap()
    w2_d = nc.dram_tensor("w2", [E, F, D], F32, kind="ExternalInput").ap()
    b2_d = nc.dram_tensor("b2", [E, D], F32, kind="ExternalInput").ap()
    triu_d = nc.dram_tensor("triu", [128, 128], BF16, kind="ExternalInput").ap()
    tris_d = nc.dram_tensor("tris", [128, 128], BF16, kind="ExternalInput").ap()
    eg_d = nc.dram_tensor("eg16", [128, 128], BF16, kind="ExternalInput").ap()
    idf_d = nc.dram_tensor("identf", [128, 128], F32, kind="ExternalInput").ap()
    idb_d = nc.dram_tensor("identb", [128, 128], BF16, kind="ExternalInput").ap()
    ids_d = nc.dram_tensor("idsc", [128, nchunks * 2], BF16, kind="ExternalInput").ap()
    cst_d = nc.dram_tensor("csts", [128, 8], F32, kind="ExternalInput").ap()
    out_d = nc.dram_tensor("out", [S, D], F32, kind="ExternalOutput").ap()

    xbf_dram = nc.dram_tensor("xbfd", [S, D], BF16, kind="Internal").ap()
    smd = nc.dram_tensor("smd", [S, E], F32, kind="Internal").ap()
    x1acc = nc.dram_tensor("x1acc", [S, D], F32, kind="Internal").ap()
    compact = [nc.dram_tensor(f"compact{e}", [KSEL, ROWB], BF16, kind="Internal").ap()
               for e in range(E)]
    d8a = nc.dram_tensor("d8a", [8, 1], F32, kind="Internal").ap()

    VS = {}
    for i, name in enumerate([
            "ones", "v1col", "xzero", "ids", "lohi",
            "cwq", *[f"ewq{j}" for j in range(8)],
            "cwk", *[f"ewk{j}" for j in range(8)],
            "cwv", *[f"ewv{j}" for j in range(8)],
            "cwo", *[f"ewo{j}" for j in range(8)],
            *[f"ewg{j}" for j in range(8)],
            *[f"eb1_{j}" for j in range(32)],
            "b2c0"]):
        VS[name] = i + 1
    N_VS = len(VS)

    TS = {}
    for i, name in enumerate([
            *[f"twq{j}" for j in range(8)],
            *[f"twk{j}" for j in range(8)],
            *[f"twv{j}" for j in range(8)],
            *[f"two{j}" for j in range(8)],
            *[f"twg{j}" for j in range(8)],
            *[f"tb1_{j}" for j in range(32)]]):
        TS[name] = i + 1
    N_TS = len(TS)

    T_PRJ = 3 * ngroups
    V_PRJ = 5 * ngroups
    _N = S // 128
    LC = [[2, 3, 4, 5, 6, 7, 8, 14], [0, 1, 9, 10, 11, 12, 13, 15]]

    def tA(c):
        return 1 if c == 0 else 16 * c

    def tK(c):
        return 2 if c == 0 else 16 * c + 1

    def tV(c):
        return 3 if c == 0 else 16 * c + 2

    def tnum(c):
        return 4 if c == 0 else 16 * c + 4

    def tdel(c):
        return 5 if c == 0 else 16 * c + 5

    def tndt(c):
        return 6 if c == 0 else 16 * c + 6

    def topj(c):
        return 7 if c == 0 else 16 * c + 7

    def ttr(c, j):
        return 16 * c + 8 + j

    def tgate(c):
        return 16 * c + 19 if c < _N - 1 else 16 * c + 16

    def vam(c):
        return 1 if c == 0 else 10 * c - 1

    def vst(c):
        return 2 if c == 0 else 10 * c

    def vnd(c):
        return 10 * c + 3

    def vadda(c):
        return 10 * c + 4

    def vaddb(c):
        return 10 * c + 5

    def vve(c):
        return 10 * c + 6

    def vnmr(c):
        return 10 * c + 7

    def vpu(c):
        return 10 * c + 8

    def vlg(c):
        return 10 * c + 11 if c < _N - 1 else 10 * c + 9

    def vfin(c):
        return 10 * c + 12 if c < _N - 1 else 10 * c + 10

    def akv(c):
        return 1 if c == 0 else 10 * c

    def ak1(c):
        return 2 if c == 0 else 10 * c + 1

    def av1(c):
        return 3 if c == 0 else 10 * c + 2

    def andt(c):
        return 4 if c == 0 else 10 * c + 4

    def asq(c):
        return 10 * c + 5

    def ar(c):
        return 10 * c + 6

    def axbf(c):
        return 10 * c + 7

    def aevA(c):
        return 10 * c + 8

    def aevB(c):
        return 10 * c + 9

    def aexp(c):
        return 10 * c + 13 if c < _N - 1 else 10 * c + 10
    T_AT = 16
    V_AT = 10
    A_AT = 10
    RANK_EXTRA = 9
    SEL_V_TOTAL = 2 * BIS + RANK_EXTRA
    T_FFN = nslots + FT + 4 * nslots
    V_FFN = 3 + 4 * nslots
    A_FFN = FT

    with ExitStack() as st_:
        def sb(name, shape, dt):
            return st_.enter_context(nc.sbuf_tensor(name, shape, dt))

        def ps(name, shape, dt=F32):
            return st_.enter_context(nc.psum_tensor(name, shape, dt))

        def sem(name):
            return st_.enter_context(nc.semaphore(name))

        big0 = sb("big0", [128, max(HT * S, FT * 512)], BF16)

        def xT(j):
            return big0[:, j * S:(j + 1) * S]

        def h1T(k_):
            return big0[:, k_ * KSEL:(k_ + 1) * KSEL]

        xbf = sb("xbf", [128, nchunks, ROWB], BF16)
        w1q = [sb(f"w1q{i}", [128, HT, 512], BF16) for i in range(3)]
        w2q = [sb(f"w2q{i}", [128, FT, 256], BF16) for i in range(3)]
        ovl = sb("ovl", [128, 4096], F32)
        w2q.append(ovl[:].bitcast(BF16).rearrange("p (f h) -> p f h", f=FT))
        exT2 = [sb(f"exT{i}", [128, 9 * KSEL], BF16) for i in range(2)]
        outsb = sb("outsb", [128, nslots, D], F32)
        qkv = outsb[:].rearrange("p s d -> p (s d)").bitcast(BF16)
        QT = qkv[0:QDIM, 0 * S:1 * S]
        KT = qkv[0:QDIM, 1 * S:2 * S]
        VT = qkv[0:QDIM, 2 * S:3 * S]

        x_tile = [ovl[:, 2048:3072], ovl[:, 3072:4096]]
        sq_scr = exT2[0][:, 0:D]
        sm_sb = sb("sm_sb", [128, nchunks, E], F32)
        x1T_stage = ovl[:, 1024:2048]

        am_sb = sb("am_sb", [128, 128], BF16)
        k1_sb = sb("k1_sb", [128, QDIM], BF16)
        v1_sb = sb("v1_sb", [128, QDIM + 1], BF16)
        nd_sb = sb("nd_sb", [128, QDIM], BF16)
        ndT_sb = sb("ndT_sb", [QDIM, 128], BF16)
        kv_sb = sb("kv_sb", [QDIM, QDIM + 1], BF16)
        state_f = sb("state_f", [QDIM, QDIM + 1], F32)
        e_sb = sb("e_sb", [128, E], F32)
        scal = sb("scal", [128, 16], F32)
        lg_sb = sb("lg_sb", [128, E], F32)

        smbis = sb("smbis", [128, S16], F32)
        cmpb = sb("cmpb", [128, S16], BF16)
        bcol = sb("bcol", [128, 8], F32)
        predu = sb("predu", [128, 1], U8)
        npredu = sb("npredu", [128, 1], U8)
        lo8 = sb("lo8", [1, 8], F32)
        lo_tm = sb("lo_tm", [128, E], F32)
        cmp_tm = sb("cmp_tm", [128, nchunks, E], F32)
        cmpb_tm = sb("cmpb_tm", [128, nchunks, E], BF16)
        rsum = sb("rsum", [128, E], BF16)
        pbufA = sb("pbufA", [128, nchunks, E], F32)
        pbufB = sb("pbufB", [128, nchunks, E], F32)
        exclp = sb("exclp", [128, E], F32)
        off_i = sb("off_i", [128, nchunks, E], I32)

        exv = sb("exv", [128, 32 * nslots], BF16)
        G_sb = sb("G_sb", [128, nslots], F32)
        idf_sb = sb("idf_sb", [128, nslots], F32)
        idx_i = sb("idx_i", [128, nslots], I32)

        stag_f = ovl[:, 0:1024]
        stag_b = sb("stag_b", [QDIM, D], BF16)
        b1tmp = exT2[1][0:8, 0:1024].bitcast(F32)
        wqT = sb("wqT", [128, HT, QDIM], BF16)
        wkT = sb("wkT", [128, HT, QDIM], BF16)
        wvT = sb("wvT", [128, HT, QDIM], BF16)
        wo_b = sb("wo_b", [128, HT, QDIM], BF16)
        woT = sb("woT", [QDIM, D], BF16)
        wgT = sb("wgT", [128, HT, E], F32)
        b1f = sb("b1f", [128, FT, E], F32)
        b2bf2 = [woT[0:1, :], stag_b[0:1, :]]
        ones_f = sb("ones_f", [1, 128], F32)
        ones_b = sb("ones_b", [1, 128], BF16)
        triu = sb("triu_s", [128, 128], BF16)
        tris = sb("tris_s", [128, 128], BF16)
        eg16 = sb("eg16_s", [128, 128], BF16)
        identf = sb("identf_s", [128, 128], F32)
        identb = sb("identb_s", [128, 128], BF16)
        ids_sb = sb("ids_sb", [128, nchunks * 2], BF16)
        cst_sb = sb("cst_sb", [128, 8], F32)
        _w1q0f = w1q[0][:].rearrange("p j f -> p (j f)").bitcast(F32)
        _w1q1f = w1q[1][:].rearrange("p j f -> p (j f)").bitcast(F32)
        x2t = [_w1q0f[:, 0:1024], _w1q0f[:, 1024:2048],
               _w1q1f[:, 0:1024], _w1q1f[:, 1024:2048]]

        ps_a = ps("ps_a", [128, 512])
        ps_b = ps("ps_b", [128, 512])
        ps_c = ps("ps_c", [128, 512])
        ps_n = ps("ps_n", [128, 512])
        ps_state = ps("ps_state", [QDIM, QDIM + 1])
        ps_tr = ps("ps_tr", [128, 128], BF16)
        ps_tr2 = ps("ps_tr2", [128, 128], BF16)
        ps_g = ps("ps_g", [128, 8])

        s_sw = sem("s_sw")
        s_ldk = sem("s_ldk")
        s_ldv = sem("s_ldv")
        s_ldo = sem("s_ldo")
        s_ldg = sem("s_ldg")
        s_b1ld = sem("s_b1ld")
        s_vsw = sem("s_vsw")
        s_tsw = sem("s_tsw")
        s_cast = sem("s_cast")
        s_xt = sem("s_xt")
        s_tprj = sem("s_tprj")
        s_vprj = sem("s_vprj")
        s_aprj = sem("s_aprj")
        s_tat = sem("s_tat")
        s_vat = sem("s_vat")
        s_aat = sem("s_aat")
        s_xch = [sem("s_xch0"), sem("s_xch1")]
        s_x1st = [sem("s_x1st0"), sem("s_x1st1")]
        s_smd = sem("s_smd")
        s_bisld = sem("s_bisld")
        s_tsel = sem("s_tsel")
        s_vsel = sem("s_vsel")
        s_sel2 = sem("s_sel2")
        s_wl1 = [sem("s_wl1a"), sem("s_wl1b"), sem("s_wl1c")]
        s_wl2 = [sem(f"s_wl2{i}") for i in range(4)]
        s_scat = [sem(f"s_scat{e}") for e in range(E)]
        s_exl2 = [sem("s_exla"), sem("s_exlb")]
        s_tffn = sem("s_tffn")
        s_vffn = sem("s_vffn")
        s_affn = sem("s_affn")
        s_b2p = [sem("s_b2a"), sem("s_b2b")]
        s_gsc = sem("s_gsc")
        s_x2ld = [sem(f"s_x2ld{i}") for i in range(4)]
        s_ost = [sem(f"s_ost{i}") for i in range(4)]
        s_vself = sem("s_vself")
        s_aself = sem("s_aself")

        block = st_.enter_context(nc.Block())

        C_EPS = cst_sb[:, 0:1]

        class StopSync(Exception):
            pass

        def _dummy_out(sync):
            if stop >= 1:
                sync.wait_ge(s_vsw, N_VS)
                sync.wait_ge(s_tsw, N_TS)
            if stop >= 2:
                sync.wait_ge(s_vprj, V_PRJ)
            if stop >= 3:
                sync.wait_ge(s_vat, V_AT * nchunks)
                sync.wait_ge(s_aat, A_AT * nchunks)
                sync.wait_ge(s_x1st[0], 16 * ((nchunks + 1) // 2))
                sync.wait_ge(s_x1st[1], 16 * (nchunks // 2))
            if stop >= 4:
                sync.wait_ge(s_vsel, SEL_V_TOTAL)
            if stop >= 5:
                for e in range(E):
                    sync.wait_ge(s_scat[e], 16 * nchunks)
            for cc in range(nchunks):
                sync.dma_start(out=out_d[128 * cc:128 * (cc + 1), :],
                               in_=x_tile[0][:]).then_inc(s_ost[cc % 4], 16)

        def _sync_body(sync):
            for src, dst in [
                    (triu_d, triu[:]), (tris_d, tris[:]), (eg_d, eg16[:]),
                    (idf_d, identf[:]), (idb_d, identb[:]), (ids_d, ids_sb[:]),
                    (cst_d, cst_sb[:]),
                    (x_d[0:128, :], x_tile[0][:]), (x_d[128:256, :], x_tile[1][:]),
                    (wq_d, stag_f[0:QDIM, :])]:
                sync.dma_start(out=dst, in_=src).then_inc(s_sw, 16)
            sync.wait_ge(s_vsw, VS["cwq"])
            sync.dma_start(out=stag_f[0:QDIM, :], in_=wk_d).then_inc(s_ldk, 16)
            sync.wait_ge(s_vsw, VS["cwk"])
            sync.dma_start(out=stag_f[0:QDIM, :], in_=wv_d).then_inc(s_ldv, 16)
            sync.wait_ge(s_vsw, VS["cwv"])
            sync.dma_start(
                out=stag_f[:, 0:512].rearrange("p (j m) -> p j m", j=HT),
                in_=wo_d.rearrange("(j p) m -> p j m", p=128)).then_inc(s_ldo, 16)
            sync.wait_ge(s_vsw, VS["cwo"])
            sync.dma_start(out=stag_f[0:E, :], in_=wg_d).then_inc(s_ldg, 16)
            sync.wait_ge(s_tsw, TS["twg7"])

            for c in range(8 if stop >= 1 else 0):
                if c > 0:
                    sync.wait_ge(s_tsw, TS[f"tb1_{4 * c - 1}"])
                sync.dma_start(out=b1tmp[:],
                               in_=b1_d[:, 512 * c:512 * (c + 1)]).then_inc(s_b1ld, 16)

            if stop < 1:
                raise StopSync

            if stop < 2:
                raise StopSync
            sync.wait_ge(s_cast, 16 * HT)
            for j in range(HT):
                sync.dma_start(out=xT(j), in_=xbf_dram[:, 128 * j:128 * (j + 1)],
                               transpose=True).then_inc(s_xt, 16)

            if stop < 3:
                raise StopSync
            for cc in range(nchunks):
                if cc >= 2:
                    sync.wait_ge(s_aat, A_AT * (cc - 2) + 7)
                    sync.wait_ge(s_x1st[cc % 2], 16 * (cc // 2))
                    sync.dma_start(out=x_tile[cc % 2][:],
                                   in_=x_d[128 * cc:128 * (cc + 1), :]).then_inc(s_xch[cc % 2], 16)
                sync.wait_ge(s_aat, A_AT * cc + 7)
                sync.dma_start(out=x1acc[128 * cc:128 * (cc + 1), :],
                               in_=x_tile[cc % 2][:]).then_inc(s_x1st[cc % 2], 16)

            if stop < 4:
                raise StopSync
            sync.wait_ge(s_vat, V_AT * nchunks)
            sync.dma_start(out=smd.rearrange("(ct p) e -> p ct e", p=128),
                           in_=sm_sb[:]).then_inc(s_smd, 16)
            sync.wait_ge(s_smd, 16)
            for e in range(4):
                sync.dma_start(
                    out=smbis[16 * e:16 * (e + 1), :].unsqueeze(2),
                    in_=smd.rearrange("(pp c) e -> pp c e", pp=16)[:, :, e:e + 1],
                    allow_slow_non_contiguous=True).then_inc(s_bisld, 16)

            sync.wait_ge(s_vsel, 2 * BIS)
            sync.dma_start(out=d8a, in_=bcol[0:128:16, 0:1]).then_inc(s_sel2, 16)
            sync.wait_ge(s_sel2, 16)
            sync.dma_start(out=lo8[:], in_=d8a.rearrange("e o -> o e")).then_inc(s_sel2, 16)

            if stop < 6:
                raise StopSync
            for e in range(E):
                sync.wait_ge(s_scat[e], 16 * nchunks)
                if e > 1:
                    sync.wait_ge(s_tffn, (e - 2) * T_FFN + nslots + FT)
                for t in range(9):
                    sync.dma_start(out=exT2[e % 2][:, t * KSEL:(t + 1) * KSEL],
                                   in_=compact[e][:, 128 * t:128 * (t + 1)],
                                   transpose=True).then_inc(s_exl2[e % 2], 16)

            sync.wait_ge(s_gsc, 16 * nslots * E)
            for cc in range(min(4, nchunks)):
                sync.dma_start(out=x2t[cc % 4][:],
                               in_=x1acc[128 * cc:128 * (cc + 1), :]).then_inc(s_x2ld[cc % 4], 16)
            for cc in range(nchunks):
                sync.wait_ge(s_vat, V_AT * nchunks + 3 * cc + 3)
                sync.dma_start(out=out_d[128 * cc:128 * (cc + 1), :],
                               in_=x2t[cc % 4][:]).then_inc(s_ost[cc % 4], 16)
                if cc + 4 < nchunks:
                    sync.wait_ge(s_ost[cc % 4], 16 * (cc // 4 + 1))
                    sync.dma_start(out=x2t[cc % 4][:],
                                   in_=x1acc[128 * (cc + 4):128 * (cc + 5), :]
                                   ).then_inc(s_x2ld[cc % 4], 16)

        @block.sync
        def _(sync):
            try:
                _sync_body(sync)
            except StopSync:
                _dummy_out(sync)

        @block.gpsimd
        def _(gpsimd):
            for j in range(HT if stop >= 2 else 0):
                gpsimd.dma_start(out=xbf_dram[:, 128 * j:128 * (j + 1)],
                                 in_=x_d[:, 128 * j:128 * (j + 1)]).then_inc(s_cast, 16)

            def w1load(e, g8):
                gi = e * 8 + g8
                if gi >= 3:
                    pe, pg = divmod(gi - 3, 8)
                    gpsimd.wait_ge(s_tffn, pe * T_FFN + nslots + pg * 4 + 4)
                gpsimd.dma_start(
                    out=w1q[gi % 3][:],
                    in_=w1_d[e].rearrange("(j p) f -> p j f", p=128)[:, :, 512 * g8:512 * (g8 + 1)],
                ).then_inc(s_wl1[gi % 3], 16)

            def w2load(e, hq):
                qi = e * 4 + hq
                if qi >= 4:
                    pe, pq = divmod(qi - 4, 4)
                    done = ((pq // 2) * nslots + nslots - 1) * 2 + (pq % 2) + 1
                    gpsimd.wait_ge(s_tffn, pe * T_FFN + nslots + FT + done)
                gpsimd.dma_start(
                    out=w2q[qi % 4][:],
                    in_=w2_d[e].rearrange("(j p) h -> p j h", p=128)[:, :, 256 * hq:256 * (hq + 1)],
                ).then_inc(s_wl2[qi % 4], 16)

            bcreg = [None]

            def scatters(e, lo=0, hi=None):
                if hi is None:
                    hi = nchunks
                if e == 0 and lo == 0:
                    gpsimd.wait_ge(s_vsel, SEL_V_TOTAL)
                    gpsimd.wait_ge(s_aat, A_AT * nchunks)
                    bcreg[0] = gpsimd.to_reg(KSEL - 1)
                elif lo == 0:
                    gpsimd.wait_ge(s_scat[e - 1], 16 * nchunks)
                for ct in range(lo, hi):
                    gpsimd.indirect_dma_start(
                        compact[e], IOA(off_i[:, ct, e:e + 1], 0),
                        xbf[:, ct, :], None,
                        bounds_check=bcreg[0], oob_is_err=False,
                    ).then_inc(s_scat[e], 16)

            def b2load(e):
                if e >= 2:
                    gpsimd.wait_ge(s_tffn, (e - 1) * T_FFN)
                gpsimd.dma_start(out=b2bf2[e % 2][:],
                                 in_=b2_d[e:e + 1, :]).then_inc(s_b2p[e % 2], 16)

            def scatter_adds(e):
                gpsimd.wait_ge(s_x1st[0], 16 * ((nchunks + 1) // 2))
                gpsimd.wait_ge(s_x1st[1], 16 * (nchunks // 2))
                gpsimd.wait_ge(s_gsc, 16 * nslots * e)
                gpsimd.wait_ge(s_vffn, e * V_FFN + V_FFN)
                for sN in range(nslots):
                    gpsimd.indirect_dma_start(
                        x1acc, IOA(idx_i[:, sN:sN + 1], 0),
                        outsb[:, sN, :], None,
                        compute_op=ALU.add,
                    ).then_inc(s_gsc, 16)

            if stop == 5:
                for e in range(E):
                    scatters(e)
            if stop >= 6:
                w1load(0, 0); w1load(0, 1); w1load(0, 2)
                w2load(0, 0); w2load(0, 1); w2load(0, 2)
                scatters(0)
                gpsimd.wait_ge(s_x1st[0], 16 * ((nchunks + 1) // 2))
                gpsimd.wait_ge(s_x1st[1], 16 * (nchunks // 2))
                w2load(0, 3)
                b2load(0); b2load(1)
            for e in range(E if stop >= 6 else 0):
                if e > 0:
                    scatter_adds(e - 1)
                for g8 in range(3, 8):
                    w1load(e, g8)
                if e + 1 < E:
                    w1load(e + 1, 0); w1load(e + 1, 1); w1load(e + 1, 2)
                    scatters(e + 1)
                    for hq in range(4):
                        w2load(e + 1, hq)
                if e + 2 < E:
                    b2load(e + 2)
            if stop >= 6:
                scatter_adds(E - 1)
            for _i in range(4):
                gpsimd.wait_ge(s_ost[_i], 16 * ((nchunks + 3 - _i) // 4))

        @block.tensor
        def _(tensor):
            tensor.nop()
            if stop < 1:
                return
            for wname in ["wq", "wk", "wv"]:
                cast_key = {"wq": "cwq", "wk": "cwk", "wv": "cwv"}[wname]
                for j in range(8):
                    if j == 0:
                        tensor.wait_ge(s_vsw, VS[cast_key])
                    if j >= 2:
                        tensor.wait_ge(s_vsw, VS[f"e{wname}{j - 2}"])
                    dst = ps_tr[:, 0:QDIM] if j % 2 == 0 else ps_tr2[:, 0:QDIM]
                    tensor.transpose(dst, stag_b[:, 128 * j:128 * (j + 1)],
                                     identb[0:QDIM, 0:QDIM]).then_inc(s_tsw, 1)
            for j in range(8):
                tensor.wait_ge(s_vsw, VS["cwo"] if j < 2 else VS[f"ewo{j - 2}"])
                dst = ps_tr[0:QDIM, 0:128] if j % 2 == 0 else ps_tr2[0:QDIM, 0:128]
                tensor.transpose(dst, wo_b[:, j, :], identb[0:128, 0:128]).then_inc(s_tsw, 1)
            tensor.wait_ge(s_ldg, 16)
            for j in range(8):
                if j >= 1:
                    tensor.wait_ge(s_vsw, VS[f"ewg{j - 1}"])
                tensor.transpose(ps_g[:], stag_f[0:E, 128 * j:128 * (j + 1)],
                                 identf[0:E, 0:E]).then_inc(s_tsw, 1)
            for j in range(FT):
                c, cc2 = divmod(j, 4)
                tensor.wait_ge(s_b1ld, 16 * (c + 1))
                if j == 0:
                    tensor.wait_ge(s_vsw, VS["ewg7"])
                if j > 0:
                    tensor.wait_ge(s_vsw, VS[f"eb1_{j - 1}"])
                tensor.transpose(ps_g[:], b1tmp[:, 128 * cc2:128 * (cc2 + 1)],
                                 identf[0:8, 0:8]).then_inc(s_tsw, 1)

            if stop < 2:
                return
            tensor.wait_ge(s_xt, 16 * HT)
            for g in range(ngroups):
                sl = slice(gw * g, gw * (g + 1))
                if g > 0:
                    tensor.wait_ge(s_vprj, 5 * (g - 1) + 2)
                for j in range(HT):
                    mm = tensor.matmul(ps_a[0:QDIM, 0:gw], wqT[:, j, :], xT(j)[:, sl],
                                       start=(j == 0), stop=(j == HT - 1))
                mm.then_inc(s_tprj, 1)
                if g > 0:
                    tensor.wait_ge(s_vprj, 5 * (g - 1) + 4)
                for j in range(HT):
                    mm = tensor.matmul(ps_b[0:QDIM, 0:gw], wkT[:, j, :], xT(j)[:, sl],
                                       start=(j == 0), stop=(j == HT - 1))
                mm.then_inc(s_tprj, 1)
                if g > 0:
                    tensor.wait_ge(s_vprj, 5 * (g - 1) + 5)
                for j in range(HT):
                    mm = tensor.matmul(ps_c[0:QDIM, 0:gw], wvT[:, j, :], xT(j)[:, sl],
                                       start=(j == 0), stop=(j == HT - 1))
                mm.then_inc(s_tprj, 1)

            if stop < 3:
                return
            def t_qkv_head(c):
                csl_ = slice(128 * c, 128 * (c + 1))
                if c > 0:
                    tensor.wait_ge(s_vat, vam(c - 1))
                tensor.matmul(ps_c[:, 0:128], KT[:, csl_], QT[:, csl_],
                              start=True, stop=True).then_inc(s_tat, 1)
                if c > 0:
                    tensor.wait_ge(s_aat, andt(c - 1))
                tensor.transpose(ps_tr[:, 0:QDIM], KT[:, csl_],
                                 identb[0:QDIM, 0:QDIM]).then_inc(s_tat, 1)
                if c > 0:
                    tensor.wait_ge(s_aat, av1(c - 1))
                tensor.transpose(ps_tr2[:, 0:QDIM], VT[:, csl_],
                                 identb[0:QDIM, 0:QDIM]).then_inc(s_tat, 1)

            def t_attn_body(c):
                csl_ = slice(128 * c, 128 * (c + 1))
                tensor.wait_ge(s_vat, vam(c))
                tensor.wait_ge(s_aat, av1(c))
                mm = tensor.matmul(ps_n[:, 0:QDIM + 1], am_sb[:], v1_sb[:],
                                   start=True, stop=(c == 0))
                if c > 0:
                    mm = tensor.matmul(ps_n[:, 0:QDIM + 1], QT[:, csl_], kv_sb[:],
                                       start=False, stop=True)
                mm.then_inc(s_tat, 1)
                tensor.wait_ge(s_vat, vst(c))
                tensor.matmul(ps_state[:], k1_sb[:], v1_sb[:],
                              start=True, stop=True).then_inc(s_tat, 1)
                tensor.wait_ge(s_vat, vnd(c))
                tensor.wait_ge(s_aat, ak1(c))
                tensor.transpose(ps_tr[0:QDIM, 0:128], nd_sb[:],
                                 identb[0:128, 0:128]).then_inc(s_tat, 1)
                tensor.wait_ge(s_aat, andt(c))
                tensor.matmul(ps_a[:, 0:512], ndT_sb[:], woT[:, 0:512],
                              start=True, stop=True)
                tensor.matmul(ps_b[:, 0:512], ndT_sb[:], woT[:, 512:1024],
                              start=True, stop=True).then_inc(s_tat, 1)

            tensor.wait_ge(s_vprj, V_PRJ)
            t_qkv_head(0)
            t_attn_body(0)
            for cc in range(nchunks):
                par_t = x_tile[cc % 2]
                for j in range(HT):
                    if j == 0:
                        tensor.wait_ge(s_vat, vpu(cc))
                    pst_x = [ps_a, ps_b][j // 4]
                    tensor.transpose(pst_x[:, 128 * (j % 4):128 * (j % 4 + 1)],
                                     par_t[:, 128 * j:128 * (j + 1)],
                                     identf[:]).then_inc(s_tat, 1)
                if cc + 1 < nchunks:
                    t_qkv_head(cc + 1)
                tensor.wait_ge(s_aat, aevB(cc))
                for j in range(HT):
                    mm = tensor.matmul(ps_g[:, 0:E],
                                       x1T_stage[:, 128 * j:128 * (j + 1)],
                                       wgT[:, j, :], start=(j == 0), stop=(j == HT - 1))
                mm.then_inc(s_tat, 1)
                if cc + 1 < nchunks:
                    t_attn_body(cc + 1)

            if stop < 4:
                return
            for i in range(BIS):
                tensor.wait_ge(s_vsel, 2 * i + 1)
                if i > 0:
                    tensor.wait_ge(s_vsel, 2 * (i - 1) + 2)
                tensor.matmul(ps_n[:, 0:S16], eg16[:], cmpb[:],
                              start=True, stop=True).then_inc(s_tsel, 1)
            tensor.wait_ge(s_sel2, 32)
            tensor.matmul(ps_g[:, 0:E], ones_f[:], lo8[:],
                          start=True, stop=True).then_inc(s_tsel, 1)
            tensor.wait_ge(s_vsel, 2 * BIS + 4)
            tensor.matmul(ps_n[:, 0:E], tris[:], rsum[:],
                          start=True, stop=True).then_inc(s_tsel, 1)

            if stop < 6:
                return
            for e in range(E):
                TB = e * T_FFN
                AB = e * A_FFN
                VB = e * V_FFN
                tensor.wait_ge(s_exl2[e % 2], 16 * 9 * (e // 2 + 1))
                if e == 0:
                    tensor.wait_ge(s_vat, V_AT * nchunks)
                    tensor.wait_ge(s_vsel, SEL_V_TOTAL)
                if e > 0:
                    tensor.wait_ge(s_vffn, (e - 1) * V_FFN + 1)
                for sN in range(nslots):
                    tensor.transpose(ps_tr[:, 32 * sN:32 * (sN + 1)],
                                     exT2[e % 2][0:32, 8 * KSEL + 128 * sN:8 * KSEL + 128 * (sN + 1)],
                                     identb[0:32, 0:32]).then_inc(s_tffn, 1)
                if e > 0:
                    tensor.wait_ge(s_vffn, e * V_FFN)
                for g8 in range(8):
                    gi = e * 8 + g8
                    tensor.wait_ge(s_wl1[gi % 3], 16 * (gi // 3 + 1))
                    for jj in range(4):
                        kk = g8 * 4 + jj
                        psum = [ps_a, ps_b][kk % 2]
                        if AB + kk - 1 > 0:
                            tensor.wait_ge(s_affn, AB + kk - 1)
                        for kt in range(HT):
                            mm = tensor.matmul(psum[:, 0:KSEL],
                                               w1q[gi % 3][:, kt, 128 * jj:128 * (jj + 1)],
                                               exT2[e % 2][:, kt * KSEL:(kt + 1) * KSEL],
                                               start=(kt == 0), stop=(kt == HT - 1))
                        mm.then_inc(s_tffn, 1)
                tensor.wait_ge(s_affn, AB + FT)
                tensor.wait_ge(s_vffn, VB + 3)
                tensor.wait_ge(s_b2p[e % 2], 16 * (e // 2 + 1))
                PS4 = [ps_c, ps_n, ps_a, ps_b]
                for p2 in range(2):
                    q0, q1 = 2 * p2, 2 * p2 + 1
                    gi0, gi1 = e * 4 + q0, e * 4 + q1
                    tensor.wait_ge(s_wl2[gi0 % 4], 16 * (gi0 // 4 + 1))
                    tensor.wait_ge(s_wl2[gi1 % 4], 16 * (gi1 // 4 + 1))
                    for sN in range(nslots):
                        it = p2 * nslots + sN
                        psA = PS4[(2 * it) % 4]
                        psB = PS4[(2 * it + 1) % 4]
                        for half in range(2):
                            k = it * 2 + half
                            if k >= 4:
                                tensor.wait_ge(s_vffn, VB + 3 + (k - 4) + 1)
                            elif e > 0:
                                tensor.wait_ge(s_vffn,
                                               (e - 1) * V_FFN + 3 + (4 * nslots - 4 + k % 4) + 1)
                        for f in range(FT):
                            tensor.matmul(psA[:, 0:256],
                                          h1T(f)[:, 128 * sN:128 * (sN + 1)],
                                          w2q[gi0 % 4][:, f, :],
                                          start=(f == 0), stop=False)
                            tensor.matmul(psB[:, 0:256],
                                          h1T(f)[:, 128 * sN:128 * (sN + 1)],
                                          w2q[gi1 % 4][:, f, :],
                                          start=(f == 0), stop=False)
                        tensor.matmul(psA[:, 0:256], ones_b[0:1, :],
                                      b2bf2[e % 2][0:1, 256 * q0:256 * (q0 + 1)],
                                      start=False, stop=True).then_inc(s_tffn, 1)
                        tensor.matmul(psB[:, 0:256], ones_b[0:1, :],
                                      b2bf2[e % 2][0:1, 256 * q1:256 * (q1 + 1)],
                                      start=False, stop=True).then_inc(s_tffn, 1)

        @block.vector
        def _(vector):
            vcnt = [0]

            def vg(ins, milestone=None, n=1):
                ins.then_inc(s_vself, 1)
                vcnt[0] += 1
                vector.wait_ge(s_vself, vcnt[0])
                if milestone is not None:
                    vector.sem_inc(milestone, n)

            vg(vector.memset(ones_f[:], 1.0), s_vsw)
            vg(vector.memset(v1_sb[:, QDIM:QDIM + 1], 1.0), s_vsw)
            vg(vector.memset(xbf[:, :, 1042:ROWB], 0), s_vsw)
            vector.wait_ge(s_sw, 160)
            vg(vector.tensor_copy(xbf[:, :, 1040:1042],
                                  ids_sb[:].rearrange("p (c t) -> p c t", t=2)), s_vsw)
            vg(vector.memset(state_f[:], 0.0))
            vg(vector.memset(bcol[:, 0:1], 0.0))
            vg(vector.memset(bcol[:, 1:2], 1.0))
            vg(vector.memset(ones_b[:], 1.0), s_vsw)

            def _evacs(wname, wbuf):
                for j in range(8):
                    vector.wait_ge(s_tsw, TS[f"t{wname}{j}"])
                    if wname == "wo":
                        s = ps_tr[0:QDIM, 0:128] if j % 2 == 0 else ps_tr2[0:QDIM, 0:128]
                        vg(vector.tensor_copy(woT[:, 128 * j:128 * (j + 1)], s), s_vsw)
                    elif wname == "wg":
                        s = ps_tr[:, 0:E] if j % 2 == 0 else ps_tr2[:, 0:E]
                        vg(vector.tensor_copy(wgT[:, j, :], s), s_vsw)
                    else:
                        s = ps_tr[:, 0:QDIM] if j % 2 == 0 else ps_tr2[:, 0:QDIM]
                        vg(vector.tensor_copy(wbuf[:, j, :], s), s_vsw)

            vg(vector.tensor_copy(stag_b[:], stag_f[0:QDIM, :]), s_vsw)
            if stop >= 1:
                _evacs("wq", wqT)
            vector.wait_ge(s_ldk, 16)
            vector.wait_ge(s_tsw, TS["twq7"])
            vg(vector.tensor_copy(stag_b[:], stag_f[0:QDIM, :]), s_vsw)
            if stop >= 1:
                _evacs("wk", wkT)
            vector.wait_ge(s_ldv, 16)
            vector.wait_ge(s_tsw, TS["twk7"])
            vg(vector.tensor_copy(stag_b[:], stag_f[0:QDIM, :]), s_vsw)
            if stop >= 1:
                _evacs("wv", wvT)
            vector.wait_ge(s_ldo, 16)
            vg(vector.tensor_copy(wo_b[:].rearrange("p j m -> p (j m)"),
                                  stag_f[:, 0:512]), s_vsw)
            if stop >= 1:
                _evacs("wo", None)
            if stop >= 1:
                for j in range(8):
                    vector.wait_ge(s_tsw, TS[f"twg{j}"])
                    vg(vector.tensor_copy(wgT[:, j, :], ps_g[:, 0:E]), s_vsw)
            if stop < 1:
                return
            for j in range(FT):
                vector.wait_ge(s_tsw, TS[f"tb1_{j}"])
                vg(vector.tensor_copy(b1f[:, j, :], ps_g[:]), s_vsw)
            vector.sem_inc(s_vsw, 1)

            if stop < 2:
                return
            for g in range(ngroups):
                sl = slice(gw * g, gw * (g + 1))
                vector.wait_ge(s_tprj, 3 * g + 1)
                vg(vector.tensor_scalar_min(stag_f[0:QDIM, 0:gw],
                                            ps_a[0:QDIM, 0:gw], 0.0), s_vprj)
                vector.wait_ge(s_aprj, 2 * g + 1)
                vg(vector.tensor_scalar_max(stag_f[0:QDIM, 0:gw],
                                            ps_a[0:QDIM, 0:gw], 0.0))
                vg(vector.tensor_add(QT[:, sl], stag_b[:, 0:gw],
                                     stag_f[0:QDIM, 0:gw]), s_vprj)
                vector.wait_ge(s_tprj, 3 * g + 2)
                vg(vector.tensor_scalar_min(stag_f[0:QDIM, 0:gw],
                                            ps_b[0:QDIM, 0:gw], 0.0), s_vprj)
                vector.wait_ge(s_aprj, 2 * g + 2)
                vg(vector.tensor_scalar_max(stag_f[0:QDIM, 0:gw],
                                            ps_b[0:QDIM, 0:gw], 0.0))
                vg(vector.tensor_add(KT[:, sl], stag_b[:, 0:gw],
                                     stag_f[0:QDIM, 0:gw]), s_vprj)
                vector.wait_ge(s_tprj, 3 * g + 3)
                vg(vector.tensor_copy(VT[:, sl], ps_c[0:QDIM, 0:gw]), s_vprj)

            inv_d = 1.0 / D
            if stop < 3:
                return
            vector.wait_ge(s_tat, tA(0))
            vg(vector.tensor_tensor(am_sb[:], ps_c[:, 0:128], triu[:],
                                    ALU.mult), s_vat)
            vector.sem_inc(s_vat, 1)
            for cc in range(nchunks):
                vector.wait_ge(s_tat, tnum(cc))
                vg(vector.tensor_scalar_add(scal[:, 9:10], ps_n[:, QDIM:QDIM + 1], 1e-6))
                vg(vector.reciprocal(scal[:, 10:11], scal[:, 9:10]))
                vg(vector.tensor_scalar(nd_sb[:], ps_n[:, 0:QDIM], scal[:, 10:11],
                                        None, ALU.mult), s_vat)
                vector.wait_ge(s_tat, topj(cc))
                if cc >= 2:
                    vector.wait_ge(s_xch[cc % 2], 16 * (cc // 2))
                par = x_tile[cc % 2]
                vg(vector.scalar_tensor_tensor(par[:, 0:512], par[:, 0:512], 1.0,
                                               ps_a[:, 0:512], ALU.mult, ALU.add,
                                               accum_out=scal[:, 0:1]), s_vat)
                vg(vector.scalar_tensor_tensor(par[:, 512:1024], par[:, 512:1024], 1.0,
                                               ps_b[:, 0:512], ALU.mult, ALU.add,
                                               accum_out=scal[:, 1:2]), s_vat)
                vg(vector.tensor_add(scal[:, 2:3], scal[:, 0:1], scal[:, 1:2]))
                vg(vector.tensor_scalar_mul(scal[:, 3:4], scal[:, 2:3], inv_d))
                vg(vector.tensor_mul(scal[:, 4:5], scal[:, 3:4], scal[:, 3:4]))
                vector.wait_ge(s_aat, asq(cc))
                vg(vector.scalar_tensor_tensor(scal[:, 5:6], scal[:, 14:15], inv_d,
                                               scal[:, 4:5], ALU.mult, ALU.subtract),
                   s_vat)
                vector.wait_ge(s_aat, ar(cc))
                vg(vector.scalar_tensor_tensor(scal[:, 8:9], scal[:, 3:4], -1.0,
                                               scal[:, 7:8], ALU.mult, ALU.mult),
                   s_vat)
                vg(vector.scalar_tensor_tensor(
                    par[:], par[:], scal[:, 7:8],
                    scal[:, 8:9].broadcast_to((128, D)),
                    ALU.mult, ALU.add), s_vat)
                if cc + 1 < nchunks:
                    vector.wait_ge(s_tat, tA(cc + 1))
                    vg(vector.tensor_tensor(am_sb[:], ps_c[:, 0:128], triu[:],
                                            ALU.mult), s_vat)
                    vector.wait_ge(s_tat, tdel(cc))
                    vector.wait_ge(s_aat, akv(cc))
                    vg(vector.tensor_tensor(state_f[:], state_f[:], ps_state[:],
                                            ALU.add), s_vat)
                vector.wait_ge(s_tat, tgate(cc))
                vg(vector.tensor_copy(lg_sb[:], ps_g[:, 0:E]))
                vg(vector.reduce_max(scal[:, 15:16], lg_sb[:], AX))
                vg(vector.tensor_scalar_mul(scal[:, 13:14], scal[:, 15:16], -1.0),
                   s_vat)
                vector.wait_ge(s_aat, aexp(cc))
                vg(vector.reciprocal(scal[:, 12:13], scal[:, 11:12]))
                vg(vector.tensor_scalar(sm_sb[:, cc, :], e_sb[:], scal[:, 12:13], None,
                                        ALU.mult))
                vg(vector.tensor_copy(xbf[:, cc, 1024:1032], sm_sb[:, cc, :]))
                vg(vector.tensor_tensor(xbf[:, cc, 1032:1040], sm_sb[:, cc, :],
                                        xbf[:, cc, 1024:1032],
                                        ALU.subtract), s_vat)

            if stop < 4:
                return
            vector.wait_ge(s_bisld, 16 * 8)
            for i in range(BIS):
                vg(vector.tensor_scalar_mul(bcol[:, 1:2], bcol[:, 1:2], 0.5))
                vg(vector.tensor_add(bcol[:, 2:3], bcol[:, 0:1], bcol[:, 1:2]))
                vg(vector.tensor_scalar(cmpb[:], smbis[:], bcol[:, 2:3], None,
                                        ALU.is_gt), s_vsel)
                vector.wait_ge(s_tsel, i + 1)
                vg(vector.reduce_sum(bcol[:, 4:5], ps_n[:, 0:S16], AX))
                vg(vector.scalar_tensor_tensor(bcol[:, 3:4], bcol[:, 4:5],
                                               float(KSEL), bcol[:, 1:2],
                                               ALU.is_ge, ALU.mult))
                vg(vector.tensor_add(bcol[:, 0:1], bcol[:, 0:1], bcol[:, 3:4]),
                   s_vsel)
            vector.wait_ge(s_tsel, BIS + 1)
            vg(vector.tensor_copy(lo_tm[:], ps_g[:, 0:E]))
            vg(vector.tensor_tensor(cmp_tm[:], sm_sb[:],
                                    lo_tm[:].unsqueeze(1).broadcast_to((128, nchunks, E)),
                                    ALU.is_gt), s_vsel)
            vg(vector.tensor_copy(cmpb_tm[:], cmp_tm[:]), s_vsel, 2)
            with nc.allow_low_precision(reason="0/1 counts <=16 exact in bf16"):
                vg(vector.reduce_sum(rsum[:].unsqueeze(2),
                                     cmpb_tm[:].rearrange("p c e -> p e c"), AX),
                   s_vsel)
            cur, nxt = cmp_tm, pbufA
            for k2 in prefix_steps:
                vg(vector.tensor_tensor(nxt[:, k2:, :], cur[:, k2:, :],
                                        cur[:, 0:nchunks - k2, :], ALU.add))
                vg(vector.tensor_copy(nxt[:, 0:k2, :], cur[:, 0:k2, :]))
                cur, nxt = nxt, (pbufB if nxt is pbufA else pbufA)
            vg(vector.tensor_sub(nxt[:], cur[:], cmp_tm[:]))
            excl_ct = nxt
            vector.wait_ge(s_tsel, BIS + 2)
            vg(vector.tensor_copy(exclp[:], ps_n[:, 0:E]))
            rank = cur
            vg(vector.tensor_tensor(rank[:], excl_ct[:],
                                    exclp[:].unsqueeze(1).broadcast_to((128, nchunks, E)),
                                    ALU.add))
            vg(vector.scalar_tensor_tensor(rank[:], cmp_tm[:], -4096.0, rank[:],
                                           ALU.mult, ALU.add))
            vg(vector.tensor_scalar_add(rank[:], rank[:], 4096.0))
            vg(vector.tensor_copy(off_i[:], rank[:]), s_vsel, RANK_EXTRA - 4)

            if stop < 6:
                return
            for e in range(E):
                TB = e * T_FFN
                VB = e * V_FFN
                vector.wait_ge(s_tffn, TB + nslots)
                if e > 0:
                    vector.wait_ge(s_gsc, 16 * nslots * e)
                vg(vector.tensor_copy(exv[:, 0:32 * nslots],
                                      ps_tr[:, 0:32 * nslots]), s_vffn)
                for sN in range(nslots):
                    o = 32 * sN
                    vg(vector.tensor_tensor(G_sb[:, sN:sN + 1], exv[:, o + e:o + e + 1],
                                            exv[:, o + 8 + e:o + 8 + e + 1], ALU.add))
                    vg(vector.scalar_tensor_tensor(idf_sb[:, sN:sN + 1],
                                                   exv[:, o + 17:o + 18], 256.0,
                                                   exv[:, o + 16:o + 17],
                                                   ALU.mult, ALU.add))
                vg(vector.tensor_copy(idx_i[:], idf_sb[:]), s_vffn)
                vector.sem_inc(s_vffn, 1)
                PS4 = [ps_c, ps_n, ps_a, ps_b]
                for p2 in range(2):
                    for sN in range(nslots):
                        it = p2 * nslots + sN
                        for half in range(2):
                            hq = 2 * p2 + half
                            psum = PS4[(2 * it + half) % 4]
                            vector.wait_ge(s_tffn, TB + nslots + FT + it * 2 + half + 1)
                            vg(vector.tensor_scalar(outsb[:, sN, 256 * hq:256 * (hq + 1)],
                                                    psum[:, 0:256], G_sb[:, sN:sN + 1],
                                                    None, ALU.mult), s_vffn)

            def _l2sum(c):
                L = LC[c % 2]
                vector.wait_ge(s_x2ld[c % 4], 16 * (c // 4 + 1))
                par_ = x2t[c % 4]
                vg(vector.reduce_sum(scal[:, L[0]:L[0] + 1], par_[:], AX))
                vg(vector.tensor_scalar_mul(scal[:, L[1]:L[1] + 1],
                                            scal[:, L[0]:L[0] + 1], inv_d))
                vg(vector.tensor_mul(scal[:, L[2]:L[2] + 1],
                                     scal[:, L[1]:L[1] + 1],
                                     scal[:, L[1]:L[1] + 1]))

            _l2sum(0)
            for cc in range(nchunks):
                L = LC[cc % 2]
                par = x2t[cc % 4]
                vector.wait_ge(s_aat, A_AT * nchunks + 2 * cc + 1)
                vg(vector.scalar_tensor_tensor(scal[:, L[3]:L[3] + 1],
                                               scal[:, L[7]:L[7] + 1], inv_d,
                                               scal[:, L[2]:L[2] + 1],
                                               ALU.mult, ALU.subtract),
                   s_vat)
                if cc + 1 < nchunks:
                    _l2sum(cc + 1)
                vector.wait_ge(s_aat, A_AT * nchunks + 2 * cc + 2)
                vg(vector.scalar_tensor_tensor(scal[:, L[6]:L[6] + 1],
                                               scal[:, L[1]:L[1] + 1], -1.0,
                                               scal[:, L[5]:L[5] + 1],
                                               ALU.mult, ALU.mult),
                   s_vat)
                vg(vector.scalar_tensor_tensor(
                    par[:], par[:], scal[:, L[5]:L[5] + 1],
                    scal[:, L[6]:L[6] + 1].broadcast_to((128, D)),
                    ALU.mult, ALU.add), s_vat)

        @block.scalar
        def _(scalar):
            acnt = [0]

            def ag(ins):
                ins.then_inc(s_aself, 1)
                acnt[0] += 1
                scalar.wait_ge(s_aself, acnt[0])

            scalar.nop()
            if stop < 2:
                return
            for g in range(ngroups):
                scalar.wait_ge(s_vprj, 5 * g + 1)
                scalar.activation(stag_b[:, 0:gw], stag_f[0:QDIM, 0:gw],
                                  AF.Exp).then_inc(s_aprj, 1)
                scalar.wait_ge(s_vprj, 5 * g + 3)
                scalar.activation(stag_b[:, 0:gw], stag_f[0:QDIM, 0:gw],
                                  AF.Exp).then_inc(s_aprj, 1)
            if stop < 3:
                return
            scalar.wait_ge(s_vat, vst(0))
            scalar.activation(kv_sb[:], state_f[:],
                              AF.Copy).then_inc(s_aat, 1)
            scalar.wait_ge(s_tat, tK(0))
            scalar.activation(k1_sb[:], ps_tr[:, 0:QDIM],
                              AF.Copy).then_inc(s_aat, 1)
            scalar.wait_ge(s_tat, tV(0))
            scalar.activation(v1_sb[:, 0:QDIM], ps_tr2[:, 0:QDIM],
                              AF.Copy).then_inc(s_aat, 1)
            scalar.wait_ge(s_tat, tndt(0))
            scalar.activation(ndT_sb[:], ps_tr[0:QDIM, 0:128],
                              AF.Copy).then_inc(s_aat, 1)
            for cc in range(nchunks):
                par = x_tile[cc % 2]
                scalar.wait_ge(s_vat, vaddb(cc))
                scalar.activation(sq_scr[:], par[:], AF.Square,
                                  accum_out=scal[:, 14:15]).then_inc(s_aat, 1)
                scalar.wait_ge(s_vat, vve(cc))
                ag(scalar.activation(scal[:, 6:7], scal[:, 5:6], AF.Ln, bias=C_EPS))
                scalar.activation(scal[:, 7:8], scal[:, 6:7], AF.Exp,
                                  scale=-0.5).then_inc(s_aat, 1)
                scalar.wait_ge(s_vat, vpu(cc))
                scalar.activation(xbf[:, cc, 0:1024], par[:],
                                  AF.Copy).then_inc(s_aat, 1)
                scalar.wait_ge(s_tat, ttr(cc, 3))
                scalar.activation(x1T_stage[:, 0:512], ps_a[:, 0:512],
                                  AF.Copy).then_inc(s_aat, 1)
                scalar.wait_ge(s_tat, ttr(cc, 7))
                scalar.activation(x1T_stage[:, 512:1024], ps_b[:, 0:512],
                                  AF.Copy).then_inc(s_aat, 1)
                if cc + 1 < nchunks:
                    scalar.wait_ge(s_vat, vst(cc + 1))
                    scalar.activation(kv_sb[:], state_f[:],
                                      AF.Copy).then_inc(s_aat, 1)
                    scalar.wait_ge(s_tat, tK(cc + 1))
                    scalar.activation(k1_sb[:], ps_tr[:, 0:QDIM],
                                      AF.Copy).then_inc(s_aat, 1)
                    scalar.wait_ge(s_tat, tV(cc + 1))
                    scalar.activation(v1_sb[:, 0:QDIM], ps_tr2[:, 0:QDIM],
                                      AF.Copy).then_inc(s_aat, 1)
                scalar.wait_ge(s_vat, vlg(cc))
                scalar.activation(e_sb[:], lg_sb[:], AF.Exp, bias=scal[:, 13:14],
                                  accum_out=scal[:, 11:12]).then_inc(s_aat, 1)
                if cc + 1 < nchunks:
                    scalar.wait_ge(s_tat, tndt(cc + 1))
                    scalar.activation(ndT_sb[:], ps_tr[0:QDIM, 0:128],
                                      AF.Copy).then_inc(s_aat, 1)
            if stop < 4:
                return
            scalar.wait_ge(s_smd, 16)
            for e in range(4, 8):
                scalar.dma_start(
                    out=smbis[16 * e:16 * (e + 1), :].unsqueeze(2),
                    in_=smd.rearrange("(pp c) e -> pp c e", pp=16)[:, :, e:e + 1],
                    allow_slow_non_contiguous=True).then_inc(s_bisld, 16)
            if stop < 6:
                return
            for e in range(E):
                TB = e * T_FFN
                AB = e * A_FFN
                for kk in range(FT):
                    psum = [ps_a, ps_b][kk % 2]
                    scalar.wait_ge(s_tffn, TB + nslots + kk + 1)
                    scalar.activation(h1T(kk), psum[:, 0:KSEL], AF.Gelu_apprx_tanh,
                                      bias=b1f[:, kk, e:e + 1]).then_inc(s_affn, 1)
            for cc in range(nchunks):
                L = LC[cc % 2]
                par = x2t[cc % 4]
                scalar.wait_ge(s_x2ld[cc % 4], 16 * (cc // 4 + 1))
                scalar.activation(sq_scr[:], par[:], AF.Square,
                                  accum_out=scal[:, L[7]:L[7] + 1]).then_inc(s_aat, 1)
                scalar.wait_ge(s_vat, V_AT * nchunks + 3 * cc + 1)
                ag(scalar.activation(scal[:, L[4]:L[4] + 1], scal[:, L[3]:L[3] + 1],
                                     AF.Ln, bias=C_EPS))
                scalar.activation(scal[:, L[5]:L[5] + 1], scal[:, L[4]:L[4] + 1],
                                  AF.Exp, scale=-0.5).then_inc(s_aat, 1)

    return nc


def _consts(S):
    nchunks = S // 128
    i = np.arange(128)
    triu = (i[:, None] <= i[None, :]).astype(ml_dtypes.bfloat16)
    tris = (i[:, None] < i[None, :]).astype(ml_dtypes.bfloat16)
    eg = (i[:, None] // 16 == i[None, :] // 16).astype(ml_dtypes.bfloat16)
    identf = np.eye(128, dtype=np.float32)
    identb = np.eye(128).astype(ml_dtypes.bfloat16)
    t = i[:, None] + 128 * np.arange(nchunks)[None, :]
    ids = np.stack([t % 256, t // 256], axis=-1).reshape(128, nchunks * 2)
    ids = ids.astype(ml_dtypes.bfloat16)
    csts = np.tile(np.array([EPS, 0, 0, 0, 0, 0, 0, 0], np.float32), (128, 1))
    return dict(triu=triu, tris=tris, eg16=eg, identf=identf, identb=identb,
                idsc=ids, csts=csts)


def kernel(L0_attn_x, L0_attn_Wq, L0_attn_Wk, L0_attn_Wv, L0_attn_Wo,
           L0_ffn_Wg, L0_ffn_W1, L0_ffn_b1, L0_ffn_W2, L0_ffn_b2):
    B, S, _ = L0_attn_x.shape
    nc = build_program(S=S, KSEL=512, BIS=30)
    consts = _consts(S)
    base = dict(wq=np.ascontiguousarray(L0_attn_Wq),
                wk=np.ascontiguousarray(L0_attn_Wk),
                wv=np.ascontiguousarray(L0_attn_Wv),
                wo=np.ascontiguousarray(L0_attn_Wo),
                wg=np.ascontiguousarray(L0_ffn_Wg),
                w1=np.ascontiguousarray(L0_ffn_W1),
                b1=np.ascontiguousarray(L0_ffn_b1),
                w2=np.ascontiguousarray(L0_ffn_W2),
                b2=np.ascontiguousarray(L0_ffn_b2),
                **consts)
    in_maps = [dict(x=np.ascontiguousarray(L0_attn_x[b]), **base) for b in range(B)]
    res = run_bass_kernel_spmd(nc, in_maps, core_ids=list(range(B)))
    return np.stack([np.asarray(r["out"]) for r in res.results], axis=0)


# version 16
# speedup vs baseline: 1.3332x; 1.0228x over previous
import numpy as np
import ml_dtypes
import concourse.bass as bass
import concourse.mybir as mybir
from concourse.bass_utils import run_bass_kernel_spmd
from contextlib import ExitStack

F32 = mybir.dt.float32
BF16 = mybir.dt.bfloat16
I32 = mybir.dt.int32
U8 = mybir.dt.uint8
AF = mybir.ActivationFunctionType
ALU = mybir.AluOpType
IOA = bass.IndirectOffsetOnAxis
AX = mybir.AxisListType.X

D = 1024
HT = 8
E = 8
F = 4096
FT = 32
QDIM = 64
ROWB = 1152
EPS = 1e-5


def build_program(S=2048, KSEL=512, BIS=30, stop=6):
    nchunks = S // 128
    gw = min(512, S)
    ngroups = S // gw
    nslots = KSEL // 128
    S16 = S // 16
    prefix_steps = []
    k = 1
    while k < nchunks:
        prefix_steps.append(k)
        k *= 2

    nc = bass.Bass("TRN2", target_bir_lowering=False)

    x_d = nc.dram_tensor("x", [S, D], F32, kind="ExternalInput").ap()
    wq_d = nc.dram_tensor("wq", [QDIM, D], F32, kind="ExternalInput").ap()
    wk_d = nc.dram_tensor("wk", [QDIM, D], F32, kind="ExternalInput").ap()
    wv_d = nc.dram_tensor("wv", [QDIM, D], F32, kind="ExternalInput").ap()
    wo_d = nc.dram_tensor("wo", [D, QDIM], F32, kind="ExternalInput").ap()
    wg_d = nc.dram_tensor("wg", [E, D], F32, kind="ExternalInput").ap()
    w1_d = nc.dram_tensor("w1", [E, D, F], F32, kind="ExternalInput").ap()
    b1_d = nc.dram_tensor("b1", [E, F], F32, kind="ExternalInput").ap()
    w2_d = nc.dram_tensor("w2", [E, F, D], F32, kind="ExternalInput").ap()
    b2_d = nc.dram_tensor("b2", [E, D], F32, kind="ExternalInput").ap()
    triu_d = nc.dram_tensor("triu", [128, 128], BF16, kind="ExternalInput").ap()
    tris_d = nc.dram_tensor("tris", [128, 128], BF16, kind="ExternalInput").ap()
    eg_d = nc.dram_tensor("eg16", [128, 128], BF16, kind="ExternalInput").ap()
    idf_d = nc.dram_tensor("identf", [128, 128], F32, kind="ExternalInput").ap()
    idb_d = nc.dram_tensor("identb", [128, 128], BF16, kind="ExternalInput").ap()
    ids_d = nc.dram_tensor("idsc", [128, nchunks * 2], BF16, kind="ExternalInput").ap()
    cst_d = nc.dram_tensor("csts", [128, 8], F32, kind="ExternalInput").ap()
    out_d = nc.dram_tensor("out", [S, D], F32, kind="ExternalOutput").ap()

    xbf_dram = nc.dram_tensor("xbfd", [S, D], BF16, kind="Internal").ap()
    smd = nc.dram_tensor("smd", [S, E], F32, kind="Internal").ap()
    x1acc = nc.dram_tensor("x1acc", [S, D], F32, kind="Internal").ap()
    compact = [nc.dram_tensor(f"compact{e}", [KSEL, ROWB], BF16, kind="Internal").ap()
               for e in range(E)]
    d8a = nc.dram_tensor("d8a", [8, 1], F32, kind="Internal").ap()

    VS = {}
    for i, name in enumerate([
            "ones", "v1col", "xzero", "ids", "lohi",
            "cwq", *[f"ewq{j}" for j in range(8)],
            "cwk", *[f"ewk{j}" for j in range(8)],
            "cwv", *[f"ewv{j}" for j in range(8)],
            "cwo", *[f"ewo{j}" for j in range(8)],
            *[f"ewg{j}" for j in range(8)],
            *[f"eb1_{j}" for j in range(32)],
            "b2c0"]):
        VS[name] = i + 1
    N_VS = len(VS)

    TS = {}
    for i, name in enumerate([
            *[f"twq{j}" for j in range(8)],
            *[f"twk{j}" for j in range(8)],
            *[f"twv{j}" for j in range(8)],
            *[f"two{j}" for j in range(8)],
            *[f"twg{j}" for j in range(8)],
            *[f"tb1_{j}" for j in range(32)]]):
        TS[name] = i + 1
    N_TS = len(TS)

    T_PRJ = 3 * ngroups
    V_PRJ = 5 * ngroups
    _N = S // 128
    LC = [[2, 3, 4, 5, 6, 7, 8, 14], [0, 1, 9, 10, 11, 12, 13, 15]]

    def tA(c):
        return 1 if c == 0 else 16 * c

    def tK(c):
        return 2 if c == 0 else 16 * c + 1

    def tV(c):
        return 3 if c == 0 else 16 * c + 2

    def tnum(c):
        return 4 if c == 0 else 16 * c + 4

    def tdel(c):
        return 5 if c == 0 else 16 * c + 5

    def tndt(c):
        return 6 if c == 0 else 16 * c + 6

    def topj(c):
        return 7 if c == 0 else 16 * c + 7

    def ttr(c, j):
        return 16 * c + 8 + j

    def tgate(c):
        return 16 * c + 19 if c < _N - 1 else 16 * c + 16

    def vam(c):
        return 1 if c == 0 else 11 * c - 1

    def vst(c):
        return 2 if c == 0 else 11 * c

    def vnd(c):
        return 11 * c + 3

    def vadda(c):
        return 11 * c + 4

    def vaddb(c):
        return 11 * c + 5

    def vve(c):
        return 11 * c + 6

    def vnmr(c):
        return 11 * c + 7

    def vpuA(c):
        return 11 * c + 8

    def vpuB(c):
        return 11 * c + 9

    def vlg(c):
        return 11 * c + 12 if c < _N - 1 else 11 * c + 10

    def vfin(c):
        return 11 * c + 13 if c < _N - 1 else 11 * c + 11

    def akv(c):
        return 1 if c == 0 else 10 * c

    def ak1(c):
        return 2 if c == 0 else 10 * c + 1

    def av1(c):
        return 3 if c == 0 else 10 * c + 2

    def andt(c):
        return 4 if c == 0 else 10 * c + 4

    def asq(c):
        return 10 * c + 5

    def ar(c):
        return 10 * c + 6

    def axbf(c):
        return 10 * c + 7

    def aevA(c):
        return 10 * c + 8

    def aevB(c):
        return 10 * c + 9

    def aexp(c):
        return 10 * c + 13 if c < _N - 1 else 10 * c + 10
    T_AT = 16
    V_AT = 11
    A_AT = 10
    RANK_EXTRA = 9
    SEL_V_TOTAL = 2 * BIS + RANK_EXTRA
    T_FFN = nslots + FT + 4 * nslots
    V_FFN = 3 + 4 * nslots
    A_FFN = FT

    with ExitStack() as st_:
        def sb(name, shape, dt):
            return st_.enter_context(nc.sbuf_tensor(name, shape, dt))

        def ps(name, shape, dt=F32):
            return st_.enter_context(nc.psum_tensor(name, shape, dt))

        def sem(name):
            return st_.enter_context(nc.semaphore(name))

        big0 = sb("big0", [128, max(HT * S, FT * 512)], BF16)

        def xT(j):
            return big0[:, j * S:(j + 1) * S]

        def h1T(k_):
            return big0[:, k_ * KSEL:(k_ + 1) * KSEL]

        xbf = sb("xbf", [128, nchunks, ROWB], BF16)
        w1q = [sb(f"w1q{i}", [128, HT, 512], BF16) for i in range(3)]
        w2q = [sb(f"w2q{i}", [128, FT, 256], BF16) for i in range(3)]
        ovl = sb("ovl", [128, 4096], F32)
        w2q.append(ovl[:].bitcast(BF16).rearrange("p (f h) -> p f h", f=FT))
        exT2 = [sb(f"exT{i}", [128, 9 * KSEL], BF16) for i in range(2)]
        outsb = sb("outsb", [128, nslots, D], F32)
        qkv = outsb[:].rearrange("p s d -> p (s d)").bitcast(BF16)
        QT = qkv[0:QDIM, 0 * S:1 * S]
        KT = qkv[0:QDIM, 1 * S:2 * S]
        VT = qkv[0:QDIM, 2 * S:3 * S]

        x_tile = [ovl[:, 2048:3072], ovl[:, 3072:4096]]
        sq_scr = exT2[0][:, 0:D]
        sm_sb = sb("sm_sb", [128, nchunks, E], F32)
        x1T_stage = ovl[:, 1024:2048]

        am_sb = sb("am_sb", [128, 128], BF16)
        k1_sb = sb("k1_sb", [128, QDIM], BF16)
        v1_sb = sb("v1_sb", [128, QDIM + 1], BF16)
        nd_sb = sb("nd_sb", [128, QDIM], BF16)
        ndT_sb = sb("ndT_sb", [QDIM, 128], BF16)
        kv_sb = sb("kv_sb", [QDIM, QDIM + 1], BF16)
        state_f = sb("state_f", [QDIM, QDIM + 1], F32)
        e_sb = sb("e_sb", [128, E], F32)
        scal = sb("scal", [128, 16], F32)
        lg_sb = sb("lg_sb", [128, E], F32)

        smbis = sb("smbis", [128, S16], F32)
        cmpb = sb("cmpb", [128, S16], BF16)
        bcol = sb("bcol", [128, 8], F32)
        predu = sb("predu", [128, 1], U8)
        npredu = sb("npredu", [128, 1], U8)
        lo8 = sb("lo8", [1, 8], F32)
        lo_tm = sb("lo_tm", [128, E], F32)
        cmp_tm = sb("cmp_tm", [128, nchunks, E], F32)
        cmpb_tm = sb("cmpb_tm", [128, nchunks, E], BF16)
        rsum = sb("rsum", [128, E], BF16)
        pbufA = sb("pbufA", [128, nchunks, E], F32)
        pbufB = sb("pbufB", [128, nchunks, E], F32)
        exclp = sb("exclp", [128, E], F32)
        off_i = sb("off_i", [128, nchunks, E], I32)

        exv = sb("exv", [128, 32 * nslots], BF16)
        G_sb = sb("G_sb", [128, nslots], F32)
        idf_sb = sb("idf_sb", [128, nslots], F32)
        idx_i = sb("idx_i", [128, nslots], I32)

        stag_f = ovl[:, 0:1024]
        stag_b = sb("stag_b", [QDIM, D], BF16)
        b1tmp = exT2[1][0:8, 0:1024].bitcast(F32)
        wqT = sb("wqT", [128, HT, QDIM], BF16)
        wkT = sb("wkT", [128, HT, QDIM], BF16)
        wvT = sb("wvT", [128, HT, QDIM], BF16)
        wo_b = sb("wo_b", [128, HT, QDIM], BF16)
        woT = sb("woT", [QDIM, D], BF16)
        wgT = sb("wgT", [128, HT, E], F32)
        b1f = sb("b1f", [128, FT, E], F32)
        b2bf2 = [woT[0:1, :], stag_b[0:1, :]]
        ones_f = sb("ones_f", [1, 128], F32)
        ones_b = sb("ones_b", [1, 128], BF16)
        triu = sb("triu_s", [128, 128], BF16)
        tris = sb("tris_s", [128, 128], BF16)
        eg16 = sb("eg16_s", [128, 128], BF16)
        identf = sb("identf_s", [128, 128], F32)
        identb = sb("identb_s", [128, 128], BF16)
        ids_sb = sb("ids_sb", [128, nchunks * 2], BF16)
        cst_sb = sb("cst_sb", [128, 8], F32)
        _w1q0f = w1q[0][:].rearrange("p j f -> p (j f)").bitcast(F32)
        _w1q1f = w1q[1][:].rearrange("p j f -> p (j f)").bitcast(F32)
        x2t = [_w1q0f[:, 0:1024], _w1q0f[:, 1024:2048],
               _w1q1f[:, 0:1024], _w1q1f[:, 1024:2048]]

        ps_a = ps("ps_a", [128, 512])
        ps_b = ps("ps_b", [128, 512])
        ps_c = ps("ps_c", [128, 512])
        ps_n = ps("ps_n", [128, 512])
        ps_state = ps("ps_state", [QDIM, QDIM + 1])
        ps_tr = ps("ps_tr", [128, 128], BF16)
        ps_tr2 = ps("ps_tr2", [128, 128], BF16)
        ps_g = ps("ps_g", [128, 8])

        s_sw = sem("s_sw")
        s_ldk = sem("s_ldk")
        s_ldv = sem("s_ldv")
        s_ldo = sem("s_ldo")
        s_ldg = sem("s_ldg")
        s_b1ld = sem("s_b1ld")
        s_vsw = sem("s_vsw")
        s_tsw = sem("s_tsw")
        s_cast = sem("s_cast")
        s_xt = sem("s_xt")
        s_tprj = sem("s_tprj")
        s_vprj = sem("s_vprj")
        s_aprj = sem("s_aprj")
        s_tat = sem("s_tat")
        s_vat = sem("s_vat")
        s_aat = sem("s_aat")
        s_xch = [sem("s_xch0"), sem("s_xch1")]
        s_x1st = [sem("s_x1st0"), sem("s_x1st1")]
        s_smd = sem("s_smd")
        s_bisld = sem("s_bisld")
        s_tsel = sem("s_tsel")
        s_vsel = sem("s_vsel")
        s_sel2 = sem("s_sel2")
        s_wl1 = [sem("s_wl1a"), sem("s_wl1b"), sem("s_wl1c")]
        s_wl2 = [sem(f"s_wl2{i}") for i in range(4)]
        s_scat = [sem(f"s_scat{e}") for e in range(E)]
        s_exl2 = [sem("s_exla"), sem("s_exlb")]
        s_tffn = sem("s_tffn")
        s_vffn = sem("s_vffn")
        s_affn = sem("s_affn")
        s_b2p = [sem("s_b2a"), sem("s_b2b")]
        s_gsc = sem("s_gsc")
        s_x2ld = [sem(f"s_x2ld{i}") for i in range(4)]
        s_ost = [sem(f"s_ost{i}") for i in range(4)]
        s_vself = sem("s_vself")
        s_aself = sem("s_aself")

        block = st_.enter_context(nc.Block())

        C_EPS = cst_sb[:, 0:1]

        class StopSync(Exception):
            pass

        def _dummy_out(sync):
            if stop >= 1:
                sync.wait_ge(s_vsw, N_VS)
                sync.wait_ge(s_tsw, N_TS)
            if stop >= 2:
                sync.wait_ge(s_vprj, V_PRJ)
            if stop >= 3:
                sync.wait_ge(s_vat, V_AT * nchunks)
                sync.wait_ge(s_aat, A_AT * nchunks)
                sync.wait_ge(s_x1st[0], 16 * ((nchunks + 1) // 2))
                sync.wait_ge(s_x1st[1], 16 * (nchunks // 2))
            if stop >= 4:
                sync.wait_ge(s_vsel, SEL_V_TOTAL)
            if stop >= 5:
                for e in range(E):
                    sync.wait_ge(s_scat[e], 16 * nchunks)
            for cc in range(nchunks):
                sync.dma_start(out=out_d[128 * cc:128 * (cc + 1), :],
                               in_=x_tile[0][:]).then_inc(s_ost[cc % 4], 16)

        def _sync_body(sync):
            for src, dst in [
                    (triu_d, triu[:]), (tris_d, tris[:]), (eg_d, eg16[:]),
                    (idf_d, identf[:]), (idb_d, identb[:]), (ids_d, ids_sb[:]),
                    (cst_d, cst_sb[:]),
                    (x_d[0:128, :], x_tile[0][:]), (x_d[128:256, :], x_tile[1][:]),
                    (wq_d, stag_f[0:QDIM, :])]:
                sync.dma_start(out=dst, in_=src).then_inc(s_sw, 16)
            sync.wait_ge(s_vsw, VS["cwq"])
            sync.dma_start(out=stag_f[0:QDIM, :], in_=wk_d).then_inc(s_ldk, 16)
            sync.wait_ge(s_vsw, VS["cwk"])
            sync.dma_start(out=stag_f[0:QDIM, :], in_=wv_d).then_inc(s_ldv, 16)
            sync.wait_ge(s_vsw, VS["cwv"])
            sync.dma_start(
                out=stag_f[:, 0:512].rearrange("p (j m) -> p j m", j=HT),
                in_=wo_d.rearrange("(j p) m -> p j m", p=128)).then_inc(s_ldo, 16)
            sync.wait_ge(s_vsw, VS["cwo"])
            sync.dma_start(out=stag_f[0:E, :], in_=wg_d).then_inc(s_ldg, 16)
            sync.wait_ge(s_tsw, TS["twg7"])

            for c in range(8 if stop >= 1 else 0):
                if c > 0:
                    sync.wait_ge(s_tsw, TS[f"tb1_{4 * c - 1}"])
                sync.dma_start(out=b1tmp[:],
                               in_=b1_d[:, 512 * c:512 * (c + 1)]).then_inc(s_b1ld, 16)

            if stop < 1:
                raise StopSync

            if stop < 2:
                raise StopSync
            sync.wait_ge(s_cast, 16 * HT)
            for j in range(HT):
                sync.dma_start(out=xT(j), in_=xbf_dram[:, 128 * j:128 * (j + 1)],
                               transpose=True).then_inc(s_xt, 16)

            if stop < 3:
                raise StopSync
            for cc in range(nchunks):
                if cc >= 2:
                    sync.wait_ge(s_aat, A_AT * (cc - 2) + 7)
                    sync.wait_ge(s_x1st[cc % 2], 16 * (cc // 2))
                    sync.dma_start(out=x_tile[cc % 2][:],
                                   in_=x_d[128 * cc:128 * (cc + 1), :]).then_inc(s_xch[cc % 2], 16)
                sync.wait_ge(s_aat, A_AT * cc + 7)
                sync.dma_start(out=x1acc[128 * cc:128 * (cc + 1), :],
                               in_=x_tile[cc % 2][:]).then_inc(s_x1st[cc % 2], 16)

            if stop < 4:
                raise StopSync
            sync.wait_ge(s_vat, V_AT * nchunks)
            sync.dma_start(out=smd.rearrange("(ct p) e -> p ct e", p=128),
                           in_=sm_sb[:]).then_inc(s_smd, 16)
            sync.wait_ge(s_smd, 16)
            for e in range(4):
                sync.dma_start(
                    out=smbis[16 * e:16 * (e + 1), :].unsqueeze(2),
                    in_=smd.rearrange("(pp c) e -> pp c e", pp=16)[:, :, e:e + 1],
                    allow_slow_non_contiguous=True).then_inc(s_bisld, 16)

            sync.wait_ge(s_vsel, 2 * BIS)
            sync.dma_start(out=d8a, in_=bcol[0:128:16, 0:1]).then_inc(s_sel2, 16)
            sync.wait_ge(s_sel2, 16)
            sync.dma_start(out=lo8[:], in_=d8a.rearrange("e o -> o e")).then_inc(s_sel2, 16)

            if stop < 6:
                raise StopSync
            for e in range(E):
                sync.wait_ge(s_scat[e], 16 * nchunks)
                if e > 1:
                    sync.wait_ge(s_tffn, (e - 2) * T_FFN + nslots + FT)
                for t in range(9):
                    sync.dma_start(out=exT2[e % 2][:, t * KSEL:(t + 1) * KSEL],
                                   in_=compact[e][:, 128 * t:128 * (t + 1)],
                                   transpose=True).then_inc(s_exl2[e % 2], 16)

            sync.wait_ge(s_gsc, 16 * nslots * E)
            for cc in range(min(4, nchunks)):
                sync.dma_start(out=x2t[cc % 4][:],
                               in_=x1acc[128 * cc:128 * (cc + 1), :]).then_inc(s_x2ld[cc % 4], 16)
            for cc in range(nchunks):
                sync.wait_ge(s_vat, V_AT * nchunks + 3 * cc + 3)
                sync.dma_start(out=out_d[128 * cc:128 * (cc + 1), :],
                               in_=x2t[cc % 4][:]).then_inc(s_ost[cc % 4], 16)
                if cc + 4 < nchunks:
                    sync.wait_ge(s_ost[cc % 4], 16 * (cc // 4 + 1))
                    sync.dma_start(out=x2t[cc % 4][:],
                                   in_=x1acc[128 * (cc + 4):128 * (cc + 5), :]
                                   ).then_inc(s_x2ld[cc % 4], 16)

        @block.sync
        def _(sync):
            try:
                _sync_body(sync)
            except StopSync:
                _dummy_out(sync)

        @block.gpsimd
        def _(gpsimd):
            for j in range(HT if stop >= 2 else 0):
                gpsimd.dma_start(out=xbf_dram[:, 128 * j:128 * (j + 1)],
                                 in_=x_d[:, 128 * j:128 * (j + 1)]).then_inc(s_cast, 16)

            def w1load(e, g8):
                gi = e * 8 + g8
                if gi >= 3:
                    pe, pg = divmod(gi - 3, 8)
                    gpsimd.wait_ge(s_tffn, pe * T_FFN + nslots + pg * 4 + 4)
                gpsimd.dma_start(
                    out=w1q[gi % 3][:],
                    in_=w1_d[e].rearrange("(j p) f -> p j f", p=128)[:, :, 512 * g8:512 * (g8 + 1)],
                ).then_inc(s_wl1[gi % 3], 16)

            def w2load(e, hq):
                qi = e * 4 + hq
                if qi >= 4:
                    pe, pq = divmod(qi - 4, 4)
                    done = ((pq // 2) * nslots + nslots - 1) * 2 + (pq % 2) + 1
                    gpsimd.wait_ge(s_tffn, pe * T_FFN + nslots + FT + done)
                gpsimd.dma_start(
                    out=w2q[qi % 4][:],
                    in_=w2_d[e].rearrange("(j p) h -> p j h", p=128)[:, :, 256 * hq:256 * (hq + 1)],
                ).then_inc(s_wl2[qi % 4], 16)

            bcreg = [None]

            def scatters(e, lo=0, hi=None):
                if hi is None:
                    hi = nchunks
                if e == 0 and lo == 0:
                    gpsimd.wait_ge(s_vsel, SEL_V_TOTAL)
                    gpsimd.wait_ge(s_aat, A_AT * nchunks)
                    bcreg[0] = gpsimd.to_reg(KSEL - 1)
                elif lo == 0:
                    gpsimd.wait_ge(s_scat[e - 1], 16 * nchunks)
                for ct in range(lo, hi):
                    gpsimd.indirect_dma_start(
                        compact[e], IOA(off_i[:, ct, e:e + 1], 0),
                        xbf[:, ct, :], None,
                        bounds_check=bcreg[0], oob_is_err=False,
                    ).then_inc(s_scat[e], 16)

            def b2load(e):
                if e >= 2:
                    gpsimd.wait_ge(s_tffn, (e - 1) * T_FFN)
                gpsimd.dma_start(out=b2bf2[e % 2][:],
                                 in_=b2_d[e:e + 1, :]).then_inc(s_b2p[e % 2], 16)

            def scatter_adds(e):
                gpsimd.wait_ge(s_x1st[0], 16 * ((nchunks + 1) // 2))
                gpsimd.wait_ge(s_x1st[1], 16 * (nchunks // 2))
                gpsimd.wait_ge(s_gsc, 16 * nslots * e)
                gpsimd.wait_ge(s_vffn, e * V_FFN + V_FFN)
                for sN in range(nslots):
                    gpsimd.indirect_dma_start(
                        x1acc, IOA(idx_i[:, sN:sN + 1], 0),
                        outsb[:, sN, :], None,
                        compute_op=ALU.add,
                    ).then_inc(s_gsc, 16)

            if stop == 5:
                for e in range(E):
                    scatters(e)
            if stop >= 6:
                w1load(0, 0); w1load(0, 1); w1load(0, 2)
                w2load(0, 0); w2load(0, 1); w2load(0, 2)
                scatters(0)
                gpsimd.wait_ge(s_x1st[0], 16 * ((nchunks + 1) // 2))
                gpsimd.wait_ge(s_x1st[1], 16 * (nchunks // 2))
                w2load(0, 3)
                b2load(0); b2load(1)
            for e in range(E if stop >= 6 else 0):
                if e > 0:
                    scatter_adds(e - 1)
                for g8 in range(3, 8):
                    w1load(e, g8)
                if e + 1 < E:
                    w1load(e + 1, 0); w1load(e + 1, 1); w1load(e + 1, 2)
                    scatters(e + 1)
                    for hq in range(4):
                        w2load(e + 1, hq)
                if e + 2 < E:
                    b2load(e + 2)
            if stop >= 6:
                scatter_adds(E - 1)
            for _i in range(4):
                gpsimd.wait_ge(s_ost[_i], 16 * ((nchunks + 3 - _i) // 4))

        @block.tensor
        def _(tensor):
            tensor.nop()
            if stop < 1:
                return
            for wname in ["wq", "wk", "wv"]:
                cast_key = {"wq": "cwq", "wk": "cwk", "wv": "cwv"}[wname]
                for j in range(8):
                    if j == 0:
                        tensor.wait_ge(s_vsw, VS[cast_key])
                    if j >= 2:
                        tensor.wait_ge(s_vsw, VS[f"e{wname}{j - 2}"])
                    dst = ps_tr[:, 0:QDIM] if j % 2 == 0 else ps_tr2[:, 0:QDIM]
                    tensor.transpose(dst, stag_b[:, 128 * j:128 * (j + 1)],
                                     identb[0:QDIM, 0:QDIM]).then_inc(s_tsw, 1)
            for j in range(8):
                tensor.wait_ge(s_vsw, VS["cwo"] if j < 2 else VS[f"ewo{j - 2}"])
                dst = ps_tr[0:QDIM, 0:128] if j % 2 == 0 else ps_tr2[0:QDIM, 0:128]
                tensor.transpose(dst, wo_b[:, j, :], identb[0:128, 0:128]).then_inc(s_tsw, 1)
            tensor.wait_ge(s_ldg, 16)
            for j in range(8):
                if j >= 1:
                    tensor.wait_ge(s_vsw, VS[f"ewg{j - 1}"])
                tensor.transpose(ps_g[:], stag_f[0:E, 128 * j:128 * (j + 1)],
                                 identf[0:E, 0:E]).then_inc(s_tsw, 1)
            for j in range(FT):
                c, cc2 = divmod(j, 4)
                tensor.wait_ge(s_b1ld, 16 * (c + 1))
                if j == 0:
                    tensor.wait_ge(s_vsw, VS["ewg7"])
                if j > 0:
                    tensor.wait_ge(s_vsw, VS[f"eb1_{j - 1}"])
                tensor.transpose(ps_g[:], b1tmp[:, 128 * cc2:128 * (cc2 + 1)],
                                 identf[0:8, 0:8]).then_inc(s_tsw, 1)

            if stop < 2:
                return
            tensor.wait_ge(s_xt, 16 * HT)
            for g in range(ngroups):
                sl = slice(gw * g, gw * (g + 1))
                if g > 0:
                    tensor.wait_ge(s_vprj, 5 * (g - 1) + 2)
                for j in range(HT):
                    mm = tensor.matmul(ps_a[0:QDIM, 0:gw], wqT[:, j, :], xT(j)[:, sl],
                                       start=(j == 0), stop=(j == HT - 1))
                mm.then_inc(s_tprj, 1)
                if g > 0:
                    tensor.wait_ge(s_vprj, 5 * (g - 1) + 4)
                for j in range(HT):
                    mm = tensor.matmul(ps_b[0:QDIM, 0:gw], wkT[:, j, :], xT(j)[:, sl],
                                       start=(j == 0), stop=(j == HT - 1))
                mm.then_inc(s_tprj, 1)
                if g > 0:
                    tensor.wait_ge(s_vprj, 5 * (g - 1) + 5)
                for j in range(HT):
                    mm = tensor.matmul(ps_c[0:QDIM, 0:gw], wvT[:, j, :], xT(j)[:, sl],
                                       start=(j == 0), stop=(j == HT - 1))
                mm.then_inc(s_tprj, 1)

            if stop < 3:
                return
            def t_qkv_head(c):
                csl_ = slice(128 * c, 128 * (c + 1))
                if c > 0:
                    tensor.wait_ge(s_vat, vam(c - 1))
                tensor.matmul(ps_c[:, 0:128], KT[:, csl_], QT[:, csl_],
                              start=True, stop=True).then_inc(s_tat, 1)
                if c > 0:
                    tensor.wait_ge(s_aat, andt(c - 1))
                tensor.transpose(ps_tr[:, 0:QDIM], KT[:, csl_],
                                 identb[0:QDIM, 0:QDIM]).then_inc(s_tat, 1)
                if c > 0:
                    tensor.wait_ge(s_aat, av1(c - 1))
                tensor.transpose(ps_tr2[:, 0:QDIM], VT[:, csl_],
                                 identb[0:QDIM, 0:QDIM]).then_inc(s_tat, 1)

            def t_attn_body(c):
                csl_ = slice(128 * c, 128 * (c + 1))
                tensor.wait_ge(s_vat, vam(c))
                tensor.wait_ge(s_aat, av1(c))
                mm = tensor.matmul(ps_n[:, 0:QDIM + 1], am_sb[:], v1_sb[:],
                                   start=True, stop=(c == 0))
                if c > 0:
                    mm = tensor.matmul(ps_n[:, 0:QDIM + 1], QT[:, csl_], kv_sb[:],
                                       start=False, stop=True)
                mm.then_inc(s_tat, 1)
                tensor.wait_ge(s_vat, vst(c))
                tensor.matmul(ps_state[:], k1_sb[:], v1_sb[:],
                              start=True, stop=True).then_inc(s_tat, 1)
                tensor.wait_ge(s_vat, vnd(c))
                tensor.wait_ge(s_aat, ak1(c))
                tensor.transpose(ps_tr[0:QDIM, 0:128], nd_sb[:],
                                 identb[0:128, 0:128]).then_inc(s_tat, 1)
                tensor.wait_ge(s_aat, andt(c))
                tensor.matmul(ps_a[:, 0:512], ndT_sb[:], woT[:, 0:512],
                              start=True, stop=True)
                tensor.matmul(ps_b[:, 0:512], ndT_sb[:], woT[:, 512:1024],
                              start=True, stop=True).then_inc(s_tat, 1)

            tensor.wait_ge(s_vprj, V_PRJ)
            t_qkv_head(0)
            t_attn_body(0)
            for cc in range(nchunks):
                par_t = x_tile[cc % 2]
                tensor.wait_ge(s_vat, vpuA(cc))
                for j in range(4):
                    tensor.transpose(ps_a[:, 128 * j:128 * (j + 1)],
                                     par_t[:, 128 * j:128 * (j + 1)],
                                     identf[:]).then_inc(s_tat, 1)
                tensor.wait_ge(s_aat, aevA(cc))
                for j in range(4):
                    tensor.matmul(ps_g[:, 0:E],
                                  x1T_stage[:, 128 * j:128 * (j + 1)],
                                  wgT[:, j, :], start=(j == 0), stop=False)
                tensor.wait_ge(s_vat, vpuB(cc))
                for j in range(4, HT):
                    tensor.transpose(ps_b[:, 128 * (j % 4):128 * (j % 4 + 1)],
                                     par_t[:, 128 * j:128 * (j + 1)],
                                     identf[:]).then_inc(s_tat, 1)
                if cc + 1 < nchunks:
                    t_qkv_head(cc + 1)
                tensor.wait_ge(s_aat, aevB(cc))
                for j in range(4, HT):
                    mm = tensor.matmul(ps_g[:, 0:E],
                                       x1T_stage[:, 128 * j:128 * (j + 1)],
                                       wgT[:, j, :], start=False, stop=(j == HT - 1))
                mm.then_inc(s_tat, 1)
                if cc + 1 < nchunks:
                    t_attn_body(cc + 1)

            if stop < 4:
                return
            for i in range(BIS):
                tensor.wait_ge(s_vsel, 2 * i + 1)
                if i > 0:
                    tensor.wait_ge(s_vsel, 2 * (i - 1) + 2)
                tensor.matmul(ps_n[:, 0:S16], eg16[:], cmpb[:],
                              start=True, stop=True).then_inc(s_tsel, 1)
            tensor.wait_ge(s_sel2, 32)
            tensor.matmul(ps_g[:, 0:E], ones_f[:], lo8[:],
                          start=True, stop=True).then_inc(s_tsel, 1)
            tensor.wait_ge(s_vsel, 2 * BIS + 4)
            tensor.matmul(ps_n[:, 0:E], tris[:], rsum[:],
                          start=True, stop=True).then_inc(s_tsel, 1)

            if stop < 6:
                return
            for e in range(E):
                TB = e * T_FFN
                AB = e * A_FFN
                VB = e * V_FFN
                tensor.wait_ge(s_exl2[e % 2], 16 * 9 * (e // 2 + 1))
                if e == 0:
                    tensor.wait_ge(s_vat, V_AT * nchunks)
                    tensor.wait_ge(s_vsel, SEL_V_TOTAL)
                if e > 0:
                    tensor.wait_ge(s_vffn, (e - 1) * V_FFN + 1)
                for sN in range(nslots):
                    tensor.transpose(ps_tr[:, 32 * sN:32 * (sN + 1)],
                                     exT2[e % 2][0:32, 8 * KSEL + 128 * sN:8 * KSEL + 128 * (sN + 1)],
                                     identb[0:32, 0:32]).then_inc(s_tffn, 1)
                if e > 0:
                    tensor.wait_ge(s_vffn, e * V_FFN)
                for g8 in range(8):
                    gi = e * 8 + g8
                    tensor.wait_ge(s_wl1[gi % 3], 16 * (gi // 3 + 1))
                    for jj in range(4):
                        kk = g8 * 4 + jj
                        psum = [ps_a, ps_b][kk % 2]
                        if AB + kk - 1 > 0:
                            tensor.wait_ge(s_affn, AB + kk - 1)
                        for kt in range(HT):
                            mm = tensor.matmul(psum[:, 0:KSEL],
                                               w1q[gi % 3][:, kt, 128 * jj:128 * (jj + 1)],
                                               exT2[e % 2][:, kt * KSEL:(kt + 1) * KSEL],
                                               start=(kt == 0), stop=(kt == HT - 1))
                        mm.then_inc(s_tffn, 1)
                tensor.wait_ge(s_affn, AB + FT)
                tensor.wait_ge(s_vffn, VB + 3)
                tensor.wait_ge(s_b2p[e % 2], 16 * (e // 2 + 1))
                PS4 = [ps_c, ps_n, ps_a, ps_b]
                for p2 in range(2):
                    q0, q1 = 2 * p2, 2 * p2 + 1
                    gi0, gi1 = e * 4 + q0, e * 4 + q1
                    tensor.wait_ge(s_wl2[gi0 % 4], 16 * (gi0 // 4 + 1))
                    tensor.wait_ge(s_wl2[gi1 % 4], 16 * (gi1 // 4 + 1))
                    for sN in range(nslots):
                        it = p2 * nslots + sN
                        psA = PS4[(2 * it) % 4]
                        psB = PS4[(2 * it + 1) % 4]
                        for half in range(2):
                            k = it * 2 + half
                            if k >= 4:
                                tensor.wait_ge(s_vffn, VB + 3 + (k - 4) + 1)
                            elif e > 0:
                                tensor.wait_ge(s_vffn,
                                               (e - 1) * V_FFN + 3 + (4 * nslots - 4 + k % 4) + 1)
                        for f in range(FT):
                            tensor.matmul(psA[:, 0:256],
                                          h1T(f)[:, 128 * sN:128 * (sN + 1)],
                                          w2q[gi0 % 4][:, f, :],
                                          start=(f == 0), stop=False)
                            tensor.matmul(psB[:, 0:256],
                                          h1T(f)[:, 128 * sN:128 * (sN + 1)],
                                          w2q[gi1 % 4][:, f, :],
                                          start=(f == 0), stop=False)
                        tensor.matmul(psA[:, 0:256], ones_b[0:1, :],
                                      b2bf2[e % 2][0:1, 256 * q0:256 * (q0 + 1)],
                                      start=False, stop=True).then_inc(s_tffn, 1)
                        tensor.matmul(psB[:, 0:256], ones_b[0:1, :],
                                      b2bf2[e % 2][0:1, 256 * q1:256 * (q1 + 1)],
                                      start=False, stop=True).then_inc(s_tffn, 1)

        @block.vector
        def _(vector):
            vcnt = [0]

            def vg(ins, milestone=None, n=1):
                ins.then_inc(s_vself, 1)
                vcnt[0] += 1
                vector.wait_ge(s_vself, vcnt[0])
                if milestone is not None:
                    vector.sem_inc(milestone, n)

            vg(vector.memset(ones_f[:], 1.0), s_vsw)
            vg(vector.memset(v1_sb[:, QDIM:QDIM + 1], 1.0), s_vsw)
            vg(vector.memset(xbf[:, :, 1042:ROWB], 0), s_vsw)
            vector.wait_ge(s_sw, 160)
            vg(vector.tensor_copy(xbf[:, :, 1040:1042],
                                  ids_sb[:].rearrange("p (c t) -> p c t", t=2)), s_vsw)
            vg(vector.memset(state_f[:], 0.0))
            vg(vector.memset(bcol[:, 0:1], 0.0))
            vg(vector.memset(bcol[:, 1:2], 1.0))
            vg(vector.memset(ones_b[:], 1.0), s_vsw)

            def _evacs(wname, wbuf):
                for j in range(8):
                    vector.wait_ge(s_tsw, TS[f"t{wname}{j}"])
                    if wname == "wo":
                        s = ps_tr[0:QDIM, 0:128] if j % 2 == 0 else ps_tr2[0:QDIM, 0:128]
                        vg(vector.tensor_copy(woT[:, 128 * j:128 * (j + 1)], s), s_vsw)
                    elif wname == "wg":
                        s = ps_tr[:, 0:E] if j % 2 == 0 else ps_tr2[:, 0:E]
                        vg(vector.tensor_copy(wgT[:, j, :], s), s_vsw)
                    else:
                        s = ps_tr[:, 0:QDIM] if j % 2 == 0 else ps_tr2[:, 0:QDIM]
                        vg(vector.tensor_copy(wbuf[:, j, :], s), s_vsw)

            vg(vector.tensor_copy(stag_b[:], stag_f[0:QDIM, :]), s_vsw)
            if stop >= 1:
                _evacs("wq", wqT)
            vector.wait_ge(s_ldk, 16)
            vector.wait_ge(s_tsw, TS["twq7"])
            vg(vector.tensor_copy(stag_b[:], stag_f[0:QDIM, :]), s_vsw)
            if stop >= 1:
                _evacs("wk", wkT)
            vector.wait_ge(s_ldv, 16)
            vector.wait_ge(s_tsw, TS["twk7"])
            vg(vector.tensor_copy(stag_b[:], stag_f[0:QDIM, :]), s_vsw)
            if stop >= 1:
                _evacs("wv", wvT)
            vector.wait_ge(s_ldo, 16)
            vg(vector.tensor_copy(wo_b[:].rearrange("p j m -> p (j m)"),
                                  stag_f[:, 0:512]), s_vsw)
            if stop >= 1:
                _evacs("wo", None)
            if stop >= 1:
                for j in range(8):
                    vector.wait_ge(s_tsw, TS[f"twg{j}"])
                    vg(vector.tensor_copy(wgT[:, j, :], ps_g[:, 0:E]), s_vsw)
            if stop < 1:
                return
            for j in range(FT):
                vector.wait_ge(s_tsw, TS[f"tb1_{j}"])
                vg(vector.tensor_copy(b1f[:, j, :], ps_g[:]), s_vsw)
            vector.sem_inc(s_vsw, 1)

            if stop < 2:
                return
            for g in range(ngroups):
                sl = slice(gw * g, gw * (g + 1))
                vector.wait_ge(s_tprj, 3 * g + 1)
                vg(vector.tensor_scalar_min(stag_f[0:QDIM, 0:gw],
                                            ps_a[0:QDIM, 0:gw], 0.0), s_vprj)
                vector.wait_ge(s_aprj, 2 * g + 1)
                vg(vector.tensor_scalar_max(stag_f[0:QDIM, 0:gw],
                                            ps_a[0:QDIM, 0:gw], 0.0))
                vg(vector.tensor_add(QT[:, sl], stag_b[:, 0:gw],
                                     stag_f[0:QDIM, 0:gw]), s_vprj)
                vector.wait_ge(s_tprj, 3 * g + 2)
                vg(vector.tensor_scalar_min(stag_f[0:QDIM, 0:gw],
                                            ps_b[0:QDIM, 0:gw], 0.0), s_vprj)
                vector.wait_ge(s_aprj, 2 * g + 2)
                vg(vector.tensor_scalar_max(stag_f[0:QDIM, 0:gw],
                                            ps_b[0:QDIM, 0:gw], 0.0))
                vg(vector.tensor_add(KT[:, sl], stag_b[:, 0:gw],
                                     stag_f[0:QDIM, 0:gw]), s_vprj)
                vector.wait_ge(s_tprj, 3 * g + 3)
                vg(vector.tensor_copy(VT[:, sl], ps_c[0:QDIM, 0:gw]), s_vprj)

            inv_d = 1.0 / D
            if stop < 3:
                return
            vector.wait_ge(s_tat, tA(0))
            vg(vector.tensor_tensor(am_sb[:], ps_c[:, 0:128], triu[:],
                                    ALU.mult), s_vat)
            vector.sem_inc(s_vat, 1)
            for cc in range(nchunks):
                vector.wait_ge(s_tat, tnum(cc))
                vg(vector.tensor_scalar_add(scal[:, 9:10], ps_n[:, QDIM:QDIM + 1], 1e-6))
                vg(vector.reciprocal(scal[:, 10:11], scal[:, 9:10]))
                vg(vector.tensor_scalar(nd_sb[:], ps_n[:, 0:QDIM], scal[:, 10:11],
                                        None, ALU.mult), s_vat)
                vector.wait_ge(s_tat, topj(cc))
                if cc >= 2:
                    vector.wait_ge(s_xch[cc % 2], 16 * (cc // 2))
                par = x_tile[cc % 2]
                vg(vector.scalar_tensor_tensor(par[:, 0:512], par[:, 0:512], 1.0,
                                               ps_a[:, 0:512], ALU.mult, ALU.add,
                                               accum_out=scal[:, 0:1]), s_vat)
                vg(vector.scalar_tensor_tensor(par[:, 512:1024], par[:, 512:1024], 1.0,
                                               ps_b[:, 0:512], ALU.mult, ALU.add,
                                               accum_out=scal[:, 1:2]), s_vat)
                vg(vector.tensor_add(scal[:, 2:3], scal[:, 0:1], scal[:, 1:2]))
                vg(vector.tensor_scalar_mul(scal[:, 3:4], scal[:, 2:3], inv_d))
                vg(vector.tensor_mul(scal[:, 4:5], scal[:, 3:4], scal[:, 3:4]))
                vector.wait_ge(s_aat, asq(cc))
                vg(vector.scalar_tensor_tensor(scal[:, 5:6], scal[:, 14:15], inv_d,
                                               scal[:, 4:5], ALU.mult, ALU.subtract),
                   s_vat)
                vector.wait_ge(s_aat, ar(cc))
                vg(vector.scalar_tensor_tensor(scal[:, 8:9], scal[:, 3:4], -1.0,
                                               scal[:, 7:8], ALU.mult, ALU.mult),
                   s_vat)
                vg(vector.scalar_tensor_tensor(
                    par[:, 0:512], par[:, 0:512], scal[:, 7:8],
                    scal[:, 8:9].broadcast_to((128, 512)),
                    ALU.mult, ALU.add), s_vat)
                vg(vector.scalar_tensor_tensor(
                    par[:, 512:1024], par[:, 512:1024], scal[:, 7:8],
                    scal[:, 8:9].broadcast_to((128, 512)),
                    ALU.mult, ALU.add), s_vat)
                if cc + 1 < nchunks:
                    vector.wait_ge(s_tat, tA(cc + 1))
                    vg(vector.tensor_tensor(am_sb[:], ps_c[:, 0:128], triu[:],
                                            ALU.mult), s_vat)
                    vector.wait_ge(s_tat, tdel(cc))
                    vector.wait_ge(s_aat, akv(cc))
                    vg(vector.tensor_tensor(state_f[:], state_f[:], ps_state[:],
                                            ALU.add), s_vat)
                vector.wait_ge(s_tat, tgate(cc))
                vg(vector.tensor_copy(lg_sb[:], ps_g[:, 0:E]))
                vg(vector.reduce_max(scal[:, 15:16], lg_sb[:], AX))
                vg(vector.tensor_scalar_mul(scal[:, 13:14], scal[:, 15:16], -1.0),
                   s_vat)
                vector.wait_ge(s_aat, aexp(cc))
                vg(vector.reciprocal(scal[:, 12:13], scal[:, 11:12]))
                vg(vector.tensor_scalar(sm_sb[:, cc, :], e_sb[:], scal[:, 12:13], None,
                                        ALU.mult))
                vg(vector.tensor_copy(xbf[:, cc, 1024:1032], sm_sb[:, cc, :]))
                vg(vector.tensor_tensor(xbf[:, cc, 1032:1040], sm_sb[:, cc, :],
                                        xbf[:, cc, 1024:1032],
                                        ALU.subtract), s_vat)

            if stop < 4:
                return
            vector.wait_ge(s_bisld, 16 * 8)
            for i in range(BIS):
                vg(vector.tensor_scalar_mul(bcol[:, 1:2], bcol[:, 1:2], 0.5))
                vg(vector.tensor_add(bcol[:, 2:3], bcol[:, 0:1], bcol[:, 1:2]))
                vg(vector.tensor_scalar(cmpb[:], smbis[:], bcol[:, 2:3], None,
                                        ALU.is_gt), s_vsel)
                vector.wait_ge(s_tsel, i + 1)
                vg(vector.reduce_sum(bcol[:, 4:5], ps_n[:, 0:S16], AX))
                vg(vector.scalar_tensor_tensor(bcol[:, 3:4], bcol[:, 4:5],
                                               float(KSEL), bcol[:, 1:2],
                                               ALU.is_ge, ALU.mult))
                vg(vector.tensor_add(bcol[:, 0:1], bcol[:, 0:1], bcol[:, 3:4]),
                   s_vsel)
            vector.wait_ge(s_tsel, BIS + 1)
            vg(vector.tensor_copy(lo_tm[:], ps_g[:, 0:E]))
            vg(vector.tensor_tensor(cmp_tm[:], sm_sb[:],
                                    lo_tm[:].unsqueeze(1).broadcast_to((128, nchunks, E)),
                                    ALU.is_gt), s_vsel)
            vg(vector.tensor_copy(cmpb_tm[:], cmp_tm[:]), s_vsel, 2)
            with nc.allow_low_precision(reason="0/1 counts <=16 exact in bf16"):
                vg(vector.reduce_sum(rsum[:].unsqueeze(2),
                                     cmpb_tm[:].rearrange("p c e -> p e c"), AX),
                   s_vsel)
            cur, nxt = cmp_tm, pbufA
            for k2 in prefix_steps:
                vg(vector.tensor_tensor(nxt[:, k2:, :], cur[:, k2:, :],
                                        cur[:, 0:nchunks - k2, :], ALU.add))
                vg(vector.tensor_copy(nxt[:, 0:k2, :], cur[:, 0:k2, :]))
                cur, nxt = nxt, (pbufB if nxt is pbufA else pbufA)
            vg(vector.tensor_sub(nxt[:], cur[:], cmp_tm[:]))
            excl_ct = nxt
            vector.wait_ge(s_tsel, BIS + 2)
            vg(vector.tensor_copy(exclp[:], ps_n[:, 0:E]))
            rank = cur
            vg(vector.tensor_tensor(rank[:], excl_ct[:],
                                    exclp[:].unsqueeze(1).broadcast_to((128, nchunks, E)),
                                    ALU.add))
            vg(vector.scalar_tensor_tensor(rank[:], cmp_tm[:], -4096.0, rank[:],
                                           ALU.mult, ALU.add))
            vg(vector.tensor_scalar_add(rank[:], rank[:], 4096.0))
            vg(vector.tensor_copy(off_i[:], rank[:]), s_vsel, RANK_EXTRA - 4)

            if stop < 6:
                return
            for e in range(E):
                TB = e * T_FFN
                VB = e * V_FFN
                vector.wait_ge(s_tffn, TB + nslots)
                if e > 0:
                    vector.wait_ge(s_gsc, 16 * nslots * e)
                vg(vector.tensor_copy(exv[:, 0:32 * nslots],
                                      ps_tr[:, 0:32 * nslots]), s_vffn)
                for sN in range(nslots):
                    o = 32 * sN
                    vg(vector.tensor_tensor(G_sb[:, sN:sN + 1], exv[:, o + e:o + e + 1],
                                            exv[:, o + 8 + e:o + 8 + e + 1], ALU.add))
                    vg(vector.scalar_tensor_tensor(idf_sb[:, sN:sN + 1],
                                                   exv[:, o + 17:o + 18], 256.0,
                                                   exv[:, o + 16:o + 17],
                                                   ALU.mult, ALU.add))
                vg(vector.tensor_copy(idx_i[:], idf_sb[:]), s_vffn)
                vector.sem_inc(s_vffn, 1)
                PS4 = [ps_c, ps_n, ps_a, ps_b]
                for p2 in range(2):
                    for sN in range(nslots):
                        it = p2 * nslots + sN
                        for half in range(2):
                            hq = 2 * p2 + half
                            psum = PS4[(2 * it + half) % 4]
                            vector.wait_ge(s_tffn, TB + nslots + FT + it * 2 + half + 1)
                            vg(vector.tensor_scalar(outsb[:, sN, 256 * hq:256 * (hq + 1)],
                                                    psum[:, 0:256], G_sb[:, sN:sN + 1],
                                                    None, ALU.mult), s_vffn)

            def _l2sum(c):
                L = LC[c % 2]
                vector.wait_ge(s_x2ld[c % 4], 16 * (c // 4 + 1))
                par_ = x2t[c % 4]
                vg(vector.reduce_sum(scal[:, L[0]:L[0] + 1], par_[:], AX))
                vg(vector.tensor_scalar_mul(scal[:, L[1]:L[1] + 1],
                                            scal[:, L[0]:L[0] + 1], inv_d))
                vg(vector.tensor_mul(scal[:, L[2]:L[2] + 1],
                                     scal[:, L[1]:L[1] + 1],
                                     scal[:, L[1]:L[1] + 1]))

            _l2sum(0)
            for cc in range(nchunks):
                L = LC[cc % 2]
                par = x2t[cc % 4]
                vector.wait_ge(s_aat, A_AT * nchunks + 2 * cc + 1)
                vg(vector.scalar_tensor_tensor(scal[:, L[3]:L[3] + 1],
                                               scal[:, L[7]:L[7] + 1], inv_d,
                                               scal[:, L[2]:L[2] + 1],
                                               ALU.mult, ALU.subtract),
                   s_vat)
                if cc + 1 < nchunks:
                    _l2sum(cc + 1)
                vector.wait_ge(s_aat, A_AT * nchunks + 2 * cc + 2)
                vg(vector.scalar_tensor_tensor(scal[:, L[6]:L[6] + 1],
                                               scal[:, L[1]:L[1] + 1], -1.0,
                                               scal[:, L[5]:L[5] + 1],
                                               ALU.mult, ALU.mult),
                   s_vat)
                vg(vector.scalar_tensor_tensor(
                    par[:], par[:], scal[:, L[5]:L[5] + 1],
                    scal[:, L[6]:L[6] + 1].broadcast_to((128, D)),
                    ALU.mult, ALU.add), s_vat)

        @block.scalar
        def _(scalar):
            acnt = [0]

            def ag(ins):
                ins.then_inc(s_aself, 1)
                acnt[0] += 1
                scalar.wait_ge(s_aself, acnt[0])

            scalar.nop()
            if stop < 2:
                return
            for g in range(ngroups):
                scalar.wait_ge(s_vprj, 5 * g + 1)
                scalar.activation(stag_b[:, 0:gw], stag_f[0:QDIM, 0:gw],
                                  AF.Exp).then_inc(s_aprj, 1)
                scalar.wait_ge(s_vprj, 5 * g + 3)
                scalar.activation(stag_b[:, 0:gw], stag_f[0:QDIM, 0:gw],
                                  AF.Exp).then_inc(s_aprj, 1)
            if stop < 3:
                return
            scalar.wait_ge(s_vat, vst(0))
            scalar.activation(kv_sb[:], state_f[:],
                              AF.Copy).then_inc(s_aat, 1)
            scalar.wait_ge(s_tat, tK(0))
            scalar.activation(k1_sb[:], ps_tr[:, 0:QDIM],
                              AF.Copy).then_inc(s_aat, 1)
            scalar.wait_ge(s_tat, tV(0))
            scalar.activation(v1_sb[:, 0:QDIM], ps_tr2[:, 0:QDIM],
                              AF.Copy).then_inc(s_aat, 1)
            scalar.wait_ge(s_tat, tndt(0))
            scalar.activation(ndT_sb[:], ps_tr[0:QDIM, 0:128],
                              AF.Copy).then_inc(s_aat, 1)
            for cc in range(nchunks):
                par = x_tile[cc % 2]
                scalar.wait_ge(s_vat, vaddb(cc))
                scalar.activation(sq_scr[:], par[:], AF.Square,
                                  accum_out=scal[:, 14:15]).then_inc(s_aat, 1)
                scalar.wait_ge(s_vat, vve(cc))
                ag(scalar.activation(scal[:, 6:7], scal[:, 5:6], AF.Ln, bias=C_EPS))
                scalar.activation(scal[:, 7:8], scal[:, 6:7], AF.Exp,
                                  scale=-0.5).then_inc(s_aat, 1)
                scalar.wait_ge(s_vat, vpuB(cc))
                scalar.activation(xbf[:, cc, 0:1024], par[:],
                                  AF.Copy).then_inc(s_aat, 1)
                scalar.wait_ge(s_tat, ttr(cc, 3))
                scalar.activation(x1T_stage[:, 0:512], ps_a[:, 0:512],
                                  AF.Copy).then_inc(s_aat, 1)
                scalar.wait_ge(s_tat, ttr(cc, 7))
                scalar.activation(x1T_stage[:, 512:1024], ps_b[:, 0:512],
                                  AF.Copy).then_inc(s_aat, 1)
                if cc + 1 < nchunks:
                    scalar.wait_ge(s_vat, vst(cc + 1))
                    scalar.activation(kv_sb[:], state_f[:],
                                      AF.Copy).then_inc(s_aat, 1)
                    scalar.wait_ge(s_tat, tK(cc + 1))
                    scalar.activation(k1_sb[:], ps_tr[:, 0:QDIM],
                                      AF.Copy).then_inc(s_aat, 1)
                    scalar.wait_ge(s_tat, tV(cc + 1))
                    scalar.activation(v1_sb[:, 0:QDIM], ps_tr2[:, 0:QDIM],
                                      AF.Copy).then_inc(s_aat, 1)
                scalar.wait_ge(s_vat, vlg(cc))
                scalar.activation(e_sb[:], lg_sb[:], AF.Exp, bias=scal[:, 13:14],
                                  accum_out=scal[:, 11:12]).then_inc(s_aat, 1)
                if cc + 1 < nchunks:
                    scalar.wait_ge(s_tat, tndt(cc + 1))
                    scalar.activation(ndT_sb[:], ps_tr[0:QDIM, 0:128],
                                      AF.Copy).then_inc(s_aat, 1)
            if stop < 4:
                return
            scalar.wait_ge(s_smd, 16)
            for e in range(4, 8):
                scalar.dma_start(
                    out=smbis[16 * e:16 * (e + 1), :].unsqueeze(2),
                    in_=smd.rearrange("(pp c) e -> pp c e", pp=16)[:, :, e:e + 1],
                    allow_slow_non_contiguous=True).then_inc(s_bisld, 16)
            if stop < 6:
                return
            for e in range(E):
                TB = e * T_FFN
                AB = e * A_FFN
                for kk in range(FT):
                    psum = [ps_a, ps_b][kk % 2]
                    scalar.wait_ge(s_tffn, TB + nslots + kk + 1)
                    scalar.activation(h1T(kk), psum[:, 0:KSEL], AF.Gelu_apprx_tanh,
                                      bias=b1f[:, kk, e:e + 1]).then_inc(s_affn, 1)
            for cc in range(nchunks):
                L = LC[cc % 2]
                par = x2t[cc % 4]
                scalar.wait_ge(s_x2ld[cc % 4], 16 * (cc // 4 + 1))
                scalar.activation(sq_scr[:], par[:], AF.Square,
                                  accum_out=scal[:, L[7]:L[7] + 1]).then_inc(s_aat, 1)
                scalar.wait_ge(s_vat, V_AT * nchunks + 3 * cc + 1)
                ag(scalar.activation(scal[:, L[4]:L[4] + 1], scal[:, L[3]:L[3] + 1],
                                     AF.Ln, bias=C_EPS))
                scalar.activation(scal[:, L[5]:L[5] + 1], scal[:, L[4]:L[4] + 1],
                                  AF.Exp, scale=-0.5).then_inc(s_aat, 1)

    return nc


def _consts(S):
    nchunks = S // 128
    i = np.arange(128)
    triu = (i[:, None] <= i[None, :]).astype(ml_dtypes.bfloat16)
    tris = (i[:, None] < i[None, :]).astype(ml_dtypes.bfloat16)
    eg = (i[:, None] // 16 == i[None, :] // 16).astype(ml_dtypes.bfloat16)
    identf = np.eye(128, dtype=np.float32)
    identb = np.eye(128).astype(ml_dtypes.bfloat16)
    t = i[:, None] + 128 * np.arange(nchunks)[None, :]
    ids = np.stack([t % 256, t // 256], axis=-1).reshape(128, nchunks * 2)
    ids = ids.astype(ml_dtypes.bfloat16)
    csts = np.tile(np.array([EPS, 0, 0, 0, 0, 0, 0, 0], np.float32), (128, 1))
    return dict(triu=triu, tris=tris, eg16=eg, identf=identf, identb=identb,
                idsc=ids, csts=csts)


def kernel(L0_attn_x, L0_attn_Wq, L0_attn_Wk, L0_attn_Wv, L0_attn_Wo,
           L0_ffn_Wg, L0_ffn_W1, L0_ffn_b1, L0_ffn_W2, L0_ffn_b2):
    B, S, _ = L0_attn_x.shape
    nc = build_program(S=S, KSEL=512, BIS=30)
    consts = _consts(S)
    base = dict(wq=np.ascontiguousarray(L0_attn_Wq),
                wk=np.ascontiguousarray(L0_attn_Wk),
                wv=np.ascontiguousarray(L0_attn_Wv),
                wo=np.ascontiguousarray(L0_attn_Wo),
                wg=np.ascontiguousarray(L0_ffn_Wg),
                w1=np.ascontiguousarray(L0_ffn_W1),
                b1=np.ascontiguousarray(L0_ffn_b1),
                w2=np.ascontiguousarray(L0_ffn_W2),
                b2=np.ascontiguousarray(L0_ffn_b2),
                **consts)
    in_maps = [dict(x=np.ascontiguousarray(L0_attn_x[b]), **base) for b in range(B)]
    res = run_bass_kernel_spmd(nc, in_maps, core_ids=list(range(B)))
    return np.stack([np.asarray(r["out"]) for r in res.results], axis=0)
